# version 02
# baseline (speedup 1.0000x reference)
from contextlib import ExitStack
import numpy as np
import ml_dtypes
import concourse.bass as bass
import concourse.mybir as mybir
from concourse.bass_utils import run_bass_kernel_spmd

F32 = mybir.dt.float32
BF16 = mybir.dt.bfloat16
AF = mybir.ActivationFunctionType
ALU = mybir.AluOpType
AX = mybir.AxisListType

T = 8192
C = 1024
NT = T // 128
H, N = 16, 64
LCH = 64
NCH = T // LCH
BF = ml_dtypes.bfloat16



def _prep_host(inputs):
    g = {k: np.asarray(v) for k, v in inputs.items()}
    d = {}
    f32 = np.float32
    ct = np.ascontiguousarray

    d["c_ident_f32"] = np.eye(128, dtype=f32)
    d["c_ident_bf"] = np.eye(128, dtype=f32).astype(BF)
    consts = np.zeros((128, 8), f32)
    consts[:, 0] = 1e-5
    consts[:, 1] = 1.6e-4
    consts[:, 2] = 1.0
    consts[:, 3] = 1e-6
    consts[:, 4] = 0.5
    consts[:, 5] = -0.5
    consts[:, 6] = 2.0
    consts[:, 7] = -1e30
    d["c_consts"] = consts

    d["s2_gate_wt"] = ct(g["L0_ffn_gate_w"].T.astype(f32))
    d["s2_gate_bias"] = ct(np.tile(g["L0_ffn_gate_bias"].astype(f32)[None, :], (128, 1)))
    w1 = np.concatenate([g["L0_ffn_w1"], g["L0_ffn_sw1"][None]], 0)
    w3 = np.concatenate([g["L0_ffn_w3"], g["L0_ffn_sw3"][None]], 0)
    w2 = np.concatenate([g["L0_ffn_w2"], g["L0_ffn_sw2"][None]], 0)
    d["s2_w1t"] = ct(np.transpose(w1, (0, 2, 1)).astype(f32))
    d["s2_w3t"] = ct(np.transpose(w3, (0, 2, 1)).astype(f32))
    d["s2_w2t"] = ct(np.transpose(w2, (0, 2, 1)).astype(f32))

    d["s3_xA"] = ct(g["L0_attn_x_A"].astype(f32))
    d["s3_xB"] = ct(g["L0_attn_x_B"].astype(f32))
    lam = g["L0_attn_lambdas"].astype(f32).reshape(6, 1024)
    lamc = np.zeros((128, 48), f32)
    for gg in range(6):
        for fc in range(8):
            lamc[:, gg * 8 + fc] = lam[gg, fc * 128:(fc + 1) * 128]
    d["s3_lam"] = lamc
    xm = g["L0_attn_x_miu"].astype(f32).reshape(1024)
    d["s3_xmiu"] = ct(xm.reshape(8, 128).T)
    d["s3_wA"] = ct(g["L0_attn_w_A"].astype(f32))
    d["s3_kA"] = ct(g["L0_attn_k_A"].astype(f32))
    d["s3_aA"] = ct(g["L0_attn_a_A"].astype(f32))
    d["s3_gA"] = ct(g["L0_attn_g_A"].astype(f32))
    d["s3_wB"] = ct(g["L0_attn_w_B"].astype(f32))
    d["s3_kB"] = ct(g["L0_attn_k_B"].astype(f32))
    d["s3_aB"] = ct(g["L0_attn_a_B"].astype(f32))
    d["s3_gB"] = ct(g["L0_attn_g_B"].astype(f32))
    d["s3_wmiu_t"] = ct(np.tile(g["L0_attn_w_miu"].astype(f32).reshape(1, 1024), (128, 1)))
    d["s3_atime_t"] = ct(np.tile(g["L0_attn_a_time"].astype(f32).reshape(1, 1024), (128, 1)))
    d["s3_xtime_t"] = ct(np.tile(g["L0_attn_x_time"].astype(f32).reshape(1, 1024), (128, 1)))
    d["s3_rWt"] = ct(g["L0_attn_rW"].T.astype(f32))
    d["s3_kWt"] = ct(g["L0_attn_kW"].T.astype(f32))
    d["s3_vWt"] = ct(g["L0_attn_vW"].T.astype(f32))
    c3 = np.zeros((128, 4), f32)
    c3[0:32, 0] = 1.0
    c3[64:96, 1] = 1.0
    c3[32:64, 2] = 1.0
    c3[96:128, 3] = 1.0
    d["s3_c3aux"] = c3

    d["s5_oWt"] = ct(g["L0_attn_oW"].T.astype(f32))
    d["s5_gng_t"] = ct(np.tile(g["L0_attn_gn_gamma"].astype(f32)[None, :], (128, 1)))
    d["s5_gnb_t"] = ct(np.tile(g["L0_attn_gn_beta"].astype(f32)[None, :], (128, 1)))

    d["s6_rw"] = ct(g["L1_ffn_router_w"].astype(f32))
    d["s6_w1"] = ct(g["L1_ffn_w1"].astype(f32))
    b1 = g["L1_ffn_b1"].astype(f32)
    b1c = np.zeros((128, 224), f32)
    for e in range(14):
        for ic in range(16):
            b1c[:, e * 16 + ic] = b1[e, ic * 128:(ic + 1) * 128]
    d["s6_b1c"] = b1c
    d["s6_w2"] = ct(g["L1_ffn_w2"].astype(f32))
    d["s6_b2"] = ct(g["L1_ffn_b2"].astype(f32))

    d["c_ones_bf"] = np.ones((128, 128), f32).astype(BF)
    qkvw = g["L1_attn_qkv_w"].astype(f32)
    qkvb = g["L1_attn_qkv_b"].astype(f32)
    d["s7_qw"] = ct(qkvw[:, 0:1024])
    d["s7_kvw"] = ct(qkvw[:, 1024:3072])
    d["s7_qbc"] = ct(qkvb[0:1024].reshape(8, 128).T)
    d["s7_kvb_t"] = ct(np.tile(qkvb[1024:3072][None, :], (128, 1)))
    d["s7_ow"] = ct(g["L1_attn_out_w"].astype(f32))
    d["s7_ob_t"] = ct(np.tile(g["L1_attn_out_b"].astype(f32)[None, :], (128, 1)))
    d["s7_w1"] = ct(g["L1_attn_w1"].astype(f32))
    d["s7_b1c"] = ct(g["L1_attn_b1"].astype(f32).reshape(32, 128).T)
    d["s7_w2"] = ct(g["L1_attn_w2"].astype(f32))
    d["s7_b2_t"] = ct(np.tile(g["L1_attn_b2"].astype(f32)[None, :], (128, 1)))
    for nm, key in [("s7_g1_t", "L1_attn_g1"), ("s7_be1_t", "L1_attn_be1"),
                    ("s7_g2_t", "L1_attn_g2"), ("s7_be2_t", "L1_attn_be2")]:
        d[nm] = ct(np.tile(g[key].astype(f32)[None, :], (128, 1)))

    dm = np.zeros((128, 128), f32)
    dp = np.zeros((128, 128), f32)
    for cj in range(2):
        for s_ in range(64):
            j = cj * 64 + s_
            for ul in range(64):
                u = cj * 64 + ul
                dm[u, j] = (1.0 if ul <= s_ else 0.0) - (1.0 if ul <= 31 else 0.0)
                dp[u, j] = (1.0 if ul <= s_ - 1 else 0.0) - (1.0 if ul <= 31 else 0.0)
    d["s4_dmatM"] = dm
    d["s4_dmatP"] = dp
    mlo = np.tril(np.ones((64, 64), f32), -1)
    mupS = np.triu(np.ones((64, 64), f32), 1)
    mupI = np.triu(np.ones((64, 64), f32), 0)
    masks = np.zeros((64, 320), f32)
    masks[:, 0:64] = mlo
    masks[:, 64:128] = mupS
    masks[:, 128:192] = mupI
    masks[:, 192:256] = mupS
    masks[:, 256:320] = mupI
    d["s4_masks"] = masks
    return d


def _prep_core(inputs, b):
    x = np.ascontiguousarray(np.asarray(inputs["L0_attn_x"])[b].astype(np.float32))
    return {"x": x}



IO_SPECS = {
    "x": ([T, C], F32, True),
    "xnT_f32": ([C, T], F32, True),
    "dbg_xn": ([T, C], F32, True),
    "dbg_cols": ([NT * 128, 12], F32, True),
    "xnT_bf": ([C, T], BF16, True),
    "f0": ([T, C], F32, True),
    "dbg_comb": ([128, NT, 12], F32, True),
    "c_ident_f32": ([128, 128], F32, False),
    "c_consts": ([128, 8], F32, False),
    "c_ident_bf": ([128, 128], BF16, False),
    "s2_gate_wt": ([C, 8], F32, False),
    "s2_gate_bias": ([128, 8], F32, False),
    "s2_w1t": ([9, C, 2048], F32, False),
    "s2_w3t": ([9, C, 2048], F32, False),
    "s2_w2t": ([9, 2048, C], F32, False),
}

for _n in ["s3_xA", "s3_wA", "s3_kA", "s3_aA", "s3_gA", "s3_wB", "s3_kB", "s3_aB",
           "s3_gB", "s3_rWt", "s3_kWt", "s3_vWt"]:
    pass
IO_SPECS.update({
    "s3_xA": ([C, 192], F32, False),
    "s3_xB": ([6, 32, C], F32, False),
    "s3_lam": ([128, 48], F32, False),
    "s3_xmiu": ([128, 8], F32, False),
    "s3_wA": ([C, 64], F32, False),
    "s3_kA": ([C, 64], F32, False),
    "s3_aA": ([C, 64], F32, False),
    "s3_gA": ([C, 128], F32, False),
    "s3_wB": ([64, C], F32, False),
    "s3_kB": ([64, C], F32, False),
    "s3_aB": ([64, C], F32, False),
    "s3_gB": ([128, C], F32, False),
    "s3_wmiu_t": ([128, C], F32, False),
    "s3_atime_t": ([128, C], F32, False),
    "s3_xtime_t": ([128, C], F32, False),
    "s3_rWt": ([C, C], F32, False),
    "s3_kWt": ([C, C], F32, False),
    "s3_vWt": ([C, C], F32, False),
    "s3_c3aux": ([128, 4], F32, False),
    "w_d": ([T, C], F32, True),
    "r_d": ([T, C], BF16, True),
    "k_d": ([T, C], BF16, True),
    "kk_d": ([T, C], BF16, True),
    "v_d": ([T, C], BF16, True),
    "b_d": ([T, C], BF16, True),
    "g_d": ([T, C], BF16, True),
    "bsum_d": ([T, 16], F32, True),
    "wlm_d": ([NCH, C], F32, True),
    "mcs_d": ([NCH, C], F32, True),
    "y_d": ([T, C], F32, True),
    "x1_d": ([T, C], F32, True),
    "xn1_d": ([T, C], F32, True),
    "xn1T_bf": ([C, T], BF16, True),
    "s5_oWt": ([C, C], F32, False),
    "s5_gng_t": ([128, C], F32, False),
    "s5_gnb_t": ([128, C], F32, False),
    "f1_d": ([T, C], F32, True),
    "s6_rw": ([C, 14], F32, False),
    "s6_w1": ([14, C, 2048], F32, False),
    "s6_b1c": ([128, 224], F32, False),
    "s6_w2": ([14, 2048, C], F32, False),
    "s6_b2": ([14, C], F32, False),
    "qfT_d": ([C, T], BF16, True),
    "kvz_d": ([128, 8 * 66], BF16, True),
    "out_d": ([T, C], F32, True),
    "c_ones_bf": ([128, 128], BF16, False),
    "s7_qw": ([C, C], F32, False),
    "s7_kvw": ([C, 2048], F32, False),
    "s7_qbc": ([128, 8], F32, False),
    "s7_kvb_t": ([128, 2048], F32, False),
    "s7_ow": ([C, C], F32, False),
    "s7_ob_t": ([128, C], F32, False),
    "s7_w1": ([C, 4096], F32, False),
    "s7_b1c": ([128, 32], F32, False),
    "s7_w2": ([4096, C], F32, False),
    "s7_b2_t": ([128, C], F32, False),
    "s7_g1_t": ([128, C], F32, False),
    "s7_be1_t": ([128, C], F32, False),
    "s7_g2_t": ([128, C], F32, False),
    "s7_be2_t": ([128, C], F32, False),
    "s4_dmatM": ([128, 128], F32, False),
    "s4_dmatP": ([128, 128], F32, False),
    "s4_masks": ([64, 320], F32, False),
})

STAGE_IN = {
    "s1": ["x", "c_ident_f32", "c_ident_bf", "c_consts"],
    "s2": ["xnT_bf", "s2_gate_wt", "s2_gate_bias", "s2_w1t", "s2_w3t", "s2_w2t"],
    "s3": ["xnT_bf", "c_consts", "s3_xA", "s3_xB", "s3_lam", "s3_xmiu", "s3_wA",
           "s3_kA", "s3_aA", "s3_gA", "s3_wB", "s3_kB", "s3_aB", "s3_gB",
           "s3_wmiu_t", "s3_atime_t", "s3_xtime_t", "s3_rWt", "s3_kWt", "s3_vWt",
           "s3_c3aux"],
    "s5": ["y_d", "bsum_d", "v_d", "g_d", "x", "f0", "c_consts", "c_ident_bf",
           "s5_oWt", "s5_gng_t", "s5_gnb_t"],
    "s6": ["xn1T_bf", "c_ident_bf", "s6_rw", "s6_w1", "s6_b1c", "s6_w2", "s6_b2"],
    "s4": ["w_d", "r_d", "k_d", "kk_d", "v_d", "b_d", "wlm_d", "mcs_d",
           "c_ident_bf", "s4_dmatM", "s4_dmatP", "s4_masks"],
    "s7a": ["xn1_d", "c_consts", "c_ident_bf", "c_ones_bf", "s7_qw", "s7_kvw",
            "s7_qbc", "s7_kvb_t", "s7_g1_t", "s7_be1_t"],
    "s7b": ["qfT_d", "kvz_d", "xn1_d", "x1_d", "f1_d", "c_consts", "c_ident_bf",
            "c_ones_bf", "s7_ow", "s7_ob_t", "s7_w1", "s7_b1c", "s7_w2", "s7_b2_t",
            "s7_g2_t", "s7_be2_t"],
}
STAGE_OUT = {
    "s1": ["xnT_f32", "xnT_bf"],
    "s2": ["f0"],
    "s3": ["w_d", "r_d", "k_d", "kk_d", "v_d", "b_d", "g_d", "bsum_d",
           "wlm_d", "mcs_d"],
    "s5": ["x1_d", "xn1_d", "xn1T_bf"],
    "s6": ["f1_d"],
    "s4": ["y_d"],
    "s7a": ["qfT_d", "kvz_d"],
    "s7b": ["out_d"],
}



def build_s1(nc, io):
    x = io["x"]
    xnT = io["xnT_f32"]
    xnTb = io["xnT_bf"]
    idf = io["c_ident_f32"]
    idb = io["c_ident_bf"]
    ccd = io["c_consts"]

    with (
        nc.sbuf_tensor("s1_cc", [128, 8], F32) as cc,
        nc.sbuf_tensor("s1_xin", [128, 2, C], F32) as xin,
        nc.sbuf_tensor("s1_xn", [128, 2, C], F32) as xn,
        nc.sbuf_tensor("s1_xnb", [128, 2, C], BF16) as xnb,
        nc.sbuf_tensor("s1_sq", [128, C], F32) as sq,
        nc.sbuf_tensor("s1_cols", [128, 2, 8], F32) as cols,
        nc.sbuf_tensor("s1_scol", [128, 2, 4], F32) as scol,
        nc.sbuf_tensor("s1_otf", [128, 2, 8, 128], F32) as otf,
        nc.sbuf_tensor("s1_otb", [128, 2, 8, 128], BF16) as otb,
        nc.sbuf_tensor("s1_idf", [128, 128], F32) as sidf,
        nc.sbuf_tensor("s1_idb", [128, 128], BF16) as sidb,
        nc.psum_tensor("s1_ptf", [128, 2, 2, 512], F32) as ptf,
        nc.psum_tensor("s1_ptb", [128, 2, 1024], BF16) as ptb,
    ):
        ld = nc.alloc_semaphore(name="s1_ld")
        vm = nc.alloc_semaphore(name="s1_vm")
        am = nc.alloc_semaphore(name="s1_am")
        tm = nc.alloc_semaphore(name="s1_tm")
        st = nc.alloc_semaphore(name="s1_st")
        blkctx = nc.Block()
        block = blkctx.__enter__()
        def vm_v(i, k):
            return 4 * i + k

        def am_v(i, k):
            return 2 * i + k

        @block.sync
        def _(sync):
            sync.dma_start(out=sidf[:], in_=idf[:, :]).then_inc(ld, 16)
            sync.dma_start(out=sidb[:], in_=idb[:, :]).then_inc(ld, 16)
            sync.dma_start(out=cc[:], in_=ccd[:, :]).then_inc(ld, 16)
            for i in range(NT):
                if i >= 2:
                    sync.wait_ge(vm, vm_v(i - 2, 3))
                sync.dma_start(out=xin[:, i % 2, :],
                               in_=x[i * 128:(i + 1) * 128, :]).then_inc(ld, 16)

        @block.scalar
        def _(scalar):
            for i in range(NT):
                p = i % 2
                scalar.wait_ge(ld, 48 + 16 * (i + 1))
                if i >= 1:
                    scalar.wait_ge(vm, vm_v(i - 1, 2))
                scalar.activation(sq[:, :], xin[:, p, :], AF.Square,
                                  accum_out=scol[:, p, 0:1])
                scalar.drain().then_inc(am, 1)
                scalar.wait_ge(vm, vm_v(i, 2))
                scalar.activation(scol[:, p, 2:3], cols[:, p, 5:6], AF.Ln, bias=cc[:, 0:1])
                scalar.drain()
                scalar.activation(scol[:, p, 1:2], scol[:, p, 2:3], AF.Exp, scale=-0.5)
                scalar.drain().then_inc(am, 1)

        @block.vector
        def _(vector):
            for i in range(NT):
                p = i % 2
                vector.wait_ge(ld, 48 + 16 * (i + 1))
                xi = xin[:, p, :]
                vector.reduce_sum(cols[:, p, 0:1], xi, AX.X).then_inc(vm, 1)
                vector.wait_ge(am, am_v(i, 1))
                vector.drain()
                vector.tensor_scalar(cols[:, p, 2:3], cols[:, p, 0:1], 1.0 / C, None, ALU.mult)
                vector.drain()
                vector.tensor_tensor(cols[:, p, 4:5], cols[:, p, 0:1], cols[:, p, 2:3], ALU.mult)
                vector.drain()
                vector.tensor_tensor(cols[:, p, 3:4], scol[:, p, 0:1], cols[:, p, 4:5], ALU.subtract)
                vector.drain()
                vector.tensor_scalar(cols[:, p, 5:6], cols[:, p, 3:4], 1.0 / C, None,
                                     ALU.mult).then_inc(vm, 1)
                vector.wait_ge(am, am_v(i, 2))
                vector.drain()
                vector.tensor_scalar(xn[:, p, :], xi, cols[:, p, 2:3], scol[:, p, 1:2],
                                     ALU.subtract, ALU.mult)
                vector.drain()
                vector.tensor_copy(xnb[:, p, :], xn[:, p, :]).then_inc(vm, 1)
                vector.wait_ge(tm, i + 1)
                if i >= 2:
                    vector.wait_ge(st, 32 * (i - 1))
                vector.tensor_copy(otf[:, p, 0:4, :].rearrange("p a b -> p (a b)"),
                                   ptf[:, p, 0, :])
                vector.tensor_copy(otf[:, p, 4:8, :].rearrange("p a b -> p (a b)"),
                                   ptf[:, p, 1, :])
                vector.tensor_copy(otb[:, p, :, :].rearrange("p a b -> p (a b)"),
                                   ptb[:, p, :]).then_inc(vm, 1)

        @block.tensor
        def _(tensor):
            tensor.wait_ge(ld, 48)
            for i in range(NT):
                p = i % 2
                tensor.wait_ge(vm, vm_v(i, 3))
                if i >= 2:
                    tensor.wait_ge(vm, vm_v(i - 2, 4))
                for j in range(8):
                    tensor.transpose(ptf[:, p, j // 4, (j % 4) * 128:(j % 4 + 1) * 128],
                                     xn[:, p, j * 128:(j + 1) * 128], sidf[:])
                mm = None
                for j in range(8):
                    mm = tensor.transpose(ptb[:, p, j * 128:(j + 1) * 128],
                                          xnb[:, p, j * 128:(j + 1) * 128], sidb[:])
                mm.then_inc(tm, 1)

        @block.gpsimd
        def _(gpsimd):
            for i in range(NT):
                p = i % 2
                gpsimd.wait_ge(vm, vm_v(i, 4))
                gpsimd.dma_start(
                    out=xnT.rearrange("(fc f) t -> f fc t", f=128)[:, :, i * 128:(i + 1) * 128],
                    in_=otf[:, p, :, :]).then_inc(st, 16)
                gpsimd.dma_start(
                    out=xnTb.rearrange("(fc f) t -> f fc t", f=128)[:, :, i * 128:(i + 1) * 128],
                    in_=otb[:, p, :, :]).then_inc(st, 16)
            gpsimd.wait_ge(st, 32 * NT)
        blkctx.__exit__(None, None, None)
    return nc



def build_s2(nc, io):
    xnTb = io["xnT_bf"]
    f0 = io["f0"]
    gate_wt = io["s2_gate_wt"]
    gate_bias = io["s2_gate_bias"]
    w1t = io["s2_w1t"]
    w3t = io["s2_w3t"]
    w2t = io["s2_w2t"]
    NE = 9
    NB = 16

    with ExitStack() as _es:
        ec = _es.enter_context
        xg = ec(nc.sbuf_tensor("s2_xg", [128, 2, 8, 128], BF16))
        gw = ec(nc.sbuf_tensor("s2_gw", [128, 8, 8], BF16))
        gb = ec(nc.sbuf_tensor("s2_gb", [128, 8], F32))
        gsc = ec(nc.sbuf_tensor("s2_gsc", [128, NT, 48], F32))
        comb = ec(nc.sbuf_tensor("s2_comb", [128, NT, 12], F32))
        xT = ec(nc.sbuf_tensor("s2_xT", [128, 2, 8, 512], BF16))
        w13 = ec(nc.sbuf_tensor("s2_w13", [128, 2, 8, 16, 128], BF16))
        w2s = ec(nc.sbuf_tensor("s2_w2", [128, 16, 1024], BF16))
        s1 = ec(nc.sbuf_tensor("s2_s1", [128, 2, 512], F32))
        hT = ec(nc.sbuf_tensor("s2_hT", [128, 16, 512], BF16))
        ft = ec(nc.sbuf_tensor("s2_ft", [128, 2, 4, 1024], F32))
        ph = ec(nc.psum_tensor("s2_ph", [128, 4, 512], F32))
        py = ec(nc.psum_tensor("s2_py", [128, 2, 512], F32))
        pg = ec(nc.psum_tensor("s2_pg", [128, 2, 512], F32))
        lgp = [nc.alloc_semaphore(name=f"s2_lg{p}") for p in range(2)]
        ltp = [nc.alloc_semaphore(name=f"s2_lt{p}") for p in range(2)]
        ldc = nc.alloc_semaphore(name="s2_ldc")
        lw = nc.alloc_semaphore(name="s2_lw")
        tm = nc.alloc_semaphore(name="s2_tm")
        vm = nc.alloc_semaphore(name="s2_vm")
        am = nc.alloc_semaphore(name="s2_am")
        st = nc.alloc_semaphore(name="s2_st")
        stb = [nc.alloc_semaphore(name=f"s2_stb{b}") for b in range(NB)]
        block = ec(nc.Block())
        TM0, VM0, AM0 = NT, 2 * NT, NT

        def tm_h(e, b, ic):
            return TM0 + e * 384 + b * 24 + ic + 1

        def tm_w2(e, b, j):
            return TM0 + e * 384 + b * 24 + 16 + j + 1

        def vm_hmult(gh):
            eb, ic = divmod(gh, 16)
            return VM0 + eb * 24 + ic + 1

        def vm_evac(gy):
            eb, j = divmod(gy, 8)
            return VM0 + eb * 24 + 16 + j + 1

        def am_silu(gh):
            return AM0 + gh + 1

        @block.sync
        def _(sync):
            sync.dma_start(out=gb[:], in_=gate_bias[:, :]).then_inc(ldc, 16)
            for i in range(NT):
                if i >= 2:
                    sync.wait_ge(tm, i - 1)
                sync.dma_start(out=xg[:, i % 2, :, :],
                               in_=xnTb.rearrange("(fc f) t -> f fc t", f=128)
                               [:, :, i * 128:(i + 1) * 128]).then_inc(lgp[i % 2], 16)
            for e in range(NE):
                for b in range(NB):
                    g = e * NB + b
                    if g >= 2:
                        sync.wait_ge(tm, TM0 + (g - 1) * 24)
                    sync.dma_start(out=xT[:, g % 2, :, :],
                                   in_=xnTb.rearrange("(fc f) t -> f fc t", f=128)
                                   [:, :, b * 512:(b + 1) * 512]).then_inc(ltp[g % 2], 16)

        @block.gpsimd
        def _(gpsimd):
            gpsimd.memset(comb[:, :, 8:9], 1.0)
            gpsimd.dma_start(out=gw[:],
                             in_=gate_wt.rearrange("(fc f) e -> f fc e", f=128)).then_inc(lw, 16)
            for e in range(NE):
                if e >= 1:
                    gpsimd.wait_ge(tm, TM0 + e * 384)
                gpsimd.dma_start(out=w13[:, 0, :, :, :],
                                 in_=w1t[e].rearrange("(fc f) (ic i) -> f fc ic i",
                                                      f=128, i=128)).then_inc(lw, 16)
                gpsimd.dma_start(out=w13[:, 1, :, :, :],
                                 in_=w3t[e].rearrange("(fc f) (ic i) -> f fc ic i",
                                                      f=128, i=128)).then_inc(lw, 16)
                gpsimd.dma_start(out=w2s[:, :, :],
                                 in_=w2t[e].rearrange("(ic i) o -> i ic o",
                                                      i=128)).then_inc(lw, 16)
                for b in range(NB):
                    gst = e * NB + b
                    gpsimd.wait_ge(vm, vm_evac(gst * 8 + 7))
                    if e >= 1:
                        gpsimd.wait_ge(stb[b], 16 * e)
                    dst = f0.rearrange("(bb p) c -> p bb c", p=128)[:, 4 * b:4 * b + 4, :]
                    if e == 0:
                        gpsimd.dma_start(out=dst, in_=ft[:, b % 2, :, :]).then_inc(stb[b], 16)
                    else:
                        gpsimd.dma_start(out=dst, in_=ft[:, b % 2, :, :],
                                         accum_op=ALU.add).then_inc(stb[b], 16)
            for b in range(NB):
                gpsimd.wait_ge(stb[b], 16 * NE)

        @block.tensor
        def _(tensor):
            tensor.wait_ge(ldc, 16)
            tensor.wait_ge(lw, 16)
            for i in range(NT):
                tensor.wait_ge(lgp[i % 2], 16 * (i // 2 + 1))
                if i >= 2:
                    tensor.wait_ge(vm, 2 * (i - 2) + 1)
                mm = None
                for k in range(8):
                    mm = tensor.matmul(pg[:, i % 2, 0:8], xg[:, i % 2, k, :], gw[:, k, :],
                                       start=(k == 0), stop=(k == 7))
                mm.then_inc(tm, 1)
            for e in range(NE):
                tensor.wait_ge(lw, 16 + 48 * (e + 1))
                for b in range(NB):
                    g = e * NB + b
                    p = g % 2
                    tensor.wait_ge(ltp[p], 16 * (g // 2 + 1))
                    for ic in range(16):
                        gh = g * 16 + ic
                        bk = 2 * (gh % 2)
                        if gh >= 2:
                            tensor.wait_ge(am, am_silu(gh - 2))
                            tensor.wait_ge(vm, vm_hmult(gh - 2))
                        for fc in range(8):
                            tensor.matmul(ph[:, bk, :], w13[:, 0, fc, ic, :],
                                          xT[:, p, fc, :], start=(fc == 0), stop=(fc == 7))
                        mm = None
                        for fc in range(8):
                            mm = tensor.matmul(ph[:, bk + 1, :], w13[:, 1, fc, ic, :],
                                               xT[:, p, fc, :], start=(fc == 0), stop=(fc == 7))
                        mm.then_inc(tm, 1)
                    tensor.wait_ge(vm, vm_hmult(g * 16 + 15))
                    for j in range(8):
                        s, oc = divmod(j, 2)
                        gy = g * 8 + j
                        if gy >= 2:
                            tensor.wait_ge(vm, vm_evac(gy - 2))
                        mm = None
                        for ic in range(16):
                            mm = tensor.matmul(py[:, gy % 2, :],
                                               hT[:, ic, s * 128:(s + 1) * 128],
                                               w2s[:, ic, oc * 512:(oc + 1) * 512],
                                               start=(ic == 0), stop=(ic == 15))
                        mm.then_inc(tm, 1)

        @block.scalar
        def _(scalar):
            for i in range(NT):
                scalar.wait_ge(vm, 2 * i + 1)
                scalar.activation(gsc[:, i, 16:24], gsc[:, i, 1:9], AF.Exp)
                scalar.drain().then_inc(am, 1)
            for e in range(NE):
                for b in range(NB):
                    g = e * NB + b
                    for ic in range(16):
                        gh = g * 16 + ic
                        bk = 2 * (gh % 2)
                        scalar.wait_ge(tm, tm_h(e, b, ic))
                        if gh >= 2:
                            scalar.wait_ge(vm, vm_hmult(gh - 2))
                        scalar.activation(s1[:, gh % 2, :], ph[:, bk, :], AF.Silu)
                        scalar.drain().then_inc(am, 1)

        @block.vector
        def _(vector):
            for i in range(NT):
                vector.wait_ge(tm, i + 1)
                pgi = pg[:, i % 2, 0:8]
                vector.reduce_max(gsc[:, i, 0:1], pgi, AX.X)
                vector.drain()
                vector.tensor_scalar(gsc[:, i, 1:9], pgi, gsc[:, i, 0:1], None,
                                     ALU.subtract).then_inc(vm, 1)
                vector.wait_ge(am, i + 1)
                exe = gsc[:, i, 16:24]
                vector.reduce_sum(gsc[:, i, 9:10], exe, AX.X)
                vector.drain()
                vector.reciprocal(gsc[:, i, 10:11], gsc[:, i, 9:10])
                vector.drain()
                vector.tensor_scalar(gsc[:, i, 24:32], exe, gsc[:, i, 10:11], None, ALU.mult)
                vector.drain()
                vector.tensor_tensor(gsc[:, i, 32:40], gsc[:, i, 24:32], gb[:, :], ALU.add)
                vector.drain()
                vector.reduce_max(gsc[:, i, 11:12], gsc[:, i, 32:40], AX.X)
                vector.drain()
                vector.tensor_scalar(gsc[:, i, 1:9], gsc[:, i, 32:40], gsc[:, i, 11:12], None, ALU.is_ge)
                vector.drain()
                vector.scalar_tensor_tensor(gsc[:, i, 16:24], gsc[:, i, 1:9], -1e30,
                                            gsc[:, i, 32:40], ALU.mult, ALU.add)
                vector.drain()
                vector.reduce_max(gsc[:, i, 12:13], gsc[:, i, 16:24], AX.X)
                vector.drain()
                vector.tensor_scalar(gsc[:, i, 32:40], gsc[:, i, 16:24], gsc[:, i, 12:13], None, ALU.is_ge)
                vector.drain()
                vector.tensor_tensor(gsc[:, i, 1:9], gsc[:, i, 1:9], gsc[:, i, 32:40], ALU.add)
                vector.drain()
                vector.tensor_tensor(comb[:, i, 0:8], gsc[:, i, 24:32], gsc[:, i, 1:9],
                                     ALU.mult).then_inc(vm, 1)
            for e in range(NE):
                for b in range(NB):
                    g = e * NB + b
                    for ic in range(16):
                        gh = g * 16 + ic
                        bk = 2 * (gh % 2)
                        vector.wait_ge(am, am_silu(gh))
                        vector.tensor_tensor(hT[:, ic, :], s1[:, gh % 2, :],
                                             ph[:, bk + 1, :], ALU.mult).then_inc(vm, 1)
                    for j in range(8):
                        s, oc = divmod(j, 2)
                        gy = g * 8 + j
                        vector.wait_ge(tm, tm_w2(e, b, j))
                        if j == 0 and g >= 2:
                            bp, ep = (b - 2, e) if b >= 2 else (b + 14, e - 1)
                            vector.wait_ge(stb[bp], 16 * (ep + 1))
                        vector.tensor_scalar(ft[:, b % 2, s, oc * 512:(oc + 1) * 512],
                                             py[:, gy % 2, :],
                                             comb[:, 4 * b + s, e:e + 1], None,
                                             ALU.mult).then_inc(vm, 1)
    return nc



_ENG_OF = {"sy": "sync", "sc": "scalar", "scd": "scalar", "ve": "vector",
           "te": "tensor", "gp": "gpsimd", "gpd": "gpsimd"}
_IS_DMA = {"sy": True, "sc": False, "scd": True, "ve": False, "te": False,
           "gp": False, "gpd": True}


_SCHED_N = [0]


class Sched:
    def __init__(self):
        _SCHED_N[0] += 1
        self.uid = _SCHED_N[0]
        self.steps = []
        self.seq = []

    def step(self, eng, fn, deps=(), no_drain=False):
        sid = len(self.steps)
        self.steps.append((eng, fn, [d for d in deps if d is not None], no_drain))
        return sid

    def emit(self, nc, block):
        cnt = {}
        for eng, _, _, _ in self.steps:
            cnt[eng] = cnt.get(eng, 0) + 1
            self.seq.append(cnt[eng])
        sems = {eng: nc.alloc_semaphore(name=f"sch_{eng}_{self.uid}")
                for eng in cnt}
        phys = sorted({_ENG_OF[e] for e in cnt})

        def section(qname):
            def body(q):
                for sid, (eng, fn, deps, no_drain) in enumerate(self.steps):
                    if _ENG_OF[eng] != qname:
                        continue
                    need_drain = False
                    waits = {}
                    for d in deps:
                        peng = self.steps[d][0]
                        pseq = self.seq[d]
                        if peng == eng and not _IS_DMA[eng] and not _IS_DMA[peng]:
                            if not (no_drain or eng == "te"):
                                need_drain = True
                            continue
                        if _ENG_OF[peng] == qname and not _IS_DMA[peng] and _IS_DMA[eng]:
                            pass
                        v = pseq * (16 if _IS_DMA[peng] else 1)
                        waits[peng] = max(waits.get(peng, 0), v)
                    if _IS_DMA[eng] and self.seq[sid] > 1:
                        waits[eng] = max(waits.get(eng, 0), (self.seq[sid] - 1) * 16)
                    for peng, v in waits.items():
                        if peng == eng and not _IS_DMA[eng]:
                            continue
                        q.wait_ge(sems[peng], v)
                    if need_drain:
                        q.drain()
                    inst = fn(q)
                    assert inst is not None, f"step {sid} on {eng} returned no instruction"
                    inst.then_inc(sems[eng], 16 if _IS_DMA[eng] else 1)
            return body

        for qname in phys:
            getattr(block, qname)(section(qname))

        def final(q):
            for eng in sorted(cnt):
                if _IS_DMA[eng]:
                    q.wait_ge(sems[eng], cnt[eng] * 16)
        block.sync(final)
        return sems, cnt


class Res:
    __slots__ = ("w", "r")

    def __init__(self):
        self.w = None
        self.r = []


def mk_S(sch):
    def S(eng, fn, reads=(), writes=(), extra=(), no_drain=False):
        deps = list(extra)
        for r in reads:
            if r.w is not None:
                deps.append(r.w)
        for w in writes:
            deps.extend(w.r)
            if w.w is not None:
                deps.append(w.w)
        sid = sch.step(eng, fn, deps, no_drain=no_drain)
        for r in reads:
            r.r.append(sid)
        for w in writes:
            w.w = sid
            w.r = []
        return sid
    return S



def build_s3(nc, io):
    xnTb = io["xnT_bf"]
    NB = 16
    o_w, o_r, o_k, o_kk = io["w_d"], io["r_d"], io["k_d"], io["kk_d"]
    o_v, o_b, o_g = io["v_d"], io["b_d"], io["g_d"]
    o_bs, o_wlm, o_mcs = io["bsum_d"], io["wlm_d"], io["mcs_d"]
    G_ORDER = [0, 3, 5, 1, 4, 2]

    with ExitStack() as _es:
        ec = _es.enter_context
        sb = lambda *a: ec(nc.sbuf_tensor(*a))
        cc = sb("s3_cc", [128, 8], F32)
        xa = sb("s3s_xa", [128, 8, 192], BF16)
        xb = sb("s3s_xb", [128, 2, 1024], BF16)
        lam = sb("s3s_lam", [128, 48], F32)
        xmiu = sb("s3s_xmiu", [128, 8], F32)
        wa = sb("s3s_wa", [128, 8, 64], BF16)
        ka = sb("s3s_ka", [128, 8, 64], BF16)
        aa = sb("s3s_aa", [128, 8, 64], BF16)
        ga = sb("s3s_ga", [128, 8, 128], BF16)
        wbb = sb("s3s_wb", [64, 1024], BF16)
        kbb = sb("s3s_kb", [64, 1024], BF16)
        abb = sb("s3s_ab", [64, 1024], BF16)
        gbb = sb("s3s_gb", [128, 1024], BF16)
        wmiu = sb("s3s_wmiu", [128, 1024], F32)
        atim = sb("s3s_atim", [128, 1024], F32)
        xtim = sb("s3s_xtim", [128, 1024], F32)
        rwt = sb("s3s_rwt", [128, 8, 1024], BF16)
        kwt = sb("s3s_kwt", [128, 8, 1024], BF16)
        vwt = sb("s3s_vwt", [128, 8, 1024], BF16)
        c3 = sb("s3s_c3", [128, 4], F32)
        win = sb("s3_win", [128, 8, 513], BF16)
        dxt = sb("s3_dx", [128, 8, 512], BF16)
        xmt = sb("s3_xm", [128, 8, 512], BF16)
        mixT = sb("s3_mix", [128, 2, 512], BF16)
        xd = sb("s3_xd", [128, 8, 512], BF16)
        mt = sb("s3_mt", [128, 4, 512], F32)
        lr = sb("s3_lr", [128, 2, 512], BF16)
        rstg = sb("s3_rstg", [128, 4, 1024], BF16)
        kstg = sb("s3_kstg", [128, 4, 1024], BF16)
        wstg = sb("s3_wstg", [128, 4, 1024], F32)
        sig = sb("s3_sig", [128, 4, 1024], BF16)
        vstg = sb("s3_vstg", [128, 1024], BF16)
        gstg = sb("s3_gstg", [128, 1024], BF16)
        kkstg = sb("s3_kkstg", [128, 1024], BF16)
        bstg = sb("s3_bstg", [128, 1024], BF16)
        cs = sb("s3_cs", [128, 2, 1024], F32)
        cs2 = sb("s3_cs2", [128, 1024], F32)
        sqd = sb("s3_sqd", [128, 1024], F32)
        kcol = sb("s3_kcol", [128, 2, 8], F32)
        bscol = sb("s3_bs", [128, 2, 16], F32)
        auxs = sb("s3_aux", [4, 2, 1024], F32)
        zrow = sb("s3_zrow", [1, 1024], F32)
        pm = ec(nc.psum_tensor("s3_pm", [128, 2, 512], F32))
        pxd = ec(nc.psum_tensor("s3_pxd", [128, 4, 512], F32))
        ppj = ec(nc.psum_tensor("s3_ppj", [128, 2, 512], F32))

        sch = Sched()
        S = mk_S(sch)
        r_win, r_dx, r_xm, r_xd, r_sqd = Res(), Res(), Res(), Res(), Res()
        r_mix = [Res(), Res()]
        r_pm = [Res(), Res()]
        r_pxd = [Res() for _ in range(4)]
        r_ppj = [Res(), Res()]
        r_mt = [Res() for _ in range(4)]
        r_lr = [Res() for _ in range(4)]
        r_rstg, r_kstg, r_wstg, r_sig = Res(), Res(), Res(), Res()
        r_vstg, r_gstg, r_kkstg, r_bstg = Res(), Res(), Res(), Res()
        r_cs = [Res(), Res()]
        r_cs2 = [Res(), Res()]
        r_kcol = [Res(), Res()]
        r_bscol = [Res(), Res()]
        r_aux = [Res(), Res()]
        r_cst = Res()

        def ld(q, dst, src):
            return q.dma_start(out=dst, in_=src)

        S("gpd", lambda q: ld(q, xa[:], io["s3_xA"].rearrange("(fc f) j -> f fc j", f=128)),
          writes=[r_cst])
        for g in range(6):
            S("gpd", lambda q, g=g: ld(q, xb[32 * (g % 3):32 * (g % 3) + 32, g // 3, :],
                                       io["s3_xB"][g]), writes=[r_cst])
        for nm, dst in [("s3_wA", wa), ("s3_kA", ka), ("s3_aA", aa)]:
            S("gpd", lambda q, nm=nm, dst=dst: ld(
                q, dst[:], io[nm].rearrange("(fc f) j -> f fc j", f=128)), writes=[r_cst])
        S("gpd", lambda q: ld(q, ga[:], io["s3_gA"].rearrange("(fc f) j -> f fc j", f=128)),
          writes=[r_cst])
        for nm, dst in [("s3_wB", wbb), ("s3_kB", kbb), ("s3_aB", abb)]:
            S("gpd", lambda q, nm=nm, dst=dst: ld(q, dst[:], io[nm][:, :]), writes=[r_cst])
        S("gpd", lambda q: ld(q, gbb[:], io["s3_gB"][:, :]), writes=[r_cst])
        for nm, dst in [("s3_rWt", rwt), ("s3_kWt", kwt), ("s3_vWt", vwt)]:
            S("gpd", lambda q, nm=nm, dst=dst: ld(
                q, dst[:], io[nm].rearrange("(fc f) o -> f fc o", f=128)), writes=[r_cst])
        for nm, dst in [("s3_lam", lam), ("s3_xmiu", xmiu), ("s3_wmiu_t", wmiu),
                        ("s3_atime_t", atim), ("s3_xtime_t", xtim), ("s3_c3aux", c3),
                        ("c_consts", cc)]:
            S("sy", lambda q, nm=nm, dst=dst: ld(q, dst[:], io[nm][:, :]), writes=[r_cst])

        sz = S("ve", lambda q: q.memset(zrow[:], 0.0))
        S("scd", lambda q: ld(q, o_mcs[NCH - 1:NCH, :], zrow[:]), extra=[sz])
        S("ve", lambda q: q.memset(win[:, :, 0:1], 0.0), writes=[r_win])

        xnTr = xnTb.rearrange("(fc f) t -> f fc t", f=128)

        for b in range(NB):
            t0 = b * 512
            if b == 0:
                S("sy", lambda q: ld(q, win[:, :, 1:513], xnTr[:, :, 0:512]),
                  reads=[r_cst], writes=[r_win])
            else:
                S("sy", lambda q, t0=t0: ld(q, win[:, :, 0:513],
                                            xnTr[:, :, t0 - 1:t0 + 512]),
                  writes=[r_win])
            S("ve", lambda q: q.tensor_tensor(dxt[:, :, :], win[:, :, 0:512],
                                              win[:, :, 1:513], ALU.subtract),
              reads=[r_win], writes=[r_dx])

            def xm_fn(q):
                ii = None
                for fc in range(8):
                    ii = q.scalar_tensor_tensor(xmt[:, fc, :], dxt[:, fc, :],
                                                xmiu[:, fc:fc + 1], win[:, fc, 1:513],
                                                ALU.mult, ALU.add)
                return ii
            S("ve", xm_fn, reads=[r_dx, r_win, r_cst], writes=[r_xm])

            for cg in range(2):
                def mix_mm(q, cg=cg):
                    ii = None
                    for gi in range(3):
                        g = cg * 3 + gi
                        for fc in range(8):
                            ii = q.matmul(pm[32 * gi:32 * gi + 32, cg, :],
                                          xa[:, fc, 32 * g:32 * g + 32],
                                          xmt[:, fc, :],
                                          start=(fc == 0), stop=(fc == 7))
                    return ii
                S("te", mix_mm, reads=[r_xm, r_cst], writes=[r_pm[cg]])

                def mix_tanh(q, cg=cg):
                    ii = None
                    for gi in range(3):
                        ii = q.activation(mixT[32 * gi:32 * gi + 32, cg, :],
                                          pm[32 * gi:32 * gi + 32, cg, :], AF.Tanh)
                    return ii
                S("sc", mix_tanh, reads=[r_pm[cg]], writes=[r_mix[cg]])

            for g in G_ORDER:
                gi, cg = g % 3, g // 3
                for wv in range(2):
                    fcs = tuple(range(4 * wv, 4 * wv + 4))

                    def miu_mm(q, gi=gi, cg=cg, fcs=fcs):
                        ii = None
                        for fc in fcs:
                            ii = q.matmul(pxd[:, fc % 4, :],
                                          xb[32 * gi:32 * gi + 32, cg,
                                             fc * 128:(fc + 1) * 128],
                                          mixT[32 * gi:32 * gi + 32, cg, :],
                                          start=True, stop=True)
                        return ii
                    S("te", miu_mm, reads=[r_mix[cg], r_cst],
                      writes=[r_pxd[fc % 4] for fc in fcs])

                    def miu_stt(q, g=g, fcs=fcs):
                        ii = None
                        for fc in fcs:
                            ii = q.scalar_tensor_tensor(mt[:, fc % 4, :], pxd[:, fc % 4, :],
                                                        lam[:, g * 8 + fc:g * 8 + fc + 1],
                                                        dxt[:, fc, :], ALU.add, ALU.mult)
                        return ii
                    S("ve", miu_stt, reads=[r_pxd[fc % 4] for fc in fcs] + [r_dx, r_cst],
                      writes=[r_mt[fc % 4] for fc in fcs])

                    def xd_add(q, fcs=fcs):
                        ii = None
                        for fc in fcs:
                            ii = q.tensor_tensor(xd[:, fc, :], mt[:, fc % 4, :],
                                                 win[:, fc, 1:513], ALU.add)
                        return ii
                    S("ve", xd_add, reads=[r_mt[fc % 4] for fc in fcs] + [r_win],
                      writes=[r_xd])

                if g in (0, 3):
                    wt = rwt if g == 0 else vwt
                    for s in range(4):
                        def proj_mm(q, wt=wt, s=s):
                            ii = None
                            for oc in range(2):
                                for fc in range(8):
                                    ii = q.matmul(ppj[:, oc, :],
                                                  xd[:, fc, s * 128:(s + 1) * 128],
                                                  wt[:, fc, oc * 512:(oc + 1) * 512],
                                                  start=(fc == 0), stop=(fc == 7))
                            return ii
                        S("te", proj_mm, reads=[r_xd, r_cst], writes=[r_ppj[0], r_ppj[1]])
                        if g == 0:
                            def ev_r(q, s=s):
                                q.tensor_copy(rstg[:, s, 0:512], ppj[:, 0, :])
                                return q.tensor_copy(rstg[:, s, 512:1024], ppj[:, 1, :])
                            S("ve", ev_r, reads=[r_ppj[0], r_ppj[1]], writes=[r_rstg])
                            S("gpd", lambda q, t0=t0, s=s: ld(
                                q, o_r[t0 + s * 128:t0 + (s + 1) * 128, :], rstg[:, s, :]),
                                reads=[r_rstg])
                        else:
                            def ev_v(q):
                                q.tensor_copy(vstg[:, 0:512], ppj[:, 0, :])
                                return q.tensor_copy(vstg[:, 512:1024], ppj[:, 1, :])
                            S("ve", ev_v, reads=[r_ppj[0], r_ppj[1]], writes=[r_vstg])
                            S("scd", lambda q, t0=t0, s=s: ld(
                                q, o_v[t0 + s * 128:t0 + (s + 1) * 128, :], vstg[:]),
                                reads=[r_vstg])
                elif g == 5:
                    def g1_mm(q):
                        ii = None
                        for fc in range(8):
                            ii = q.matmul(pxd[:, 0, :], ga[:, fc, :], xd[:, fc, :],
                                          start=(fc == 0), stop=(fc == 7))
                        return ii
                    S("te", g1_mm, reads=[r_xd, r_cst], writes=[r_pxd[0]])
                    S("sc", lambda q: q.activation(lr[:, 0, :], pxd[:, 0, :], AF.Tanh),
                      reads=[r_pxd[0]], writes=[r_lr[0]])
                    for s in range(4):
                        def g2_mm(q, s=s):
                            ii = None
                            for oc in range(2):
                                ii = q.matmul(ppj[:, oc, :],
                                              lr[:, 0, s * 128:(s + 1) * 128],
                                              gbb[:, oc * 512:(oc + 1) * 512],
                                              start=True, stop=True)
                            return ii
                        S("te", g2_mm, reads=[r_lr[0], r_cst], writes=[r_ppj[0], r_ppj[1]])

                        def ev_g(q):
                            q.tensor_copy(gstg[:, 0:512], ppj[:, 0, :])
                            return q.tensor_copy(gstg[:, 512:1024], ppj[:, 1, :])
                        S("ve", ev_g, reads=[r_ppj[0], r_ppj[1]], writes=[r_gstg])
                        S("scd", lambda q, t0=t0, s=s: ld(
                            q, o_g[t0 + s * 128:t0 + (s + 1) * 128, :], gstg[:]),
                            reads=[r_gstg])
                elif g == 1:
                    def w1_mm(q):
                        ii = None
                        for fc in range(8):
                            ii = q.matmul(pxd[0:64, 1, :], wa[:, fc, :], xd[:, fc, :],
                                          start=(fc == 0), stop=(fc == 7))
                        return ii
                    S("te", w1_mm, reads=[r_xd, r_cst], writes=[r_pxd[1]])
                    S("sc", lambda q: q.activation(lr[0:64, 1, :], pxd[0:64, 1, :], AF.Tanh),
                      reads=[r_pxd[1]], writes=[r_lr[1]])
                    for s in range(4):
                        sp2 = s % 2

                        def w2_mm(q, s=s):
                            ii = None
                            for oc in range(2):
                                ii = q.matmul(ppj[:, oc, :],
                                              lr[0:64, 1, s * 128:(s + 1) * 128],
                                              wbb[:, oc * 512:(oc + 1) * 512],
                                              start=True, stop=True)
                            return ii
                        S("te", w2_mm, reads=[r_lr[1], r_cst], writes=[r_ppj[0], r_ppj[1]])

                        def w_u(q, sp2=sp2):
                            q.tensor_tensor(cs[:, sp2, 0:512], ppj[:, 0, :],
                                            wmiu[:, 0:512], ALU.add)
                            return q.tensor_tensor(cs[:, sp2, 512:1024], ppj[:, 1, :],
                                                   wmiu[:, 512:1024], ALU.add)
                        S("ve", w_u, reads=[r_ppj[0], r_ppj[1], r_cst], writes=[r_cs[sp2]])
                        S("sc", lambda q, sp2=sp2: q.activation(cs2[:],
                                                                cs[:, sp2, :], AF.Abs),
                          reads=[r_cs[sp2]], writes=[r_cs2[0]])
                        S("sc", lambda q, sp2=sp2: q.activation(sqd[:], cs2[:],
                                                                AF.Exp, scale=-1.0),
                          reads=[r_cs2[0]], writes=[r_sqd])
                        S("sc", lambda q, sp2=sp2: q.activation(cs2[:], sqd[:],
                                                                AF.Ln, bias=cc[:, 2:3]),
                          reads=[r_sqd, r_cst], writes=[r_cs2[0]])
                        S("ve", lambda q, sp2=sp2: q.tensor_scalar(
                            mt[:, 0:2, :].rearrange("p a b -> p (a b)"), cs[:, sp2, :],
                            0.0, None, ALU.min),
                          reads=[r_cs[sp2]], writes=[r_mt[0], r_mt[1]])
                        S("ve", lambda q, s=s, sp2=sp2: q.scalar_tensor_tensor(
                            wstg[:, s, :], mt[:, 0:2, :].rearrange("p a b -> p (a b)"),
                            -0.5, cs2[:], ALU.add, ALU.subtract),
                          reads=[r_mt[0], r_mt[1], r_cs2[0]], writes=[r_wstg])
                        S("scd", lambda q, t0=t0, s=s: ld(
                            q, o_w[t0 + s * 128:t0 + (s + 1) * 128, :], wstg[:, s, :]),
                            reads=[r_wstg])

                        def aux_mm(q, s=s):
                            q.matmul(pxd[0:4, 2, :], c3[:], wstg[:, s, 0:512],
                                     start=True, stop=True)
                            return q.matmul(pxd[0:4, 3, :], c3[:], wstg[:, s, 512:1024],
                                            start=True, stop=True)
                        S("te", aux_mm, reads=[r_wstg, r_cst],
                          writes=[r_pxd[2], r_pxd[3]])

                        def aux_ev(q, sp2=sp2):
                            q.tensor_copy(auxs[:, sp2, 0:512], pxd[0:4, 2, :])
                            return q.tensor_copy(auxs[:, sp2, 512:1024], pxd[0:4, 3, :])
                        S("ve", aux_ev, reads=[r_pxd[2], r_pxd[3]], writes=[r_aux[sp2]])
                        c0 = b * 8 + 2 * s
                        if c0 == 0:
                            S("scd", lambda q, sp2=sp2: ld(
                                q, o_mcs[0:1, :], auxs[1:2, sp2, :]), reads=[r_aux[sp2]])
                        else:
                            S("scd", lambda q, c0=c0, sp2=sp2: ld(
                                q, o_mcs[c0 - 1:c0 + 1, :], auxs[0:2, sp2, :]),
                                reads=[r_aux[sp2]])
                        S("scd", lambda q, c0=c0, sp2=sp2: ld(
                            q, o_wlm[c0:c0 + 2, :], auxs[2:4, sp2, :]),
                            reads=[r_aux[sp2]])
                elif g == 4:
                    def a1_mm(q):
                        ii = None
                        for fc in range(8):
                            ii = q.matmul(pxd[0:64, 1, :], aa[:, fc, :], xd[:, fc, :],
                                          start=(fc == 0), stop=(fc == 7))
                        return ii
                    S("te", a1_mm, reads=[r_xd, r_cst], writes=[r_pxd[1]])
                    S("ve", lambda q: q.tensor_copy(lr[0:64, 1, :], pxd[0:64, 1, :]),
                      reads=[r_pxd[1]], writes=[r_lr[1]])
                    for s in range(4):
                        sp2 = s % 2

                        def a2_mm(q, s=s):
                            ii = None
                            for oc in range(2):
                                ii = q.matmul(ppj[:, oc, :],
                                              lr[0:64, 1, s * 128:(s + 1) * 128],
                                              abb[:, oc * 512:(oc + 1) * 512],
                                              start=True, stop=True)
                            return ii
                        S("te", a2_mm, reads=[r_lr[1], r_cst], writes=[r_ppj[0], r_ppj[1]])

                        def a_pre(q, sp2=sp2):
                            q.tensor_tensor(cs[:, sp2, 0:512], ppj[:, 0, :],
                                            atim[:, 0:512], ALU.add)
                            return q.tensor_tensor(cs[:, sp2, 512:1024], ppj[:, 1, :],
                                                   atim[:, 512:1024], ALU.add)
                        S("ve", a_pre, reads=[r_ppj[0], r_ppj[1], r_cst], writes=[r_cs[sp2]])
                        S("sc", lambda q, s=s, sp2=sp2: q.activation(
                            sig[:, s, :], cs[:, sp2, :], AF.Sigmoid),
                          reads=[r_cs[sp2]], writes=[r_sig])
                else:
                    def k1_mm(q):
                        ii = None
                        for fc in range(8):
                            ii = q.matmul(pxd[0:64, 1, :], ka[:, fc, :], xd[:, fc, :],
                                          start=(fc == 0), stop=(fc == 7))
                        return ii
                    S("te", k1_mm, reads=[r_xd, r_cst], writes=[r_pxd[1]])
                    S("sc", lambda q: q.activation(lr[0:64, 1, :], pxd[0:64, 1, :], AF.Tanh),
                      reads=[r_pxd[1]], writes=[r_lr[1]])
                    for s in range(4):
                        sp2 = s % 2

                        def k_mm(q, s=s):
                            ii = None
                            for oc in range(2):
                                for fc in range(8):
                                    q.matmul(ppj[:, oc, :],
                                             xd[:, fc, s * 128:(s + 1) * 128],
                                             kwt[:, fc, oc * 512:(oc + 1) * 512],
                                             start=(fc == 0), stop=False)
                                ii = q.matmul(ppj[:, oc, :],
                                              lr[0:64, 1, s * 128:(s + 1) * 128],
                                              kbb[:, oc * 512:(oc + 1) * 512],
                                              start=False, stop=True)
                            return ii
                        S("te", k_mm, reads=[r_xd, r_lr[1], r_cst],
                          writes=[r_ppj[0], r_ppj[1]])

                        def k_ev(q, sp2=sp2):
                            q.tensor_copy(cs[:, sp2, 0:512], ppj[:, 0, :])
                            return q.tensor_copy(cs[:, sp2, 512:1024], ppj[:, 1, :])
                        S("ve", k_ev, reads=[r_ppj[0], r_ppj[1]], writes=[r_cs[sp2]])
                        S("sc", lambda q, sp2=sp2: q.activation(
                            sqd[:], cs[:, sp2, :], AF.Square, accum_out=kcol[:, sp2, 0:1]),
                          reads=[r_cs[sp2]], writes=[r_sqd, r_kcol[sp2]])
                        S("sc", lambda q, sp2=sp2: q.activation(
                            kcol[:, sp2, 1:2], kcol[:, sp2, 0:1], AF.Sqrt),
                          reads=[r_kcol[sp2]], writes=[r_kcol[sp2]])
                        S("ve", lambda q, sp2=sp2: q.tensor_scalar(
                            kcol[:, sp2, 2:3], kcol[:, sp2, 1:2], 1e-12, None, ALU.max),
                          reads=[r_kcol[sp2]], writes=[r_kcol[sp2]])
                        S("ve", lambda q, sp2=sp2: q.reciprocal(
                            kcol[:, sp2, 3:4], kcol[:, sp2, 2:3]),
                          reads=[r_kcol[sp2]], writes=[r_kcol[sp2]])
                        S("ve", lambda q, s=s, sp2=sp2: q.tensor_scalar(
                            kstg[:, s, :], cs[:, sp2, :], kcol[:, sp2, 3:4], None,
                            ALU.mult),
                          reads=[r_cs[sp2], r_kcol[sp2]], writes=[r_kstg])
                        S("gpd", lambda q, t0=t0, s=s: ld(
                            q, o_k[t0 + s * 128:t0 + (s + 1) * 128, :], kstg[:, s, :]),
                            reads=[r_kstg])
                        S("ve", lambda q, s=s: q.tensor_scalar(
                            cs2[:], wstg[:, s, :], 0.0, None, ALU.min),
                          reads=[r_wstg], writes=[r_cs2[0]])
                        S("sc", lambda q: q.activation(
                            sqd[:], cs2[:], AF.Exp, scale=0.5),
                          reads=[r_cs2[0]], writes=[r_sqd])
                        S("ve", lambda q, s=s: q.tensor_tensor(
                            kkstg[:], kstg[:, s, :], sqd[:], ALU.mult),
                          reads=[r_kstg, r_sqd], writes=[r_kkstg])
                        S("gpd", lambda q, t0=t0, s=s: ld(
                            q, o_kk[t0 + s * 128:t0 + (s + 1) * 128, :], kkstg[:]),
                            reads=[r_kkstg])
                        S("ve", lambda q, s=s: q.scalar_tensor_tensor(
                            bstg[:], kstg[:, s, :], 2.0, sig[:, s, :], ALU.mult, ALU.mult),
                          reads=[r_kstg, r_sig], writes=[r_bstg])
                        S("gpd", lambda q, t0=t0, s=s: ld(
                            q, o_b[t0 + s * 128:t0 + (s + 1) * 128, :], bstg[:]),
                            reads=[r_bstg])
                        S("ve", lambda q, s=s, sp2=sp2: q.tensor_tensor(
                            cs[:, sp2, :], rstg[:, s, :], kstg[:, s, :], ALU.mult),
                          reads=[r_rstg, r_kstg], writes=[r_cs[sp2]])
                        S("ve", lambda q, sp2=sp2: q.tensor_tensor(
                            cs2[:], cs[:, sp2, :], xtim[:], ALU.mult),
                          reads=[r_cs[sp2], r_cst], writes=[r_cs2[0]])
                        S("ve", lambda q, sp2=sp2: q.tensor_reduce(
                            bscol[:, sp2, :],
                            cs2[:].rearrange("p (h n) -> p h n", h=16),
                            AX.X, ALU.add),
                          reads=[r_cs2[0]], writes=[r_bscol[sp2]])
                        S("scd", lambda q, t0=t0, s=s, sp2=sp2: ld(
                            q, o_bs[t0 + s * 128:t0 + (s + 1) * 128, :], bscol[:, sp2, :]),
                            reads=[r_bscol[sp2]])

        with nc.Block() as block:
            sch.emit(nc, block)
    return nc




def build_s5(nc, io):
    o_x1, o_xn1, o_xn1T = io["x1_d"], io["xn1_d"], io["xn1T_bf"]
    with ExitStack() as _es:
        ec = _es.enter_context
        sb = lambda *a: ec(nc.sbuf_tensor(*a))
        cc = sb("s5_cc", [128, 8], F32)
        idb = sb("s5_idb", [128, 128], BF16)
        owt = sb("s5_owt", [128, 8, 1024], BF16)
        gng = sb("s5_gng", [128, 1024], F32)
        gnb = sb("s5_gnb", [128, 1024], F32)
        yt = sb("s5_y", [128, 2, 1024], F32)
        bst = sb("s5_bs", [128, 2, 16], F32)
        vt = sb("s5_v", [128, 2, 1024], BF16)
        gt = sb("s5_g", [128, 2, 1024], BF16)
        xt = sb("s5_x", [128, 2, 1024], F32)
        f0t = sb("s5_f0", [128, 2, 1024], F32)
        sq = sb("s5_sq", [128, 1024], F32)
        gcol = sb("s5_gcol", [128, 2, 96], F32)
        y2 = sb("s5_y2", [128, 1024], F32)
        zc = sb("s5_zc", [128, 2, 1024], BF16)
        zcT = sb("s5_zcT", [128, 2, 8, 128], BF16)
        x1s = sb("s5_x1", [128, 2, 1024], F32)
        ncol = sb("s5_ncol", [128, 2, 8], F32)
        xn1s = sb("s5_xn1", [128, 2, 1024], F32)
        xn1b = sb("s5_xn1b", [128, 1024], BF16)
        xn1Ts = sb("s5_xn1T", [128, 2, 8, 128], BF16)
        ptr = ec(nc.psum_tensor("s5_ptr", [128, 2, 1024], BF16))
        ppj = ec(nc.psum_tensor("s5_ppj", [128, 2, 512], F32))

        sch = Sched()
        S = mk_S(sch)
        r_cst = Res()
        r_y, r_bs, r_v, r_g, r_x, r_f0 = [[Res(), Res()] for _ in range(6)]
        r_sq, r_y2, r_xn1b = Res(), Res(), Res()
        r_gcol = [Res(), Res()]
        r_zc = [Res(), Res()]
        r_zcT = [Res(), Res()]
        r_x1 = [Res(), Res()]
        r_nc = [Res(), Res()]
        r_xn1 = [Res(), Res()]
        r_xn1T = [Res(), Res()]
        r_ptr = [Res(), Res()]
        r_ppj = [Res(), Res()]

        def ld(q, dst, src):
            return q.dma_start(out=dst, in_=src)

        S("gpd", lambda q: ld(q, owt[:], io["s5_oWt"].rearrange("(fc f) o -> f fc o", f=128)),
          writes=[r_cst])
        S("gpd", lambda q: ld(q, idb[:], io["c_ident_bf"][:, :]), writes=[r_cst])
        for nm, dst in [("s5_gng_t", gng), ("s5_gnb_t", gnb), ("c_consts", cc)]:
            S("sy", lambda q, nm=nm, dst=dst: ld(q, dst[:], io[nm][:, :]), writes=[r_cst])

        for i in range(NT):
            p = i % 2
            t0 = i * 128
            rows = slice(t0, t0 + 128)
            S("sy", lambda q, rows=rows, p=p: ld(q, yt[:, p, :], io["y_d"][rows, :]),
              reads=[r_cst], writes=[r_y[p]])
            S("sy", lambda q, rows=rows, p=p: ld(q, bst[:, p, :], io["bsum_d"][rows, :]),
              writes=[r_bs[p]])
            S("sy", lambda q, rows=rows, p=p: ld(q, vt[:, p, :], io["v_d"][rows, :]),
              writes=[r_v[p]])
            S("sy", lambda q, rows=rows, p=p: ld(q, gt[:, p, :], io["g_d"][rows, :]),
              writes=[r_g[p]])
            S("sy", lambda q, rows=rows, p=p: ld(q, xt[:, p, :], io["x"][rows, :]),
              writes=[r_x[p]])
            S("sy", lambda q, rows=rows, p=p: ld(q, f0t[:, p, :], io["f0"][rows, :]),
              writes=[r_f0[p]])
            S("ve", lambda q, p=p: q.tensor_reduce(
                gcol[:, p, 0:16], yt[:, p, :].rearrange("a (h n) -> a h n", h=16),
                AX.X, ALU.add), reads=[r_y[p]], writes=[r_gcol[p]])
            S("sc", lambda q, p=p: q.activation(sq[:], yt[:, p, :], AF.Square),
              reads=[r_y[p]], writes=[r_sq])
            S("ve", lambda q, p=p: q.tensor_reduce(
                gcol[:, p, 16:32], sq[:].rearrange("a (h n) -> a h n", h=16),
                AX.X, ALU.add), reads=[r_sq], writes=[r_gcol[p]])
            S("ve", lambda q, p=p: q.tensor_scalar(
                gcol[:, p, 32:48], gcol[:, p, 0:16], 1.0 / 64, None, ALU.mult),
              reads=[r_gcol[p]], writes=[r_gcol[p]])
            S("ve", lambda q, p=p: q.tensor_tensor(
                gcol[:, p, 48:64], gcol[:, p, 0:16], gcol[:, p, 32:48], ALU.mult),
              reads=[r_gcol[p]], writes=[r_gcol[p]])
            S("ve", lambda q, p=p: q.tensor_tensor(
                gcol[:, p, 64:80], gcol[:, p, 16:32], gcol[:, p, 48:64], ALU.subtract),
              reads=[r_gcol[p]], writes=[r_gcol[p]])
            S("ve", lambda q, p=p: q.tensor_scalar(
                gcol[:, p, 80:96], gcol[:, p, 64:80], 1.0 / 64, None, ALU.mult),
              reads=[r_gcol[p]], writes=[r_gcol[p]])
            S("sc", lambda q, p=p: q.activation(
                gcol[:, p, 48:64], gcol[:, p, 80:96], AF.Ln, bias=cc[:, 1:2]),
              reads=[r_gcol[p], r_cst], writes=[r_gcol[p]])
            S("sc", lambda q, p=p: q.activation(
                gcol[:, p, 64:80], gcol[:, p, 48:64], AF.Exp, scale=-0.5),
              reads=[r_gcol[p]], writes=[r_gcol[p]])

            def gnorm(q, p=p):
                ii = None
                for h in range(16):
                    sg = slice(h * 64, (h + 1) * 64)
                    ii = q.tensor_scalar(y2[:, sg], yt[:, p, sg],
                                         gcol[:, p, 32 + h:33 + h],
                                         gcol[:, p, 64 + h:65 + h],
                                         ALU.subtract, ALU.mult)
                return ii
            S("ve", gnorm, reads=[r_y[p], r_gcol[p]], writes=[r_y2])
            S("ve", lambda q: q.tensor_tensor(y2[:], y2[:], gng[:], ALU.mult),
              reads=[r_y2, r_cst], writes=[r_y2])
            S("ve", lambda q: q.tensor_tensor(y2[:], y2[:], gnb[:], ALU.add),
              reads=[r_y2], writes=[r_y2])

            def bonus(q, p=p):
                ii = None
                for h in range(16):
                    sg = slice(h * 64, (h + 1) * 64)
                    ii = q.scalar_tensor_tensor(y2[:, sg], vt[:, p, sg],
                                                bst[:, p, h:h + 1], y2[:, sg],
                                                ALU.mult, ALU.add)
                return ii
            S("ve", bonus, reads=[r_v[p], r_bs[p], r_y2], writes=[r_y2])
            S("ve", lambda q, p=p: q.tensor_tensor(zc[:, p, :], y2[:], gt[:, p, :],
                                                   ALU.mult),
              reads=[r_y2, r_g[p]], writes=[r_zc[p]])

            def tr_zc(q, p=p):
                ii = None
                for j in range(8):
                    ii = q.transpose(ptr[:, p, j * 128:(j + 1) * 128],
                                     zc[:, p, j * 128:(j + 1) * 128], idb[:])
                return ii
            S("te", tr_zc, reads=[r_zc[p], r_cst], writes=[r_ptr[p]], no_drain=True)
            S("ve", lambda q, p=p: q.tensor_copy(
                zcT[:, p, :, :].rearrange("a b c -> a (b c)"), ptr[:, p, :]),
              reads=[r_ptr[p]], writes=[r_zcT[p]])

            def ow_mm(q, p=p):
                ii = None
                for oc in range(2):
                    for fc in range(8):
                        ii = q.matmul(ppj[:, oc, :], zcT[:, p, fc, :],
                                      owt[:, fc, oc * 512:(oc + 1) * 512],
                                      start=(fc == 0), stop=(fc == 7))
                return ii
            S("te", ow_mm, reads=[r_zcT[p], r_cst], writes=[r_ppj[0], r_ppj[1]])

            def x1c(q, p=p):
                for oc in range(2):
                    cols = slice(oc * 512, (oc + 1) * 512)
                    q.tensor_tensor(x1s[:, p, cols], ppj[:, oc, :], xt[:, p, cols],
                                    ALU.add)
                ii = None
                for oc in range(2):
                    cols = slice(oc * 512, (oc + 1) * 512)
                    ii = q.tensor_tensor(x1s[:, p, cols], x1s[:, p, cols],
                                         f0t[:, p, cols], ALU.add)
                return ii
            S("ve", x1c, reads=[r_ppj[0], r_ppj[1], r_x[p], r_f0[p]], writes=[r_x1[p]])
            S("scd", lambda q, rows=rows, p=p: ld(q, o_x1[rows, :], x1s[:, p, :]),
              reads=[r_x1[p]])
            S("ve", lambda q, p=p: q.reduce_sum(ncol[:, p, 0:1], x1s[:, p, :], AX.X),
              reads=[r_x1[p]], writes=[r_nc[p]])
            S("sc", lambda q, p=p: q.activation(sq[:], x1s[:, p, :], AF.Square,
                                                accum_out=ncol[:, p, 1:2]),
              reads=[r_x1[p]], writes=[r_sq, r_nc[p]])
            S("ve", lambda q, p=p: q.tensor_scalar(ncol[:, p, 2:3], ncol[:, p, 0:1],
                                                   1.0 / C, None, ALU.mult),
              reads=[r_nc[p]], writes=[r_nc[p]])
            S("ve", lambda q, p=p: q.tensor_tensor(ncol[:, p, 3:4], ncol[:, p, 0:1],
                                                   ncol[:, p, 2:3], ALU.mult),
              reads=[r_nc[p]], writes=[r_nc[p]])
            S("ve", lambda q, p=p: q.tensor_tensor(ncol[:, p, 4:5], ncol[:, p, 1:2],
                                                   ncol[:, p, 3:4], ALU.subtract),
              reads=[r_nc[p]], writes=[r_nc[p]])
            S("ve", lambda q, p=p: q.tensor_scalar(ncol[:, p, 5:6], ncol[:, p, 4:5],
                                                   1.0 / C, None, ALU.mult),
              reads=[r_nc[p]], writes=[r_nc[p]])
            S("sc", lambda q, p=p: q.activation(ncol[:, p, 6:7], ncol[:, p, 5:6],
                                                AF.Ln, bias=cc[:, 0:1]),
              reads=[r_nc[p], r_cst], writes=[r_nc[p]])
            S("sc", lambda q, p=p: q.activation(ncol[:, p, 7:8], ncol[:, p, 6:7],
                                                AF.Exp, scale=-0.5),
              reads=[r_nc[p]], writes=[r_nc[p]])
            S("ve", lambda q, p=p: q.tensor_scalar(xn1s[:, p, :], x1s[:, p, :],
                                                   ncol[:, p, 2:3], ncol[:, p, 7:8],
                                                   ALU.subtract, ALU.mult),
              reads=[r_x1[p], r_nc[p]], writes=[r_xn1[p]])
            S("scd", lambda q, rows=rows, p=p: ld(q, o_xn1[rows, :], xn1s[:, p, :]),
              reads=[r_xn1[p]])
            S("ve", lambda q, p=p: q.tensor_copy(xn1b[:], xn1s[:, p, :]),
              reads=[r_xn1[p]], writes=[r_xn1b])

            def tr_xn1(q, p=p):
                ii = None
                for j in range(8):
                    ii = q.transpose(ptr[:, p, j * 128:(j + 1) * 128],
                                     xn1b[:, j * 128:(j + 1) * 128], idb[:])
                return ii
            S("te", tr_xn1, reads=[r_xn1b, r_cst], writes=[r_ptr[p]], no_drain=True)
            S("ve", lambda q, p=p: q.tensor_copy(
                xn1Ts[:, p, :, :].rearrange("a b c -> a (b c)"), ptr[:, p, :]),
              reads=[r_ptr[p]], writes=[r_xn1T[p]])
            S("gpd", lambda q, t0=t0, p=p: ld(
                q, o_xn1T.rearrange("(fc f) t -> f fc t", f=128)[:, :, t0:t0 + 128],
                xn1Ts[:, p, :, :]), reads=[r_xn1T[p]])

        with nc.Block() as block:
            sch.emit(nc, block)
    return nc




def build_s6(nc, io):
    xnTb = io["xn1T_bf"]
    o_f1 = io["f1_d"]
    NE, NB = 14, 16
    with ExitStack() as _es:
        ec = _es.enter_context
        sb = lambda *a: ec(nc.sbuf_tensor(*a))
        idb = sb("s6_idb", [128, 128], BF16)
        rw = sb("s6s_rw", [128, 8, 14], BF16)
        gs = sb("s6_gs", [128, NT, 48], F32)
        Mall = sb("s6_mall", [128, NT, 16], F32)
        Mb = sb("s6_mb", [128, 16], BF16)
        MT = sb("s6_mt", [14, T], BF16)
        xg = sb("s6_xg", [128, 2, 8, 128], BF16)
        xT = sb("s6_xT", [128, 2, 8, 512], BF16)
        w1s = sb("s6_w1s", [128, 8, 16, 128], BF16)
        w2s = sb("s6_w2s", [128, 16, 1024], BF16)
        b1c = sb("s6s_b1c", [128, 224], F32)
        b2s = sb("s6_b2s", [14, 1024], BF16)
        hT = sb("s6_hT", [128, 16, 512], BF16)
        ft = sb("s6_ft", [128, 2, 4, 1024], F32)
        pg = ec(nc.psum_tensor("s6_pg", [128, 2, 512], F32))
        pmt = ec(nc.psum_tensor("s6_pmt", [128, 1024], BF16))
        ph = ec(nc.psum_tensor("s6_ph", [128, 2, 512], F32))
        py = ec(nc.psum_tensor("s6_py", [128, 2, 512], F32))

        sch = Sched()
        S = mk_S(sch)
        r_cst, r_w1, r_w2 = Res(), Res(), Res()
        r_xg = [Res(), Res()]
        r_xT = [Res(), Res()]
        r_pg = [Res(), Res()]
        r_ph = [Res(), Res()]
        r_py = [Res(), Res()]
        r_pmt, r_mb, r_MT, r_gs, r_mall, r_hT = Res(), Res(), Res(), Res(), Res(), Res()
        r_ft = [Res(), Res()]
        r_f1b = [Res() for _ in range(NB)]

        def ld(q, dst, src):
            return q.dma_start(out=dst, in_=src)

        S("gpd", lambda q: ld(q, rw[:], io["s6_rw"].rearrange("(fc f) e -> f fc e", f=128)),
          writes=[r_cst])
        S("gpd", lambda q: ld(q, idb[:], io["c_ident_bf"][:, :]), writes=[r_cst])
        S("gpd", lambda q: ld(q, b2s[:], io["s6_b2"][:, :]), writes=[r_cst])
        S("sy", lambda q: ld(q, b1c[:], io["s6_b1c"][:, :]), writes=[r_cst])

        xr = xnTb.rearrange("(fc f) t -> f fc t", f=128)

        for i in range(NT):
            p = i % 2
            S("sy", lambda q, i=i, p=p: ld(q, xg[:, p, :, :],
                                           xr[:, :, i * 128:(i + 1) * 128]),
              reads=[r_cst], writes=[r_xg[p]])

            def g_mm(q, p=p):
                ii = None
                for fc in range(8):
                    ii = q.matmul(pg[:, p, 0:14], xg[:, p, fc, :], rw[:, fc, :],
                                  start=(fc == 0), stop=(fc == 7))
                return ii
            S("te", g_mm, reads=[r_xg[p], r_cst], writes=[r_pg[p]])
            S("ve", lambda q, i=i, p=p: q.reduce_max(gs[:, i, 0:1], pg[:, p, 0:14], AX.X),
              reads=[r_pg[p]], writes=[r_gs])
            S("ve", lambda q, i=i, p=p: q.tensor_scalar(
                gs[:, i, 16:30], pg[:, p, 0:14], gs[:, i, 0:1], None, ALU.is_ge),
              reads=[r_pg[p], r_gs], writes=[r_gs])
            S("ve", lambda q, i=i, p=p: q.tensor_scalar(
                gs[:, i, 32:46], pg[:, p, 0:14], gs[:, i, 0:1], None, ALU.subtract),
              reads=[r_pg[p], r_gs], writes=[r_gs])
            S("sc", lambda q, i=i: q.activation(gs[:, i, 2:16], gs[:, i, 32:46], AF.Exp),
              reads=[r_gs], writes=[r_gs])
            S("ve", lambda q, i=i: q.reduce_sum(gs[:, i, 1:2], gs[:, i, 2:16], AX.X),
              reads=[r_gs], writes=[r_gs])
            S("ve", lambda q, i=i: q.reciprocal(gs[:, i, 46:47], gs[:, i, 1:2]),
              reads=[r_gs], writes=[r_gs])
            S("ve", lambda q, i=i: q.tensor_scalar(
                Mall[:, i, 0:14], gs[:, i, 16:30], gs[:, i, 46:47], None, ALU.mult),
              reads=[r_gs], writes=[r_mall])
            S("ve", lambda q, i=i: q.tensor_copy(Mb[:, 0:14], Mall[:, i, 0:14]),
              reads=[r_mall], writes=[r_mb])
            S("te", lambda q: q.transpose(pmt[0:14, 0:128], Mb[:, 0:14], idb[:]),
              reads=[r_mb, r_cst], writes=[r_pmt], no_drain=True)
            S("ve", lambda q, i=i: q.tensor_copy(MT[0:14, i * 128:(i + 1) * 128],
                                                 pmt[0:14, 0:128]),
              reads=[r_pmt], writes=[r_MT])

        for e in range(NE):
            S("gpd", lambda q, e=e: ld(
                q, w1s[:], io["s6_w1"][e].rearrange("(fc f) (ic i) -> f fc ic i",
                                                    f=128, i=128)), writes=[r_w1])
            S("gpd", lambda q, e=e: ld(
                q, w2s[:], io["s6_w2"][e].rearrange("(ic i) o -> i ic o", i=128)),
              writes=[r_w2])
            for b in range(NB):
                p = (e * NB + b) % 2
                S("sy", lambda q, b=b, p=p: ld(q, xT[:, p, :, :],
                                               xr[:, :, b * 512:(b + 1) * 512]),
                  writes=[r_xT[p]])
                for ic in range(16):
                    hp = ic % 2

                    def h_mm(q, ic=ic, p=p, hp=hp):
                        ii = None
                        for fc in range(8):
                            ii = q.matmul(ph[:, hp, :], w1s[:, fc, ic, :],
                                          xT[:, p, fc, :],
                                          start=(fc == 0), stop=(fc == 7))
                        return ii
                    S("te", h_mm, reads=[r_xT[p], r_w1], writes=[r_ph[hp]])
                    S("sc", lambda q, e=e, ic=ic, hp=hp: q.activation(
                        hT[:, ic, :], ph[:, hp, :], AF.Gelu,
                        bias=b1c[:, e * 16 + ic:e * 16 + ic + 1]),
                      reads=[r_ph[hp], r_cst], writes=[r_hT])
                for j in range(8):
                    s, oc = divmod(j, 2)
                    yp = j % 2

                    def y_mm(q, s=s, oc=oc, yp=yp):
                        ii = None
                        for ic in range(16):
                            ii = q.matmul(py[:, yp, :], hT[:, ic, s * 128:(s + 1) * 128],
                                          w2s[:, ic, oc * 512:(oc + 1) * 512],
                                          start=(ic == 0), stop=(ic == 15))
                        return ii
                    S("te", y_mm, reads=[r_hT, r_w2], writes=[r_py[yp]])
                    S("ve", lambda q, e=e, b=b, s=s, oc=oc, yp=yp: q.tensor_scalar(
                        ft[:, b % 2, s, oc * 512:(oc + 1) * 512], py[:, yp, :],
                        Mall[:, 4 * b + s, e:e + 1], None, ALU.mult),
                      reads=[r_py[yp], r_mall], writes=[r_ft[b % 2]])
                dst = o_f1.rearrange("(bb q) c -> q bb c", q=128)
                if e == 0:
                    S("gpd", lambda q, b=b, dst=dst: ld(
                        q, dst[:, 4 * b:4 * b + 4, :], ft[:, b % 2, :, :]),
                      reads=[r_ft[b % 2]], writes=[r_f1b[b]])
                else:
                    S("gpd", lambda q, b=b, dst=dst: q.dma_start(
                        out=dst[:, 4 * b:4 * b + 4, :], in_=ft[:, b % 2, :, :],
                        accum_op=ALU.add),
                      reads=[r_ft[b % 2]], writes=[r_f1b[b]])
        for b in range(NB):
            for j in range(8):
                s, oc = divmod(j, 2)
                yp = j % 2

                def b2_mm(q, b=b, s=s, oc=oc, yp=yp):
                    return q.matmul(py[:, yp, :], MT[0:14, (4 * b + s) * 128:(4 * b + s + 1) * 128],
                                    b2s[:, oc * 512:(oc + 1) * 512],
                                    start=True, stop=True)
                S("te", b2_mm, reads=[r_MT, r_cst], writes=[r_py[yp]])
                S("ve", lambda q, b=b, s=s, oc=oc, yp=yp: q.tensor_copy(
                    ft[:, b % 2, s, oc * 512:(oc + 1) * 512], py[:, yp, :]),
                  reads=[r_py[yp]], writes=[r_ft[b % 2]])
            dst = o_f1.rearrange("(bb q) c -> q bb c", q=128)
            S("gpd", lambda q, b=b, dst=dst: q.dma_start(
                out=dst[:, 4 * b:4 * b + 4, :], in_=ft[:, b % 2, :, :],
                accum_op=ALU.add),
              reads=[r_ft[b % 2]], writes=[r_f1b[b]])

        with nc.Block() as block:
            sch.emit(nc, block)
    return nc



def build_s7a(nc, io):
    o_qfT, o_kvz = io["qfT_d"], io["kvz_d"]
    with ExitStack() as _es:
        ec = _es.enter_context
        sb = lambda *a: ec(nc.sbuf_tensor(*a))
        cc = sb("s7a_cc", [128, 8], F32)
        idb = sb("s7a_idb", [128, 128], BF16)
        ones = sb("s7a_ones", [128, 128], BF16)
        qw = sb("s7a_qw", [128, 8, 1024], BF16)
        kvw = sb("s7a_kvw", [128, 8, 2048], BF16)
        qbc = sb("s7a_qbc", [128, 8], F32)
        kvb = sb("s7a_kvb", [128, 2048], F32)
        g1 = sb("s7a_g1", [128, 1024], F32)
        be1 = sb("s7a_be1", [128, 1024], F32)
        xt = sb("s7a_x", [128, 2, 1024], F32)
        ncol = sb("s7a_ncol", [128, 2, 8], F32)
        sq = sb("s7a_sq", [128, 1024], F32)
        ln1 = sb("s7a_ln1", [128, 1024], F32)
        ln1b = sb("s7a_ln1b", [128, 1024], BF16)
        lnT = sb("s7a_lnT", [128, 2, 8, 128], BF16)
        qcs = sb("s7a_qcs", [128, 2, 512], F32)
        qfT = sb("s7a_qfT", [128, 8, 128], BF16)
        kvcs = sb("s7a_kvcs", [128, 2048], F32)
        kfb = sb("s7a_kfb", [128, 1024], BF16)
        vfb = sb("s7a_vfb", [128, 1024], BF16)
        kvzs = sb("s7a_kvz", [128, 8, 66], BF16)
        zrow = sb("s7a_zrow", [1, 1024], F32)
        zrowb = sb("s7a_zrowb", [1, 1024], BF16)
        zcs = sb("s7a_zc", [128, 8], F32)
        ptr = ec(nc.psum_tensor("s7a_ptr", [128, 2, 1024], BF16))
        pq = ec(nc.psum_tensor("s7a_pq", [128, 2, 512], F32))
        pkv = ec(nc.psum_tensor("s7a_pkv", [128, 2, 512], F32))
        pacc = ec(nc.psum_tensor("s7a_pacc", [128, 512], F32))
        pz = ec(nc.psum_tensor("s7a_pz", [128, 512], F32))

        sch = Sched()
        S = mk_S(sch)
        r_cst = Res()
        r_x = [Res(), Res()]
        r_nc = [Res(), Res()]
        r_sq, r_ln1, r_ln1b = Res(), Res(), Res()
        r_lnT = [Res(), Res()]
        r_ptr = [Res(), Res()]
        r_pq = [Res(), Res()]
        r_pkv = [Res(), Res()]
        r_qcs = [Res(), Res()]
        r_qfT, r_kvcs, r_kfb, r_vfb = Res(), Res(), Res(), Res()
        r_pacc, r_pz, r_kvz, r_zrow, r_zc = Res(), Res(), Res(), Res(), Res()

        def ld(q, dst, src):
            return q.dma_start(out=dst, in_=src)

        S("gpd", lambda q: ld(q, qw[:], io["s7_qw"].rearrange("(fc f) o -> f fc o", f=128)),
          writes=[r_cst])
        S("gpd", lambda q: ld(q, kvw[:], io["s7_kvw"].rearrange("(fc f) o -> f fc o", f=128)),
          writes=[r_cst])
        S("gpd", lambda q: ld(q, idb[:], io["c_ident_bf"][:, :]), writes=[r_cst])
        S("gpd", lambda q: ld(q, ones[:], io["c_ones_bf"][:, :]), writes=[r_cst])
        for nm, dst in [("s7_qbc", qbc), ("s7_kvb_t", kvb), ("s7_g1_t", g1),
                        ("s7_be1_t", be1), ("c_consts", cc)]:
            S("sy", lambda q, nm=nm, dst=dst: ld(q, dst[:], io[nm][:, :]), writes=[r_cst])

        for i in range(NT):
            p = i % 2
            rows = slice(i * 128, (i + 1) * 128)
            S("sy", lambda q, rows=rows, p=p: ld(q, xt[:, p, :], io["xn1_d"][rows, :]),
              reads=[r_cst], writes=[r_x[p]])
            S("ve", lambda q, p=p: q.reduce_sum(ncol[:, p, 0:1], xt[:, p, :], AX.X),
              reads=[r_x[p]], writes=[r_nc[p]])
            S("sc", lambda q, p=p: q.activation(sq[:], xt[:, p, :], AF.Square,
                                                accum_out=ncol[:, p, 1:2]),
              reads=[r_x[p]], writes=[r_sq, r_nc[p]])
            S("ve", lambda q, p=p: q.tensor_scalar(ncol[:, p, 2:3], ncol[:, p, 0:1],
                                                   1.0 / C, None, ALU.mult),
              reads=[r_nc[p]], writes=[r_nc[p]])
            S("ve", lambda q, p=p: q.tensor_tensor(ncol[:, p, 3:4], ncol[:, p, 0:1],
                                                   ncol[:, p, 2:3], ALU.mult),
              reads=[r_nc[p]], writes=[r_nc[p]])
            S("ve", lambda q, p=p: q.tensor_tensor(ncol[:, p, 4:5], ncol[:, p, 1:2],
                                                   ncol[:, p, 3:4], ALU.subtract),
              reads=[r_nc[p]], writes=[r_nc[p]])
            S("ve", lambda q, p=p: q.tensor_scalar(ncol[:, p, 5:6], ncol[:, p, 4:5],
                                                   1.0 / C, None, ALU.mult),
              reads=[r_nc[p]], writes=[r_nc[p]])
            S("sc", lambda q, p=p: q.activation(ncol[:, p, 6:7], ncol[:, p, 5:6],
                                                AF.Ln, bias=cc[:, 0:1]),
              reads=[r_nc[p], r_cst], writes=[r_nc[p]])
            S("sc", lambda q, p=p: q.activation(ncol[:, p, 7:8], ncol[:, p, 6:7],
                                                AF.Exp, scale=-0.5),
              reads=[r_nc[p]], writes=[r_nc[p]])
            S("ve", lambda q, p=p: q.tensor_scalar(ln1[:], xt[:, p, :],
                                                   ncol[:, p, 2:3], ncol[:, p, 7:8],
                                                   ALU.subtract, ALU.mult),
              reads=[r_x[p], r_nc[p]], writes=[r_ln1])
            S("ve", lambda q: q.tensor_tensor(ln1[:], ln1[:], g1[:], ALU.mult),
              reads=[r_ln1, r_cst], writes=[r_ln1])
            S("ve", lambda q: q.tensor_tensor(ln1b[:], ln1[:], be1[:], ALU.add),
              reads=[r_ln1], writes=[r_ln1b])

            def tr_ln(q, p=p):
                ii = None
                for j in range(8):
                    ii = q.transpose(ptr[:, p, j * 128:(j + 1) * 128],
                                     ln1b[:, j * 128:(j + 1) * 128], idb[:])
                return ii
            S("te", tr_ln, reads=[r_ln1b, r_cst], writes=[r_ptr[p]], no_drain=True)
            S("ve", lambda q, p=p: q.tensor_copy(
                lnT[:, p, :, :].rearrange("a b c -> a (b c)"), ptr[:, p, :]),
              reads=[r_ptr[p]], writes=[r_lnT[p]])

            for hb in range(2):
                def q_mm(q, p=p, hb=hb):
                    ii = None
                    for oj in range(4):
                        oc = hb * 4 + oj
                        for fc in range(8):
                            ii = q.matmul(pq[:, hb, oj * 128:(oj + 1) * 128],
                                          qw[:, fc, oc * 128:(oc + 1) * 128],
                                          lnT[:, p, fc, :],
                                          start=(fc == 0), stop=(fc == 7))
                    return ii
                S("te", q_mm, reads=[r_lnT[p], r_cst], writes=[r_pq[hb]])

                def q_elu(q, hb=hb):
                    for oj in range(4):
                        oc = hb * 4 + oj
                        cl = slice(oj * 128, (oj + 1) * 128)
                        q.tensor_scalar(qcs[:, hb, cl], pq[:, hb, cl],
                                        qbc[:, oc:oc + 1], None, ALU.add)
                    return q.tensor_scalar(qcs[:, hb, 0:1], qcs[:, hb, 0:1], 0.0, None,
                                           ALU.add)
                S("ve", q_elu, reads=[r_pq[hb], r_cst], writes=[r_qcs[hb]])
                S("ve", lambda q, hb=hb: q.tensor_scalar(
                    pq[:, hb, :], qcs[:, hb, :], 0.0, None, ALU.min),
                  reads=[r_qcs[hb]], writes=[r_pq[hb]])
                S("sc", lambda q, hb=hb: q.activation(sq[:, hb * 512:(hb + 1) * 512],
                                                      pq[:, hb, :], AF.Exp),
                  reads=[r_pq[hb]], writes=[r_sq])
                S("ve", lambda q, hb=hb: q.tensor_relu(qcs[:, hb, :], qcs[:, hb, :]),
                  reads=[r_qcs[hb]], writes=[r_qcs[hb]])
                S("ve", lambda q, hb=hb: q.tensor_tensor(
                    qfT[:, :, :].rearrange("a b c -> a (b c)")[:, hb * 512:(hb + 1) * 512],
                    qcs[:, hb, :], sq[:, hb * 512:(hb + 1) * 512], ALU.add),
                  reads=[r_qcs[hb], r_sq], writes=[r_qfT])
            S("gpd", lambda q, i=i: ld(
                q, o_qfT.rearrange("(fc f) t -> f fc t", f=128)[:, :, i * 128:(i + 1) * 128],
                qfT[:, :, :]), reads=[r_qfT])

            for ob in range(4):
                def kv_mm(q, p=p, ob=ob):
                    ii = None
                    for fc in range(8):
                        ii = q.matmul(pkv[:, ob % 2, :], lnT[:, p, fc, :],
                                      kvw[:, fc, ob * 512:(ob + 1) * 512],
                                      start=(fc == 0), stop=(fc == 7))
                    return ii
                S("te", kv_mm, reads=[r_lnT[p], r_cst], writes=[r_pkv[ob % 2]])
                S("ve", lambda q, ob=ob: q.tensor_tensor(
                    kvcs[:, ob * 512:(ob + 1) * 512], pkv[:, ob % 2, :],
                    kvb[:, ob * 512:(ob + 1) * 512], ALU.add),
                  reads=[r_pkv[ob % 2], r_cst], writes=[r_kvcs])
            S("ve", lambda q: q.tensor_scalar(sq[:], kvcs[:, 0:1024], 0.0, None, ALU.min),
              reads=[r_kvcs], writes=[r_sq])
            S("sc", lambda q: q.activation(sq[:], sq[:], AF.Exp),
              reads=[r_sq], writes=[r_sq])
            S("ve", lambda q: q.tensor_relu(kvcs[:, 0:1024], kvcs[:, 0:1024]),
              reads=[r_kvcs], writes=[r_kvcs])
            S("ve", lambda q: q.tensor_tensor(kfb[:], kvcs[:, 0:1024], sq[:], ALU.add),
              reads=[r_kvcs, r_sq], writes=[r_kfb])
            S("ve", lambda q: q.tensor_copy(vfb[:], kvcs[:, 1024:2048]),
              reads=[r_kvcs], writes=[r_vfb])

            def kvacc(q, i=i):
                ii = None
                for h in range(16):
                    rr = slice((h % 2) * 64, (h % 2) * 64 + 64)
                    ii = q.matmul(pacc[rr, (h // 2) * 64:(h // 2) * 64 + 64],
                                  kfb[:, h * 64:(h + 1) * 64],
                                  vfb[:, h * 64:(h + 1) * 64],
                                  start=(i == 0), stop=(i == NT - 1))
                return ii
            S("te", kvacc, reads=[r_kfb, r_vfb], writes=[r_pacc], no_drain=True)

            def zacc(q, i=i):
                q.matmul(pz[0:1, 0:512], ones[:, 0:1], kfb[:, 0:512],
                         start=(i == 0), stop=(i == NT - 1))
                return q.matmul(pz[32:33, 0:512], ones[:, 0:1], kfb[:, 512:1024],
                                start=(i == 0), stop=(i == NT - 1))
            S("te", zacc, reads=[r_kfb, r_cst], writes=[r_pz], no_drain=True)

        def kv_ev(q):
            ii = None
            for h2 in range(8):
                ii = q.tensor_copy(kvzs[:, h2, 0:64], pacc[:, h2 * 64:(h2 + 1) * 64])
            return ii
        S("ve", kv_ev, reads=[r_pacc], writes=[r_kvz])
        S("ve", lambda q: q.tensor_copy(zrow[0:1, 0:512], pz[0:1, 0:512]),
          reads=[r_pz], writes=[r_zrow])
        S("ve", lambda q: q.tensor_copy(zrow[0:1, 512:1024], pz[32:33, 0:512]),
          reads=[r_pz], writes=[r_zrow])

        S("ve", lambda q: q.tensor_copy(zrowb[:], zrow[:]), reads=[r_zrow],
          writes=[r_zrow])

        def z_tr(q):
            ii = None
            for j in range(8):
                ii = q.transpose(ptr[:, 0, j * 128:(j + 1) * 128][:, 0:1],
                                 zrowb[0:1, j * 128:(j + 1) * 128], idb[0:1, 0:1])
            return ii
        S("te", z_tr, reads=[r_zrow, r_cst], writes=[r_ptr[0]], no_drain=True)

        def z_ev(q):
            ii = None
            for j in range(8):
                ii = q.tensor_copy(zcs[:, j:j + 1], ptr[:, 0, j * 128:j * 128 + 1])
            return ii
        S("ve", z_ev, reads=[r_ptr[0]], writes=[r_zc])

        def z_fill(q):
            ii = None
            for h2 in range(8):
                ii = q.tensor_copy(kvzs[:, h2, 64:65], zcs[:, h2:h2 + 1])
            return ii
        S("ve", z_fill, reads=[r_zc], writes=[r_kvz])
        S("ve", lambda q: q.memset(kvzs[:, :, 65:66], 0.0), writes=[r_kvz])
        S("gpd", lambda q: ld(q, o_kvz[:, :],
                              kvzs[:, :, :].rearrange("a b c -> a (b c)")),
          reads=[r_kvz])

        with nc.Block() as block:
            sch.emit(nc, block)
    return nc



def build_s7b(nc, io):
    o_out = io["out_d"]
    with ExitStack() as _es:
        ec = _es.enter_context
        sb = lambda *a: ec(nc.sbuf_tensor(*a))
        cc = sb("s7b_cc", [128, 8], F32)
        idb = sb("s7b_idb", [128, 128], BF16)
        ones = sb("s7b_ones", [128, 128], BF16)
        ow = sb("s7b_ow", [128, 8, 1024], BF16)
        obt = sb("s7b_obt", [128, 1024], BF16)
        w1 = sb("s7b_w1", [128, 8, 4096], BF16)
        b1c = sb("s7b_b1c", [128, 32], F32)
        w2 = sb("s7b_w2", [128, 32, 1024], BF16)
        b2t = sb("s7b_b2t", [128, 1024], BF16)
        g2 = sb("s7b_g2", [128, 1024], BF16)
        be2 = sb("s7b_be2", [128, 1024], BF16)
        kvzs = sb("s7b_kvz", [128, 8, 66], BF16)
        qfb = sb("s7b_qfb", [128, 8, 512], BF16)
        den = sb("s7b_den", [1, 1024], F32)
        bcs = sb("s7b_bc", [64, 512], BF16)
        ysb = sb("s7b_y", [128, 8, 512], BF16)
        xn1t = sb("s7b_xn1", [128, 1024], F32)
        x1t = sb("s7b_x1", [128, 1024], F32)
        xxs = sb("s7b_xx", [128, 1, 1024], F32)
        ncol = sb("s7b_ncol", [128, 2, 8], F32)
        sq = sb("s7b_sq", [128, 1024], F32)
        ln2b = sb("s7b_ln2b", [128, 1024], BF16)
        lnT = sb("s7b_lnT", [128, 8, 128], BF16)
        hms = sb("s7b_hm", [128, 32, 128], BF16)
        outs = sb("s7b_out", [128, 1024], F32)
        pA = ec(nc.psum_tensor("s7b_pA", [128, 2, 512], F32))
        pB = ec(nc.psum_tensor("s7b_pB", [128, 2, 512], F32))
        pC = ec(nc.psum_tensor("s7b_pC", [128, 2, 512], F32))
        ptr = ec(nc.psum_tensor("s7b_ptr", [128, 1024], BF16))
        pD = ec(nc.psum_tensor("s7b_pD", [128, 512], F32))

        sch = Sched()
        S = mk_S(sch)
        r_cst = Res()
        r_qfb, r_kvz = Res(), Res()
        r_pA = [Res(), Res()]
        r_pB = [Res(), Res()]
        r_pC = [Res(), Res()]
        r_pD, r_ptr = Res(), Res()
        r_den = [Res(), Res()]
        r_rdb = [Res(), Res()]
        r_bc = [Res(), Res()]
        r_y, r_xn1, r_x1 = Res(), Res(), Res()
        r_xx = [Res(), Res()]
        r_nc = [Res(), Res()]
        r_sq, r_ln2b, r_lnT, r_hm, r_out = Res(), Res(), Res(), Res(), Res()

        def ld(q, dst, src):
            return q.dma_start(out=dst, in_=src)

        S("gpd", lambda q: ld(q, ow[:], io["s7_ow"].rearrange("(fc f) o -> f fc o", f=128)),
          writes=[r_cst])
        S("gpd", lambda q: ld(q, w1[:], io["s7_w1"].rearrange("(fc f) o -> f fc o", f=128)),
          writes=[r_cst])
        S("gpd", lambda q: ld(q, w2[:], io["s7_w2"].rearrange("(ic i) o -> i ic o", i=128)),
          writes=[r_cst])
        S("gpd", lambda q: ld(q, idb[:], io["c_ident_bf"][:, :]), writes=[r_cst])
        S("gpd", lambda q: ld(q, ones[:], io["c_ones_bf"][:, :]), writes=[r_cst])
        S("gpd", lambda q: ld(q, kvzs[:].rearrange("a b c -> a (b c)"), io["kvz_d"][:, :]),
          writes=[r_kvz])
        for nm, dst in [("s7_ob_t", obt), ("s7_b2_t", b2t), ("s7_g2_t", g2),
                        ("s7_be2_t", be2)]:
            S("gpd", lambda q, nm=nm, dst=dst: ld(q, dst[:], io[nm][:, :]), writes=[r_cst])
        for nm, dst in [("s7_b1c", b1c), ("c_consts", cc)]:
            S("sy", lambda q, nm=nm, dst=dst: ld(q, dst[:], io[nm][:, :]), writes=[r_cst])

        qfr = io["qfT_d"].rearrange("(fc f) t -> f fc t", f=128)

        for bb in range(16):
            tb = bb * 512
            S("sy", lambda q, tb=tb: ld(q, qfb[:, :, :], qfr[:, :, tb:tb + 512]),
              reads=[r_cst], writes=[r_qfb])
            for h in range(16):
                rr = slice((h % 2) * 64, (h % 2) * 64 + 64)
                hp = h % 2

                def aug_mm(q, h=h, rr=rr, hp=hp):
                    return q.matmul(pA[0:65, hp, :], kvzs[rr, h // 2, 0:65],
                                    qfb[rr, h // 2, :], start=True, stop=True)
                S("te", aug_mm, reads=[r_kvz, r_qfb], writes=[r_pA[hp]])
                S("ve", lambda q, hp=hp: q.tensor_scalar(
                    den[0:1, 0:512], pA[64:65, hp, :], 1e-6, None, ALU.max),
                  reads=[r_pA[hp]], writes=[r_den[0]])
                S("ve", lambda q: q.reciprocal(den[0:1, 512:1024], den[0:1, 0:512]),
                  reads=[r_den[0]], writes=[r_den[0]])
                S("ve", lambda q: q.tensor_copy(bcs[0:1, :], den[0:1, 512:1024]),
                  reads=[r_den[0]], writes=[r_rdb[0]])

                def bc_mm(q, hp=hp):
                    return q.matmul(pB[0:64, hp, :], ones[0:1, 0:64], bcs[0:1, :],
                                    start=True, stop=True)
                S("te", bc_mm, reads=[r_rdb[0], r_cst], writes=[r_pB[hp]])
                S("ve", lambda q, hp=hp: q.tensor_copy(bcs[:, :], pB[0:64, hp, :]),
                  reads=[r_pB[hp]], writes=[r_bc[0]])
                S("ve", lambda q, h=h, rr=rr, hp=hp: q.tensor_tensor(
                    ysb[rr, h // 2, :], pA[0:64, hp, :], bcs[:, :], ALU.mult),
                  reads=[r_pA[hp], r_bc[0]], writes=[r_y])

            for s in range(4):
                sp = s % 2
                rows = slice(tb + s * 128, tb + (s + 1) * 128)

                def proj_mm(q, s=s):
                    ii = None
                    for oc in range(2):
                        for fc in range(8):
                            ii = q.matmul(pC[:, oc, :], ysb[:, fc, s * 128:(s + 1) * 128],
                                          ow[:, fc, oc * 512:(oc + 1) * 512],
                                          start=(fc == 0), stop=(fc == 7))
                    return ii
                S("te", proj_mm, reads=[r_y, r_cst], writes=[r_pC[0], r_pC[1]])
                S("sy", lambda q, rows=rows: ld(q, xn1t[:], io["xn1_d"][rows, :]),
                  writes=[r_xn1])

                def xx_c(q, sp=sp):
                    for oc in range(2):
                        cl = slice(oc * 512, (oc + 1) * 512)
                        q.tensor_tensor(xxs[:, 0, cl], pC[:, oc, :], obt[:, cl], ALU.add)
                    ii = None
                    for oc in range(2):
                        cl = slice(oc * 512, (oc + 1) * 512)
                        ii = q.tensor_tensor(xxs[:, 0, cl], xxs[:, 0, cl],
                                             xn1t[:, cl], ALU.add)
                    return ii
                S("ve", xx_c, reads=[r_pC[0], r_pC[1], r_xn1, r_cst], writes=[r_xx[sp]])
                S("ve", lambda q, sp=sp: q.reduce_sum(ncol[:, sp, 0:1], xxs[:, 0, :], AX.X),
                  reads=[r_xx[sp]], writes=[r_nc[sp]])
                S("sc", lambda q, sp=sp: q.activation(sq[:], xxs[:, 0, :], AF.Square,
                                                      accum_out=ncol[:, sp, 1:2]),
                  reads=[r_xx[sp]], writes=[r_sq, r_nc[sp]])
                S("ve", lambda q, sp=sp: q.tensor_scalar(ncol[:, sp, 2:3], ncol[:, sp, 0:1],
                                                         1.0 / C, None, ALU.mult),
                  reads=[r_nc[sp]], writes=[r_nc[sp]])
                S("ve", lambda q, sp=sp: q.tensor_tensor(ncol[:, sp, 3:4], ncol[:, sp, 0:1],
                                                         ncol[:, sp, 2:3], ALU.mult),
                  reads=[r_nc[sp]], writes=[r_nc[sp]])
                S("ve", lambda q, sp=sp: q.tensor_tensor(ncol[:, sp, 4:5], ncol[:, sp, 1:2],
                                                         ncol[:, sp, 3:4], ALU.subtract),
                  reads=[r_nc[sp]], writes=[r_nc[sp]])
                S("ve", lambda q, sp=sp: q.tensor_scalar(ncol[:, sp, 5:6], ncol[:, sp, 4:5],
                                                         1.0 / C, None, ALU.mult),
                  reads=[r_nc[sp]], writes=[r_nc[sp]])
                S("sc", lambda q, sp=sp: q.activation(ncol[:, sp, 6:7], ncol[:, sp, 5:6],
                                                      AF.Ln, bias=cc[:, 0:1]),
                  reads=[r_nc[sp], r_cst], writes=[r_nc[sp]])
                S("sc", lambda q, sp=sp: q.activation(ncol[:, sp, 7:8], ncol[:, sp, 6:7],
                                                      AF.Exp, scale=-0.5),
                  reads=[r_nc[sp]], writes=[r_nc[sp]])
                S("ve", lambda q, sp=sp: q.tensor_scalar(sq[:], xxs[:, 0, :],
                                                         ncol[:, sp, 2:3], ncol[:, sp, 7:8],
                                                         ALU.subtract, ALU.mult),
                  reads=[r_xx[sp], r_nc[sp]], writes=[r_sq])
                S("ve", lambda q: q.tensor_tensor(ln2b[:], sq[:], g2[:], ALU.mult),
                  reads=[r_sq, r_cst], writes=[r_ln2b])
                S("ve", lambda q: q.scalar_tensor_tensor(ln2b[:], ln2b[:], 1.0, be2[:],
                                                         ALU.mult, ALU.add),
                  reads=[r_ln2b, r_cst], writes=[r_ln2b])

                def tr_ln2(q):
                    ii = None
                    for j in range(8):
                        ii = q.transpose(ptr[:, j * 128:(j + 1) * 128],
                                         ln2b[:, j * 128:(j + 1) * 128], idb[:])
                    return ii
                S("te", tr_ln2, reads=[r_ln2b, r_cst], writes=[r_ptr], no_drain=True)
                S("ve", lambda q: q.tensor_copy(
                    lnT[:, :, :].rearrange("a b c -> a (b c)"), ptr[:, :]),
                  reads=[r_ptr], writes=[r_lnT])

                for hb in range(8):
                    def h_mm(q, hb=hb):
                        ii = None
                        for oj in range(4):
                            oc = hb * 4 + oj
                            for fc in range(8):
                                ii = q.matmul(pC[:, hb % 2, oj * 128:(oj + 1) * 128],
                                              w1[:, fc, oc * 128:(oc + 1) * 128],
                                              lnT[:, fc, :],
                                              start=(fc == 0), stop=(fc == 7))
                        return ii
                    S("te", h_mm, reads=[r_lnT, r_cst], writes=[r_pC[hb % 2]])

                    def h_act(q, hb=hb):
                        ii = None
                        for oj in range(4):
                            oc = hb * 4 + oj
                            ii = q.activation(hms[:, oc, :],
                                              pC[:, hb % 2, oj * 128:(oj + 1) * 128],
                                              AF.Gelu, bias=b1c[:, oc:oc + 1])
                        return ii
                    S("sc", h_act, reads=[r_pC[hb % 2], r_cst], writes=[r_hm])

                S("sy", lambda q, rows=rows: ld(q, x1t[:], io["x1_d"][rows, :]),
                  writes=[r_x1])
                S("gpd", lambda q, rows=rows: q.dma_start(
                    out=x1t[:], in_=io["f1_d"][rows, :], accum_op=ALU.add),
                  reads=[r_x1], writes=[r_x1])
                for oc in range(2):
                    def o_mm(q, oc=oc):
                        ii = None
                        for ic in range(32):
                            ii = q.matmul(pD[:, :], hms[:, ic, :],
                                          w2[:, ic, oc * 512:(oc + 1) * 512],
                                          start=(ic == 0), stop=(ic == 31))
                        return ii
                    S("te", o_mm, reads=[r_hm, r_cst], writes=[r_pD])

                    def fin(q, sp=sp, oc=oc):
                        cl = slice(oc * 512, (oc + 1) * 512)
                        q.tensor_tensor(outs[:, cl], pD[:, :], b2t[:, cl], ALU.add)
                        q.drain()
                        q.tensor_tensor(outs[:, cl], outs[:, cl], xxs[:, 0, cl], ALU.add)
                        q.drain()
                        return q.tensor_tensor(outs[:, cl], outs[:, cl], x1t[:, cl], ALU.add)
                    S("ve", fin, reads=[r_pD, r_xx[sp], r_x1, r_cst], writes=[r_out])
                S("scd", lambda q, rows=rows: ld(q, o_out[rows, :], outs[:]),
                  reads=[r_out])

        with nc.Block() as block:
            sch.emit(nc, block)
    return nc



S4_TILES = NT
S4_PHASE = 5
S4_APPS = 6
S4_CPRE = 3


def build_s4(nc, io):
    o_y = io["y_d"]
    f_int = nc.dram_tensor("s4_fint", [128, 1024], BF16, kind="Internal").ap()
    with ExitStack() as _es:
        ec = _es.enter_context
        sb = lambda *a: ec(nc.sbuf_tensor(*a))
        dmM = sb("s4_dmM", [128, 128], F32)
        dmP = sb("s4_dmP", [128, 128], F32)
        msk = sb("s4_msk", [64, 320], F32)
        idb = sb("s4_idb", [128, 128], BF16)
        F_sb = sb("s4_F", [128, 8, 128], BF16)
        Ff = sb("s4_Ff", [128, 8, 128], F32)
        fsum = sb("s4_fsum", [128, 1024], F32)
        fb = sb("s4_fb", [128, 1024], BF16)
        wt = sb("s4_w", [128, 2, 1024], F32)
        rt = sb("s4_r", [128, 2, 1024], BF16)
        kt = sb("s4_k", [128, 2, 1024], BF16)
        kkt = sb("s4_kk", [128, 2, 1024], BF16)
        bt = sb("s4_b", [128, 2, 1024], BF16)
        vt = sb("s4_v", [64, 2, 2, 1024], BF16)
        ers = sb("s4_er", [128, 1024], F32)
        eas = sb("s4_ea", [128, 1024], F32)
        ebs = sb("s4_eb", [128, 1024], F32)
        rts = sb("s4_rt", [64, 2, 1024], BF16)
        ats = sb("s4_at", [64, 2, 1024], BF16)
        bhs = sb("s4_bh", [64, 2, 1024], BF16)
        khs = sb("s4_kh", [64, 2, 1024], BF16)
        chT = sb("s4_chT", [128, 2, 4, 8, 64], BF16)
        TmA = sb("s4_TmA", [64, 3, 64], BF16)
        TmB = sb("s4_TmB", [64, 2, 64], BF16)
        Qp = sb("s4_Qp", [64, 5, 64], BF16)
        Qu = sb("s4_Qu", [64, 2, 64], BF16)
        Xs = sb("s4_X", [64, 2, 64], BF16)
        Cs = sb("s4_C", [64, 64], BF16)
        Gs = sb("s4_G", [128, 8, 64], BF16)
        yts = sb("s4_y", [128, 2, 1024], F32)
        pcs = ec(nc.psum_tensor("s4_pcs", [128, 2, 512], F32))
        pcp = ec(nc.psum_tensor("s4_pcp", [128, 2, 512], F32))
        ptr = ec(nc.psum_tensor("s4_ptr", [128, 2, 1024], BF16))
        pT = ec(nc.psum_tensor("s4_pT", [128, 2, 512], F32))

        sch = Sched()
        S = mk_S(sch)
        r_cst, r_fsum, r_fb, r_fint, r_F = Res(), Res(), Res(), Res(), Res()
        r_w = [Res(), Res()]
        r_r = [Res(), Res()]
        r_k = [Res(), Res()]
        r_kk = [Res(), Res()]
        r_b = [Res(), Res()]
        r_v = [Res(), Res()]
        r_pcs = [Res(), Res()]
        r_pcp = [Res(), Res()]
        r_ptr = [Res(), Res()]
        r_pT = [Res(), Res()]
        r_er, r_ea, r_eb = Res(), Res(), Res()
        r_rts, r_ats, r_bhs, r_khs = Res(), Res(), Res(), Res()
        r_chT = [Res(), Res()]
        r_Tm, r_Qp, r_Cs = Res(), Res(), Res()
        r_Qu = [Res(), Res()]
        r_X = [Res(), Res()]
        r_G = [Res() for _ in range(8)]
        r_y = [Res(), Res()]

        def ld(q, dst, src):
            return q.dma_start(out=dst, in_=src)

        for nm, dst in [("s4_dmatM", dmM), ("s4_dmatP", dmP), ("s4_masks", msk)]:
            S("sy", lambda q, nm=nm, dst=dst: ld(q, dst[:], io[nm][:, :]), writes=[r_cst])
        S("gpd", lambda q: ld(q, idb[:], io["c_ident_bf"][:, :]), writes=[r_cst])
        gz = S("ve", lambda q: q.memset(Gs[:].rearrange("a b c -> a (b c)"), 0.0))
        for h2 in range(8):
            r_G[h2].w = gz

        S("sy", lambda q: ld(q, fsum[:], io["wlm_d"][:, :]), writes=[r_fsum])
        S("gpd", lambda q: q.dma_start(out=fsum[:], in_=io["mcs_d"][:, :],
                                       accum_op=ALU.add),
          reads=[r_fsum], writes=[r_fsum])
        S("sc", lambda q: q.activation(fb[:], fsum[:], AF.Exp),
          reads=[r_fsum], writes=[r_fb])
        S("gpd", lambda q: ld(q, f_int[:, :], fb[:]), reads=[r_fb], writes=[r_fint])

        for fc in range(8):
            S("sy", lambda q, fc=fc: q.dma_start_transpose(
                F_sb[:, fc, :], f_int[:, fc * 128:(fc + 1) * 128]),
              reads=[r_fint], writes=[r_F])
        S("ve", lambda q: q.tensor_copy(Ff[:].rearrange("a b c -> a (b c)"),
                                        F_sb[:].rearrange("a b c -> a (b c)")),
          reads=[r_F], writes=[r_F])

        for i in range(S4_TILES):
            p = i % 2
            rows = slice(i * 128, (i + 1) * 128)
            S("sy", lambda q, rows=rows, p=p: ld(q, wt[:, p, :], io["w_d"][rows, :]),
              reads=[r_cst], writes=[r_w[p]])
            S("sy", lambda q, rows=rows, p=p: ld(q, rt[:, p, :], io["r_d"][rows, :]),
              writes=[r_r[p]])
            S("sy", lambda q, rows=rows, p=p: ld(q, kt[:, p, :], io["k_d"][rows, :]),
              writes=[r_k[p]])
            S("scd", lambda q, rows=rows, p=p: ld(q, kkt[:, p, :], io["kk_d"][rows, :]),
              writes=[r_kk[p]])
            S("scd", lambda q, rows=rows, p=p: ld(q, bt[:, p, :], io["b_d"][rows, :]),
              writes=[r_b[p]])
            S("scd", lambda q, rows=rows, p=p: ld(
                q, vt[:, p, :, :],
                io["v_d"][rows, :].rearrange("(cj t) c -> t cj c", cj=2)),
              writes=[r_v[p]])

            def cs_mm(q, p=p):
                q.matmul(pcs[:, 0, :], dmM[:], wt[:, p, 0:512], start=True, stop=True)
                q.matmul(pcs[:, 1, :], dmM[:], wt[:, p, 512:1024], start=True, stop=True)
                q.matmul(pcp[:, 0, :], dmP[:], wt[:, p, 0:512], start=True, stop=True)
                return q.matmul(pcp[:, 1, :], dmP[:], wt[:, p, 512:1024],
                                start=True, stop=True)
            S("te", cs_mm, reads=[r_w[p], r_cst],
              writes=[r_pcs[0], r_pcs[1], r_pcp[0], r_pcp[1]])
            def exps(q):
                q.activation(ers[:, 0:512], pcs[:, 0, :], AF.Exp)
                q.drain()
                q.activation(ers[:, 512:1024], pcs[:, 1, :], AF.Exp)
                q.drain()
                q.activation(ebs[:, 0:512], pcs[:, 0, :], AF.Exp, scale=-1.0)
                q.drain()
                q.activation(ebs[:, 512:1024], pcs[:, 1, :], AF.Exp, scale=-1.0)
                q.drain()
                q.activation(eas[:, 0:512], pcp[:, 0, :], AF.Exp)
                q.drain()
                return q.activation(eas[:, 512:1024], pcp[:, 1, :], AF.Exp)
            S("sc", exps, reads=[r_pcs[0], r_pcs[1], r_pcp[0], r_pcp[1]],
              writes=[r_er, r_eb, r_ea])

            def scl(q, p=p):
                ii = None
                for cj in range(2):
                    rsl = slice(cj * 64, (cj + 1) * 64)
                    q.tensor_tensor(rts[:, cj, :], rt[rsl, p, :], ers[rsl, :], ALU.mult)
                    q.scalar_tensor_tensor(ats[:, cj, :], kt[rsl, p, :], -1.0,
                                           eas[rsl, :], ALU.mult, ALU.mult)
                    q.tensor_tensor(bhs[:, cj, :], bt[rsl, p, :], ebs[rsl, :], ALU.mult)
                    ii = q.tensor_tensor(khs[:, cj, :], kkt[rsl, p, :], ebs[rsl, :],
                                         ALU.mult)
                return ii
            S("ve", scl, reads=[r_r[p], r_k[p], r_b[p], r_kk[p], r_er, r_ea, r_eb],
              writes=[r_rts, r_ats, r_bhs, r_khs])

            for cj in range(2):
                def tr_w1(q, cj=cj):
                    ii = None
                    for ai, arr in enumerate((ats, bhs)):
                        for fc in range(8):
                            ii = q.transpose(
                                ptr[:, cj, (ai * 8 + fc) * 64:(ai * 8 + fc + 1) * 64],
                                arr[:, cj, fc * 128:(fc + 1) * 128], idb[0:64, 0:64])
                    return ii
                S("te", tr_w1, reads=[r_ats, r_bhs, r_cst], writes=[r_ptr[cj]],
                  no_drain=True)
                S("ve", lambda q, cj=cj: q.tensor_copy(
                    chT[:, cj, 0:2, :, :].rearrange("a b c d -> a (b c d)"),
                    ptr[:, cj, :]),
                  reads=[r_ptr[cj]], writes=[r_chT[cj]])

                def tr_w2(q, cj=cj):
                    ii = None
                    for ai, arr in enumerate((khs, rts)):
                        for fc in range(8):
                            ii = q.transpose(
                                ptr[:, cj, (ai * 8 + fc) * 64:(ai * 8 + fc + 1) * 64],
                                arr[:, cj, fc * 128:(fc + 1) * 128], idb[0:64, 0:64])
                    return ii
                S("te", tr_w2, reads=[r_khs, r_rts, r_cst], writes=[r_ptr[cj]],
                  no_drain=True)
                S("ve", lambda q, cj=cj: q.tensor_copy(
                    chT[:, cj, 2:4, :, :].rearrange("a b c d -> a (b c d)"),
                    ptr[:, cj, :]),
                  reads=[r_ptr[cj]], writes=[r_chT[cj]])

            for cj in range(2):
                for h in range(16):
                    rr = slice((h % 2) * 64, (h % 2) * 64 + 64)
                    h2 = h // 2
                    hc = slice(h * 64, (h + 1) * 64)

                    def tf_mm(q, cj=cj, rr=rr, h2=h2):
                        a_ = chT[rr, cj, 0, h2, :]
                        b_ = chT[rr, cj, 1, h2, :]
                        k_ = chT[rr, cj, 2, h2, :]
                        r_ = chT[rr, cj, 3, h2, :]
                        q.matmul(pT[0:64, 0, 0:64], a_, b_, start=True, stop=True)
                        q.matmul(pT[0:64, 1, 0:64], b_, a_, start=True, stop=True)
                        q.matmul(pT[0:64, 0, 64:128], k_, a_, start=True, stop=True)
                        q.matmul(pT[0:64, 1, 64:128], b_, r_, start=True, stop=True)
                        return q.matmul(pT[0:64, 0, 128:192], k_, r_, start=True,
                                        stop=True)
                    S("te", tf_mm, reads=[r_chT[cj], r_cst], writes=[r_pT[0], r_pT[1]])

                    def tf_msk(q):
                        q.tensor_tensor(TmA[:, :, :].rearrange("a b c -> a (b c)"),
                                        pT[0:64, 0, 0:192], msk[:, 0:192], ALU.mult)
                        return q.tensor_tensor(TmB[:, :, :].rearrange("a b c -> a (b c)"),
                                               pT[0:64, 1, 0:128], msk[:, 192:320],
                                               ALU.mult)
                    S("ve", tf_msk, reads=[r_pT[0], r_pT[1], r_cst], writes=[r_Tm])

                    for li in range(5):
                        bk = li % 2

                        def sq_mm(q, li=li, bk=bk):
                            if li == 0:
                                qu_, qp_ = TmA[:, 0, :], TmB[:, 0, :]
                            else:
                                qu_, qp_ = Qu[:, (li - 1) % 2, :], Qp[:, li - 1, :]
                            if li < 4:
                                q.matmul(pT[0:64, bk, 256:320], qp_, qu_, start=True,
                                         stop=True)
                            return q.matmul(pT[0:64, 1 - bk, 256:320], qu_, qp_,
                                            start=True, stop=True)
                        S("te", sq_mm,
                          reads=[r_Tm, r_Qu[(li - 1) % 2] if li > 0 else r_Tm, r_Qp],
                          writes=[r_pT[0], r_pT[1]])

                        def sq_ev(q, li=li, bk=bk):
                            if li < 4:
                                q.tensor_copy(Qu[:, li % 2, :], pT[0:64, bk, 256:320])
                            return q.tensor_copy(Qp[:, li, :], pT[0:64, 1 - bk, 256:320])
                        S("ve", sq_ev, reads=[r_pT[0], r_pT[1]],
                          writes=[r_Qu[li % 2], r_Qp])

                    def cpre_mm(q, cj=cj, rr=rr, h2=h2, hc=hc, p=p):
                        q.matmul(pcs[0:64, 0, 0:64], chT[rr, cj, 0, h2, :],
                                 Gs[rr, h2, :], start=True, stop=True)
                        return q.matmul(pcs[0:64, 1, 0:64], TmA[:, 1, :],
                                        vt[:, p, cj, hc], start=True, stop=True)
                    S("te", cpre_mm, reads=[r_chT[cj], r_G[h2], r_Tm, r_v[p]],
                      writes=[r_pcs[0], r_pcs[1]])
                    S("ve", lambda q: q.tensor_copy(Xs[:, 1, :], pcs[0:64, 0, 0:64]),
                      reads=[r_pcs[0]], writes=[r_X[1]])
                    S("ve", lambda q: q.tensor_tensor(Xs[:, 0, :], pcs[0:64, 1, 0:64],
                                                      Xs[:, 1, :], ALU.add),
                      reads=[r_pcs[1], r_X[1]], writes=[r_X[0]])
                    for ai2 in range(6):
                        xp = ai2 % 2
                        bk2 = ai2 % 2

                        def app_mm(q, ai2=ai2, xp=xp, bk2=bk2):
                            qq = TmB[:, 0, :] if ai2 == 5 else Qp[:, 4 - ai2, :]
                            return q.matmul(pcs[0:64, bk2, 0:64], qq, Xs[:, xp, :],
                                            start=True, stop=True)
                        S("te", app_mm, reads=[r_Qp, r_Tm, r_X[xp]], writes=[r_pcs[bk2]])
                        if ai2 < 5:
                            S("ve", lambda q, xp=xp, bk2=bk2: q.tensor_tensor(
                                Xs[:, 1 - xp, :], pcs[0:64, bk2, 0:64], Xs[:, xp, :],
                                ALU.add),
                              reads=[r_pcs[bk2], r_X[xp]], writes=[r_X[1 - xp]])
                        else:
                            S("ve", lambda q, xp=xp, bk2=bk2: q.tensor_tensor(
                                Cs[:], pcs[0:64, bk2, 0:64], Xs[:, xp, :], ALU.add),
                              reads=[r_pcs[bk2], r_X[xp]], writes=[r_Cs])

                    def y_mm(q, cj=cj, rr=rr, h2=h2, p=p, hc=hc):
                        q.matmul(pT[0:64, 0, 192:256], chT[rr, cj, 3, h2, :],
                                 Gs[rr, h2, :], start=True, stop=True)
                        q.matmul(pT[0:64, 1, 192:256], TmB[:, 1, :], Cs[:],
                                 start=True, stop=False)
                        return q.matmul(pT[0:64, 1, 192:256], TmA[:, 2, :],
                                        vt[:, p, cj, hc], start=False, stop=True)
                    S("te", y_mm, reads=[r_chT[cj], r_G[h2], r_Tm, r_Cs, r_v[p]],
                      writes=[r_pT[0], r_pT[1]])
                    S("ve", lambda q, cj=cj, p=p, hc=hc: q.tensor_copy(
                        yts[cj * 64:(cj + 1) * 64, p, hc], pT[0:64, 0, 192:256]),
                      reads=[r_pT[0]], writes=[r_y[p]])
                    S("ve", lambda q, cj=cj, p=p, hc=hc: q.tensor_tensor(
                        yts[cj * 64:(cj + 1) * 64, p, hc], pT[0:64, 1, 192:256],
                        yts[cj * 64:(cj + 1) * 64, p, hc], ALU.add),
                      reads=[r_pT[1], r_y[p]], writes=[r_y[p]])

                    def st_mm(q, cj=cj, p=p, hc=hc):
                        q.matmul(pcp[0:64, 0, 0:64], bhs[:, cj, hc], Cs[:],
                                 start=True, stop=False)
                        return q.matmul(pcp[0:64, 0, 0:64], khs[:, cj, hc],
                                        vt[:, p, cj, hc], start=False, stop=True)
                    S("te", st_mm, reads=[r_bhs, r_khs, r_Cs, r_v[p]], writes=[r_pcp[0]])

                    def st_up(q, cj=cj, rr=rr, h2=h2, i=i):
                        ci = 2 * i + cj
                        q.tensor_tensor(Xs[:, 0, :], pcp[0:64, 0, 0:64], Gs[rr, h2, :],
                                        ALU.add)
                        q.drain()
                        return q.tensor_scalar(Gs[rr, h2, :], Xs[:, 0, :],
                                               Ff[rr, h2, ci:ci + 1], None, ALU.mult)
                    S("ve", st_up, reads=[r_pcp[0], r_G[h2], r_F, r_X[0]],
                      writes=[r_G[h2], r_X[0]])

            S("scd", lambda q, rows=rows, p=p: ld(q, o_y[rows, :], yts[:, p, :]),
              reads=[r_y[p]])

        with nc.Block() as block:
            sch.emit(nc, block)
    return nc


BUILDERS = {"s1": build_s1, "s2": build_s2, "s3": build_s3, "s5": build_s5,
            "s6": build_s6, "s7a": build_s7a, "s7b": build_s7b, "s4": build_s4}


def build_program(stages):
    nc = bass.Bass("TRN2", target_bir_lowering=False)
    needed_in, needed_out = [], []
    for s in stages:
        for n in STAGE_IN[s]:
            if n not in needed_in and not any(n in STAGE_OUT[t] for t in stages[:stages.index(s)]):
                needed_in.append(n)
        for n in STAGE_OUT[s]:
            needed_out.append(n)
    io = {}
    for n in needed_in:
        shape, dt, _ = IO_SPECS[n]
        io[n] = nc.dram_tensor(n, shape, dt, kind="ExternalInput").ap()
    for n in needed_out:
        shape, dt, _ = IO_SPECS[n]
        io[n] = nc.dram_tensor(n, shape, dt, kind="ExternalOutput").ap()
    for s in stages:
        BUILDERS[s](nc, io)
    return nc, needed_in, needed_out


_NP_DT = {F32: np.float32, BF16: BF}


def run_stages(stage_groups, host_shared, core_vals):
    for stages in stage_groups:
        nc, ins, outs = build_program(stages)
        in_maps = []
        for cv in core_vals:
            m = {}
            for n in ins:
                m[n] = cv[n] if IO_SPECS[n][2] else host_shared[n]
            in_maps.append(m)
        res = run_bass_kernel_spmd(nc, in_maps, core_ids=list(range(8)))
        for c, r in enumerate(res.results):
            for n in outs:
                core_vals[c][n] = r[n]
    return core_vals


ALL_STAGES = ["s1", "s2", "s3", "s4", "s5", "s6", "s7a", "s7b"]


def build_fused(stages=None):
    stages = stages or ALL_STAGES
    nc = bass.Bass("TRN2", target_bir_lowering=False)
    needed = []
    for s in stages:
        for n in STAGE_IN[s] + STAGE_OUT[s]:
            if n not in needed:
                needed.append(n)
    produced = {n for s in stages for n in STAGE_OUT[s]}
    io, ext_in = {}, []
    for n in needed:
        shape, dt, per_core = IO_SPECS[n]
        if n == "out_d":
            kind = "ExternalOutput"
        elif n in produced:
            kind = "Internal"
        else:
            kind = "ExternalInput"
            ext_in.append(n)
        io[n] = nc.dram_tensor(n, shape, dt, kind=kind).ap()
    bar = nc.alloc_semaphore(name="g_bar")
    for k, s in enumerate(stages):
        BUILDERS[s](nc, io)
        with nc.Block() as bb:
            for eng in ["sync", "vector", "scalar", "tensor", "gpsimd"]:
                def mk(eng=eng, k=k):
                    def f(q):
                        q.drain().then_inc(bar, 1)
                        q.wait_ge(bar, 5 * (k + 1))
                    return f
                getattr(bb, eng)(mk())
    return nc, ext_in


def kernel(**inputs):
    host_shared = _prep_host(inputs)
    core_vals = [_prep_core(inputs, b) for b in range(8)]
    nc, ext_in = build_fused()
    in_maps = []
    for cv in core_vals:
        m = {}
        for n in ext_in:
            m[n] = cv[n] if IO_SPECS[n][2] else host_shared[n]
        in_maps.append(m)
    res = run_bass_kernel_spmd(nc, in_maps, core_ids=list(range(8)))
    return np.stack([res.results[b]["out_d"] for b in range(8)], axis=0)


# version 03
# speedup vs baseline: 1.2059x; 1.2059x over previous
from contextlib import ExitStack
import numpy as np
import ml_dtypes
import concourse.bass as bass
import concourse.mybir as mybir
from concourse.bass_utils import run_bass_kernel_spmd

F32 = mybir.dt.float32
BF16 = mybir.dt.bfloat16
AF = mybir.ActivationFunctionType
ALU = mybir.AluOpType
AX = mybir.AxisListType

T = 8192
C = 1024
NT = T // 128
H, N = 16, 64
LCH = 64
NCH = T // LCH
BF = ml_dtypes.bfloat16



def _prep_host(inputs):
    g = {k: np.asarray(v) for k, v in inputs.items()}
    d = {}
    f32 = np.float32
    ct = np.ascontiguousarray

    d["c_ident_f32"] = np.eye(128, dtype=f32)
    d["c_ident_bf"] = np.eye(128, dtype=f32).astype(BF)
    consts = np.zeros((128, 8), f32)
    consts[:, 0] = 1e-5
    consts[:, 1] = 1.6e-4
    consts[:, 2] = 1.0
    consts[:, 3] = 1e-6
    consts[:, 4] = 0.5
    consts[:, 5] = -0.5
    consts[:, 6] = 2.0
    consts[:, 7] = -1e30
    d["c_consts"] = consts

    d["s2_gate_wt"] = ct(g["L0_ffn_gate_w"].T.astype(f32))
    d["s2_gate_bias"] = ct(np.tile(g["L0_ffn_gate_bias"].astype(f32)[None, :], (128, 1)))
    w1 = np.concatenate([g["L0_ffn_w1"], g["L0_ffn_sw1"][None]], 0)
    w3 = np.concatenate([g["L0_ffn_w3"], g["L0_ffn_sw3"][None]], 0)
    w2 = np.concatenate([g["L0_ffn_w2"], g["L0_ffn_sw2"][None]], 0)
    d["s2_w1t"] = ct(np.transpose(w1, (0, 2, 1)).astype(f32))
    d["s2_w3t"] = ct(np.transpose(w3, (0, 2, 1)).astype(f32))
    d["s2_w2t"] = ct(np.transpose(w2, (0, 2, 1)).astype(f32))

    d["s3_xA"] = ct(g["L0_attn_x_A"].astype(f32))
    d["s3_xB"] = ct(g["L0_attn_x_B"].astype(f32))
    lam = g["L0_attn_lambdas"].astype(f32).reshape(6, 1024)
    lamc = np.zeros((128, 48), f32)
    for gg in range(6):
        for fc in range(8):
            lamc[:, gg * 8 + fc] = lam[gg, fc * 128:(fc + 1) * 128]
    d["s3_lam"] = lamc
    xm = g["L0_attn_x_miu"].astype(f32).reshape(1024)
    d["s3_xmiu"] = ct(xm.reshape(8, 128).T)
    d["s3_wA"] = ct(g["L0_attn_w_A"].astype(f32))
    d["s3_kA"] = ct(g["L0_attn_k_A"].astype(f32))
    d["s3_aA"] = ct(g["L0_attn_a_A"].astype(f32))
    d["s3_gA"] = ct(g["L0_attn_g_A"].astype(f32))
    d["s3_wB"] = ct(g["L0_attn_w_B"].astype(f32))
    d["s3_kB"] = ct(g["L0_attn_k_B"].astype(f32))
    d["s3_aB"] = ct(g["L0_attn_a_B"].astype(f32))
    d["s3_gB"] = ct(g["L0_attn_g_B"].astype(f32))
    d["s3_wmiu_t"] = ct(np.tile(g["L0_attn_w_miu"].astype(f32).reshape(1, 1024), (128, 1)))
    d["s3_atime_t"] = ct(np.tile(g["L0_attn_a_time"].astype(f32).reshape(1, 1024), (128, 1)))
    d["s3_xtime_t"] = ct(np.tile(g["L0_attn_x_time"].astype(f32).reshape(1, 1024), (128, 1)))
    d["s3_rWt"] = ct(g["L0_attn_rW"].T.astype(f32))
    d["s3_kWt"] = ct(g["L0_attn_kW"].T.astype(f32))
    d["s3_vWt"] = ct(g["L0_attn_vW"].T.astype(f32))
    c3 = np.zeros((128, 4), f32)
    c3[0:32, 0] = 1.0
    c3[64:96, 1] = 1.0
    c3[32:64, 2] = 1.0
    c3[96:128, 3] = 1.0
    d["s3_c3aux"] = c3

    d["s5_oWt"] = ct(g["L0_attn_oW"].T.astype(f32))
    d["s5_gng_t"] = ct(np.tile(g["L0_attn_gn_gamma"].astype(f32)[None, :], (128, 1)))
    d["s5_gnb_t"] = ct(np.tile(g["L0_attn_gn_beta"].astype(f32)[None, :], (128, 1)))

    d["s6_rw"] = ct(g["L1_ffn_router_w"].astype(f32))
    d["s6_w1"] = ct(g["L1_ffn_w1"].astype(f32))
    b1 = g["L1_ffn_b1"].astype(f32)
    b1c = np.zeros((128, 224), f32)
    for e in range(14):
        for ic in range(16):
            b1c[:, e * 16 + ic] = b1[e, ic * 128:(ic + 1) * 128]
    d["s6_b1c"] = b1c
    d["s6_w2"] = ct(g["L1_ffn_w2"].astype(f32))
    d["s6_b2"] = ct(g["L1_ffn_b2"].astype(f32))

    d["c_ones_bf"] = np.ones((128, 128), f32).astype(BF)
    qkvw = g["L1_attn_qkv_w"].astype(f32)
    qkvb = g["L1_attn_qkv_b"].astype(f32)
    d["s7_qw"] = ct(qkvw[:, 0:1024])
    d["s7_kvw"] = ct(qkvw[:, 1024:3072])
    d["s7_qbc"] = ct(qkvb[0:1024].reshape(8, 128).T)
    d["s7_kvb_t"] = ct(np.tile(qkvb[1024:3072][None, :], (128, 1)))
    d["s7_ow"] = ct(g["L1_attn_out_w"].astype(f32))
    d["s7_ob_t"] = ct(np.tile(g["L1_attn_out_b"].astype(f32)[None, :], (128, 1)))
    d["s7_w1"] = ct(g["L1_attn_w1"].astype(f32))
    d["s7_b1c"] = ct(g["L1_attn_b1"].astype(f32).reshape(32, 128).T)
    d["s7_w2"] = ct(g["L1_attn_w2"].astype(f32))
    d["s7_b2_t"] = ct(np.tile(g["L1_attn_b2"].astype(f32)[None, :], (128, 1)))
    for nm, key in [("s7_g1_t", "L1_attn_g1"), ("s7_be1_t", "L1_attn_be1"),
                    ("s7_g2_t", "L1_attn_g2"), ("s7_be2_t", "L1_attn_be2")]:
        d[nm] = ct(np.tile(g[key].astype(f32)[None, :], (128, 1)))

    dm = np.zeros((128, 128), f32)
    dp = np.zeros((128, 128), f32)
    for cj in range(2):
        for s_ in range(64):
            j = cj * 64 + s_
            for ul in range(64):
                u = cj * 64 + ul
                dm[u, j] = (1.0 if ul <= s_ else 0.0) - (1.0 if ul <= 31 else 0.0)
                dp[u, j] = (1.0 if ul <= s_ - 1 else 0.0) - (1.0 if ul <= 31 else 0.0)
    d["s4_dmatM"] = dm
    d["s4_dmatP"] = dp
    mlo = np.tril(np.ones((64, 64), f32), -1)
    mupS = np.triu(np.ones((64, 64), f32), 1)
    mupI = np.triu(np.ones((64, 64), f32), 0)
    masks = np.zeros((64, 320), f32)
    masks[:, 0:64] = mlo
    masks[:, 64:128] = mupS
    masks[:, 128:192] = mupI
    masks[:, 192:256] = mupS
    masks[:, 256:320] = mupI
    d["s4_masks"] = masks
    return d


def _prep_core(inputs, b):
    x = np.ascontiguousarray(np.asarray(inputs["L0_attn_x"])[b].astype(np.float32))
    return {"x": x}



IO_SPECS = {
    "x": ([T, C], F32, True),
    "xnT_f32": ([C, T], F32, True),
    "dbg_xn": ([T, C], F32, True),
    "dbg_cols": ([NT * 128, 12], F32, True),
    "xnT_bf": ([C, T], BF16, True),
    "f0": ([T, C], F32, True),
    "dbg_comb": ([128, NT, 12], F32, True),
    "c_ident_f32": ([128, 128], F32, False),
    "c_consts": ([128, 8], F32, False),
    "c_ident_bf": ([128, 128], BF16, False),
    "s2_gate_wt": ([C, 8], F32, False),
    "s2_gate_bias": ([128, 8], F32, False),
    "s2_w1t": ([9, C, 2048], F32, False),
    "s2_w3t": ([9, C, 2048], F32, False),
    "s2_w2t": ([9, 2048, C], F32, False),
}

for _n in ["s3_xA", "s3_wA", "s3_kA", "s3_aA", "s3_gA", "s3_wB", "s3_kB", "s3_aB",
           "s3_gB", "s3_rWt", "s3_kWt", "s3_vWt"]:
    pass
IO_SPECS.update({
    "s3_xA": ([C, 192], F32, False),
    "s3_xB": ([6, 32, C], F32, False),
    "s3_lam": ([128, 48], F32, False),
    "s3_xmiu": ([128, 8], F32, False),
    "s3_wA": ([C, 64], F32, False),
    "s3_kA": ([C, 64], F32, False),
    "s3_aA": ([C, 64], F32, False),
    "s3_gA": ([C, 128], F32, False),
    "s3_wB": ([64, C], F32, False),
    "s3_kB": ([64, C], F32, False),
    "s3_aB": ([64, C], F32, False),
    "s3_gB": ([128, C], F32, False),
    "s3_wmiu_t": ([128, C], F32, False),
    "s3_atime_t": ([128, C], F32, False),
    "s3_xtime_t": ([128, C], F32, False),
    "s3_rWt": ([C, C], F32, False),
    "s3_kWt": ([C, C], F32, False),
    "s3_vWt": ([C, C], F32, False),
    "s3_c3aux": ([128, 4], F32, False),
    "w_d": ([T, C], F32, True),
    "r_d": ([T, C], BF16, True),
    "k_d": ([T, C], BF16, True),
    "kk_d": ([T, C], BF16, True),
    "v_d": ([T, C], BF16, True),
    "b_d": ([T, C], BF16, True),
    "g_d": ([T, C], BF16, True),
    "bsum_d": ([T, 16], F32, True),
    "wlm_d": ([NCH, C], F32, True),
    "mcs_d": ([NCH, C], F32, True),
    "y_d": ([T, C], F32, True),
    "x1_d": ([T, C], F32, True),
    "xn1_d": ([T, C], F32, True),
    "xn1T_bf": ([C, T], BF16, True),
    "s5_oWt": ([C, C], F32, False),
    "s5_gng_t": ([128, C], F32, False),
    "s5_gnb_t": ([128, C], F32, False),
    "f1_d": ([T, C], F32, True),
    "s6_rw": ([C, 14], F32, False),
    "s6_w1": ([14, C, 2048], F32, False),
    "s6_b1c": ([128, 224], F32, False),
    "s6_w2": ([14, 2048, C], F32, False),
    "s6_b2": ([14, C], F32, False),
    "qfT_d": ([C, T], BF16, True),
    "kvz_d": ([128, 8 * 66], BF16, True),
    "out_d": ([T, C], F32, True),
    "c_ones_bf": ([128, 128], BF16, False),
    "s7_qw": ([C, C], F32, False),
    "s7_kvw": ([C, 2048], F32, False),
    "s7_qbc": ([128, 8], F32, False),
    "s7_kvb_t": ([128, 2048], F32, False),
    "s7_ow": ([C, C], F32, False),
    "s7_ob_t": ([128, C], F32, False),
    "s7_w1": ([C, 4096], F32, False),
    "s7_b1c": ([128, 32], F32, False),
    "s7_w2": ([4096, C], F32, False),
    "s7_b2_t": ([128, C], F32, False),
    "s7_g1_t": ([128, C], F32, False),
    "s7_be1_t": ([128, C], F32, False),
    "s7_g2_t": ([128, C], F32, False),
    "s7_be2_t": ([128, C], F32, False),
    "s4_dmatM": ([128, 128], F32, False),
    "s4_dmatP": ([128, 128], F32, False),
    "s4_masks": ([64, 320], F32, False),
})

STAGE_IN = {
    "s1": ["x", "c_ident_f32", "c_ident_bf", "c_consts"],
    "s2": ["xnT_bf", "s2_gate_wt", "s2_gate_bias", "s2_w1t", "s2_w3t", "s2_w2t"],
    "s3": ["xnT_bf", "c_consts", "s3_xA", "s3_xB", "s3_lam", "s3_xmiu", "s3_wA",
           "s3_kA", "s3_aA", "s3_gA", "s3_wB", "s3_kB", "s3_aB", "s3_gB",
           "s3_wmiu_t", "s3_atime_t", "s3_xtime_t", "s3_rWt", "s3_kWt", "s3_vWt",
           "s3_c3aux"],
    "s5": ["y_d", "bsum_d", "v_d", "g_d", "x", "f0", "c_consts", "c_ident_bf",
           "s5_oWt", "s5_gng_t", "s5_gnb_t"],
    "s6": ["xn1T_bf", "c_ident_bf", "s6_rw", "s6_w1", "s6_b1c", "s6_w2", "s6_b2"],
    "s4": ["w_d", "r_d", "k_d", "kk_d", "v_d", "b_d", "wlm_d", "mcs_d",
           "c_ident_bf", "s4_dmatM", "s4_dmatP", "s4_masks"],
    "s7a": ["xn1_d", "c_consts", "c_ident_bf", "c_ones_bf", "s7_qw", "s7_kvw",
            "s7_qbc", "s7_kvb_t", "s7_g1_t", "s7_be1_t"],
    "s7b": ["qfT_d", "kvz_d", "xn1_d", "x1_d", "f1_d", "c_consts", "c_ident_bf",
            "c_ones_bf", "s7_ow", "s7_ob_t", "s7_w1", "s7_b1c", "s7_w2", "s7_b2_t",
            "s7_g2_t", "s7_be2_t"],
}
STAGE_OUT = {
    "s1": ["xnT_f32", "xnT_bf"],
    "s2": ["f0"],
    "s3": ["w_d", "r_d", "k_d", "kk_d", "v_d", "b_d", "g_d", "bsum_d",
           "wlm_d", "mcs_d"],
    "s5": ["x1_d", "xn1_d", "xn1T_bf"],
    "s6": ["f1_d"],
    "s4": ["y_d"],
    "s7a": ["qfT_d", "kvz_d"],
    "s7b": ["out_d"],
}



def build_s1(nc, io):
    x = io["x"]
    xnT = io["xnT_f32"]
    xnTb = io["xnT_bf"]
    idf = io["c_ident_f32"]
    idb = io["c_ident_bf"]
    ccd = io["c_consts"]

    with (
        nc.sbuf_tensor("s1_cc", [128, 8], F32) as cc,
        nc.sbuf_tensor("s1_xin", [128, 2, C], F32) as xin,
        nc.sbuf_tensor("s1_xn", [128, 2, C], F32) as xn,
        nc.sbuf_tensor("s1_xnb", [128, 2, C], BF16) as xnb,
        nc.sbuf_tensor("s1_sq", [128, C], F32) as sq,
        nc.sbuf_tensor("s1_cols", [128, 2, 8], F32) as cols,
        nc.sbuf_tensor("s1_scol", [128, 2, 4], F32) as scol,
        nc.sbuf_tensor("s1_otf", [128, 2, 8, 128], F32) as otf,
        nc.sbuf_tensor("s1_otb", [128, 2, 8, 128], BF16) as otb,
        nc.sbuf_tensor("s1_idf", [128, 128], F32) as sidf,
        nc.sbuf_tensor("s1_idb", [128, 128], BF16) as sidb,
        nc.psum_tensor("s1_ptf", [128, 2, 2, 512], F32) as ptf,
        nc.psum_tensor("s1_ptb", [128, 2, 1024], BF16) as ptb,
    ):
        ld = nc.alloc_semaphore(name="s1_ld")
        vm = nc.alloc_semaphore(name="s1_vm")
        am = nc.alloc_semaphore(name="s1_am")
        tm = nc.alloc_semaphore(name="s1_tm")
        st = nc.alloc_semaphore(name="s1_st")
        blkctx = nc.Block()
        block = blkctx.__enter__()
        def vm_v(i, k):
            return 4 * i + k

        def am_v(i, k):
            return 2 * i + k

        @block.sync
        def _(sync):
            sync.dma_start(out=sidf[:], in_=idf[:, :]).then_inc(ld, 16)
            sync.dma_start(out=sidb[:], in_=idb[:, :]).then_inc(ld, 16)
            sync.dma_start(out=cc[:], in_=ccd[:, :]).then_inc(ld, 16)
            for i in range(NT):
                if i >= 2:
                    sync.wait_ge(vm, vm_v(i - 2, 3))
                sync.dma_start(out=xin[:, i % 2, :],
                               in_=x[i * 128:(i + 1) * 128, :]).then_inc(ld, 16)

        @block.scalar
        def _(scalar):
            for i in range(NT):
                p = i % 2
                scalar.wait_ge(ld, 48 + 16 * (i + 1))
                if i >= 1:
                    scalar.wait_ge(vm, vm_v(i - 1, 2))
                scalar.activation(sq[:, :], xin[:, p, :], AF.Square,
                                  accum_out=scol[:, p, 0:1])
                scalar.drain().then_inc(am, 1)
                scalar.wait_ge(vm, vm_v(i, 2))
                scalar.activation(scol[:, p, 2:3], cols[:, p, 5:6], AF.Ln, bias=cc[:, 0:1])
                scalar.drain()
                scalar.activation(scol[:, p, 1:2], scol[:, p, 2:3], AF.Exp, scale=-0.5)
                scalar.drain().then_inc(am, 1)

        @block.vector
        def _(vector):
            for i in range(NT):
                p = i % 2
                vector.wait_ge(ld, 48 + 16 * (i + 1))
                xi = xin[:, p, :]
                vector.reduce_sum(cols[:, p, 0:1], xi, AX.X).then_inc(vm, 1)
                vector.wait_ge(am, am_v(i, 1))
                vector.drain()
                vector.tensor_scalar(cols[:, p, 2:3], cols[:, p, 0:1], 1.0 / C, None, ALU.mult)
                vector.drain()
                vector.tensor_tensor(cols[:, p, 4:5], cols[:, p, 0:1], cols[:, p, 2:3], ALU.mult)
                vector.drain()
                vector.tensor_tensor(cols[:, p, 3:4], scol[:, p, 0:1], cols[:, p, 4:5], ALU.subtract)
                vector.drain()
                vector.tensor_scalar(cols[:, p, 5:6], cols[:, p, 3:4], 1.0 / C, None,
                                     ALU.mult).then_inc(vm, 1)
                vector.wait_ge(am, am_v(i, 2))
                vector.drain()
                vector.tensor_scalar(xn[:, p, :], xi, cols[:, p, 2:3], scol[:, p, 1:2],
                                     ALU.subtract, ALU.mult)
                vector.drain()
                vector.tensor_copy(xnb[:, p, :], xn[:, p, :]).then_inc(vm, 1)
                vector.wait_ge(tm, i + 1)
                if i >= 2:
                    vector.wait_ge(st, 32 * (i - 1))
                vector.tensor_copy(otf[:, p, 0:4, :].rearrange("p a b -> p (a b)"),
                                   ptf[:, p, 0, :])
                vector.tensor_copy(otf[:, p, 4:8, :].rearrange("p a b -> p (a b)"),
                                   ptf[:, p, 1, :])
                vector.tensor_copy(otb[:, p, :, :].rearrange("p a b -> p (a b)"),
                                   ptb[:, p, :]).then_inc(vm, 1)

        @block.tensor
        def _(tensor):
            tensor.wait_ge(ld, 48)
            for i in range(NT):
                p = i % 2
                tensor.wait_ge(vm, vm_v(i, 3))
                if i >= 2:
                    tensor.wait_ge(vm, vm_v(i - 2, 4))
                for j in range(8):
                    tensor.transpose(ptf[:, p, j // 4, (j % 4) * 128:(j % 4 + 1) * 128],
                                     xn[:, p, j * 128:(j + 1) * 128], sidf[:])
                mm = None
                for j in range(8):
                    mm = tensor.transpose(ptb[:, p, j * 128:(j + 1) * 128],
                                          xnb[:, p, j * 128:(j + 1) * 128], sidb[:])
                mm.then_inc(tm, 1)

        @block.gpsimd
        def _(gpsimd):
            for i in range(NT):
                p = i % 2
                gpsimd.wait_ge(vm, vm_v(i, 4))
                gpsimd.dma_start(
                    out=xnT.rearrange("(fc f) t -> f fc t", f=128)[:, :, i * 128:(i + 1) * 128],
                    in_=otf[:, p, :, :]).then_inc(st, 16)
                gpsimd.dma_start(
                    out=xnTb.rearrange("(fc f) t -> f fc t", f=128)[:, :, i * 128:(i + 1) * 128],
                    in_=otb[:, p, :, :]).then_inc(st, 16)
            gpsimd.wait_ge(st, 32 * NT)
        blkctx.__exit__(None, None, None)
    return nc



def build_s2(nc, io):
    xnTb = io["xnT_bf"]
    f0 = io["f0"]
    gate_wt = io["s2_gate_wt"]
    gate_bias = io["s2_gate_bias"]
    w1t = io["s2_w1t"]
    w3t = io["s2_w3t"]
    w2t = io["s2_w2t"]
    NE = 9
    NB = 16

    with ExitStack() as _es:
        ec = _es.enter_context
        xg = ec(nc.sbuf_tensor("s2_xg", [128, 2, 8, 128], BF16))
        gw = ec(nc.sbuf_tensor("s2_gw", [128, 8, 8], BF16))
        gb = ec(nc.sbuf_tensor("s2_gb", [128, 8], F32))
        gsc = ec(nc.sbuf_tensor("s2_gsc", [128, NT, 48], F32))
        comb = ec(nc.sbuf_tensor("s2_comb", [128, NT, 12], F32))
        xT = ec(nc.sbuf_tensor("s2_xT", [128, 2, 8, 512], BF16))
        w13 = ec(nc.sbuf_tensor("s2_w13", [128, 2, 8, 16, 128], BF16))
        w2s = ec(nc.sbuf_tensor("s2_w2", [128, 16, 1024], BF16))
        s1 = ec(nc.sbuf_tensor("s2_s1", [128, 2, 512], F32))
        hT = ec(nc.sbuf_tensor("s2_hT", [128, 16, 512], BF16))
        ft = ec(nc.sbuf_tensor("s2_ft", [128, 2, 4, 1024], F32))
        ph = ec(nc.psum_tensor("s2_ph", [128, 4, 512], F32))
        py = ec(nc.psum_tensor("s2_py", [128, 2, 512], F32))
        pg = ec(nc.psum_tensor("s2_pg", [128, 2, 512], F32))
        lgp = [nc.alloc_semaphore(name=f"s2_lg{p}") for p in range(2)]
        ltp = [nc.alloc_semaphore(name=f"s2_lt{p}") for p in range(2)]
        ldc = nc.alloc_semaphore(name="s2_ldc")
        lw = nc.alloc_semaphore(name="s2_lw")
        tm = nc.alloc_semaphore(name="s2_tm")
        vm = nc.alloc_semaphore(name="s2_vm")
        am = nc.alloc_semaphore(name="s2_am")
        st = nc.alloc_semaphore(name="s2_st")
        stb = [nc.alloc_semaphore(name=f"s2_stb{b}") for b in range(NB)]
        block = ec(nc.Block())
        TM0, VM0, AM0 = NT, 2 * NT, NT

        def tm_h(e, b, ic):
            return TM0 + e * 384 + b * 24 + ic + 1

        def tm_w2(e, b, j):
            return TM0 + e * 384 + b * 24 + 16 + j + 1

        def vm_hmult(gh):
            eb, ic = divmod(gh, 16)
            return VM0 + eb * 24 + ic + 1

        def vm_evac(gy):
            eb, j = divmod(gy, 8)
            return VM0 + eb * 24 + 16 + j + 1

        def am_silu(gh):
            return AM0 + gh + 1

        @block.sync
        def _(sync):
            sync.dma_start(out=gb[:], in_=gate_bias[:, :]).then_inc(ldc, 16)
            for i in range(NT):
                if i >= 2:
                    sync.wait_ge(tm, i - 1)
                sync.dma_start(out=xg[:, i % 2, :, :],
                               in_=xnTb.rearrange("(fc f) t -> f fc t", f=128)
                               [:, :, i * 128:(i + 1) * 128]).then_inc(lgp[i % 2], 16)
            for e in range(NE):
                for b in range(NB):
                    g = e * NB + b
                    if g >= 2:
                        sync.wait_ge(tm, TM0 + (g - 1) * 24)
                    sync.dma_start(out=xT[:, g % 2, :, :],
                                   in_=xnTb.rearrange("(fc f) t -> f fc t", f=128)
                                   [:, :, b * 512:(b + 1) * 512]).then_inc(ltp[g % 2], 16)

        @block.gpsimd
        def _(gpsimd):
            gpsimd.memset(comb[:, :, 8:9], 1.0)
            gpsimd.dma_start(out=gw[:],
                             in_=gate_wt.rearrange("(fc f) e -> f fc e", f=128)).then_inc(lw, 16)
            for e in range(NE):
                if e >= 1:
                    gpsimd.wait_ge(tm, TM0 + e * 384)
                gpsimd.dma_start(out=w13[:, 0, :, :, :],
                                 in_=w1t[e].rearrange("(fc f) (ic i) -> f fc ic i",
                                                      f=128, i=128)).then_inc(lw, 16)
                gpsimd.dma_start(out=w13[:, 1, :, :, :],
                                 in_=w3t[e].rearrange("(fc f) (ic i) -> f fc ic i",
                                                      f=128, i=128)).then_inc(lw, 16)
                gpsimd.dma_start(out=w2s[:, :, :],
                                 in_=w2t[e].rearrange("(ic i) o -> i ic o",
                                                      i=128)).then_inc(lw, 16)
                for b in range(NB):
                    gst = e * NB + b
                    gpsimd.wait_ge(vm, vm_evac(gst * 8 + 7))
                    if e >= 1:
                        gpsimd.wait_ge(stb[b], 16 * e)
                    dst = f0.rearrange("(bb p) c -> p bb c", p=128)[:, 4 * b:4 * b + 4, :]
                    if e == 0:
                        gpsimd.dma_start(out=dst, in_=ft[:, b % 2, :, :]).then_inc(stb[b], 16)
                    else:
                        gpsimd.dma_start(out=dst, in_=ft[:, b % 2, :, :],
                                         accum_op=ALU.add).then_inc(stb[b], 16)
            for b in range(NB):
                gpsimd.wait_ge(stb[b], 16 * NE)

        @block.tensor
        def _(tensor):
            tensor.wait_ge(ldc, 16)
            tensor.wait_ge(lw, 16)
            for i in range(NT):
                tensor.wait_ge(lgp[i % 2], 16 * (i // 2 + 1))
                if i >= 2:
                    tensor.wait_ge(vm, 2 * (i - 2) + 1)
                mm = None
                for k in range(8):
                    mm = tensor.matmul(pg[:, i % 2, 0:8], xg[:, i % 2, k, :], gw[:, k, :],
                                       start=(k == 0), stop=(k == 7))
                mm.then_inc(tm, 1)
            for e in range(NE):
                tensor.wait_ge(lw, 16 + 48 * (e + 1))
                for b in range(NB):
                    g = e * NB + b
                    p = g % 2
                    tensor.wait_ge(ltp[p], 16 * (g // 2 + 1))
                    for ic in range(16):
                        gh = g * 16 + ic
                        bk = 2 * (gh % 2)
                        if gh >= 2:
                            tensor.wait_ge(am, am_silu(gh - 2))
                            tensor.wait_ge(vm, vm_hmult(gh - 2))
                        for fc in range(8):
                            tensor.matmul(ph[:, bk, :], w13[:, 0, fc, ic, :],
                                          xT[:, p, fc, :], start=(fc == 0), stop=(fc == 7))
                        mm = None
                        for fc in range(8):
                            mm = tensor.matmul(ph[:, bk + 1, :], w13[:, 1, fc, ic, :],
                                               xT[:, p, fc, :], start=(fc == 0), stop=(fc == 7))
                        mm.then_inc(tm, 1)
                    tensor.wait_ge(vm, vm_hmult(g * 16 + 15))
                    for j in range(8):
                        s, oc = divmod(j, 2)
                        gy = g * 8 + j
                        if gy >= 2:
                            tensor.wait_ge(vm, vm_evac(gy - 2))
                        mm = None
                        for ic in range(16):
                            mm = tensor.matmul(py[:, gy % 2, :],
                                               hT[:, ic, s * 128:(s + 1) * 128],
                                               w2s[:, ic, oc * 512:(oc + 1) * 512],
                                               start=(ic == 0), stop=(ic == 15))
                        mm.then_inc(tm, 1)

        @block.scalar
        def _(scalar):
            for i in range(NT):
                scalar.wait_ge(vm, 2 * i + 1)
                scalar.activation(gsc[:, i, 16:24], gsc[:, i, 1:9], AF.Exp)
                scalar.drain().then_inc(am, 1)
            for e in range(NE):
                for b in range(NB):
                    g = e * NB + b
                    for ic in range(16):
                        gh = g * 16 + ic
                        bk = 2 * (gh % 2)
                        scalar.wait_ge(tm, tm_h(e, b, ic))
                        if gh >= 2:
                            scalar.wait_ge(vm, vm_hmult(gh - 2))
                        scalar.activation(s1[:, gh % 2, :], ph[:, bk, :], AF.Silu)
                        scalar.drain().then_inc(am, 1)

        @block.vector
        def _(vector):
            for i in range(NT):
                vector.wait_ge(tm, i + 1)
                pgi = pg[:, i % 2, 0:8]
                vector.reduce_max(gsc[:, i, 0:1], pgi, AX.X)
                vector.drain()
                vector.tensor_scalar(gsc[:, i, 1:9], pgi, gsc[:, i, 0:1], None,
                                     ALU.subtract).then_inc(vm, 1)
                vector.wait_ge(am, i + 1)
                exe = gsc[:, i, 16:24]
                vector.reduce_sum(gsc[:, i, 9:10], exe, AX.X)
                vector.drain()
                vector.reciprocal(gsc[:, i, 10:11], gsc[:, i, 9:10])
                vector.drain()
                vector.tensor_scalar(gsc[:, i, 24:32], exe, gsc[:, i, 10:11], None, ALU.mult)
                vector.drain()
                vector.tensor_tensor(gsc[:, i, 32:40], gsc[:, i, 24:32], gb[:, :], ALU.add)
                vector.drain()
                vector.reduce_max(gsc[:, i, 11:12], gsc[:, i, 32:40], AX.X)
                vector.drain()
                vector.tensor_scalar(gsc[:, i, 1:9], gsc[:, i, 32:40], gsc[:, i, 11:12], None, ALU.is_ge)
                vector.drain()
                vector.scalar_tensor_tensor(gsc[:, i, 16:24], gsc[:, i, 1:9], -1e30,
                                            gsc[:, i, 32:40], ALU.mult, ALU.add)
                vector.drain()
                vector.reduce_max(gsc[:, i, 12:13], gsc[:, i, 16:24], AX.X)
                vector.drain()
                vector.tensor_scalar(gsc[:, i, 32:40], gsc[:, i, 16:24], gsc[:, i, 12:13], None, ALU.is_ge)
                vector.drain()
                vector.tensor_tensor(gsc[:, i, 1:9], gsc[:, i, 1:9], gsc[:, i, 32:40], ALU.add)
                vector.drain()
                vector.tensor_tensor(comb[:, i, 0:8], gsc[:, i, 24:32], gsc[:, i, 1:9],
                                     ALU.mult).then_inc(vm, 1)
            for e in range(NE):
                for b in range(NB):
                    g = e * NB + b
                    for ic in range(16):
                        gh = g * 16 + ic
                        bk = 2 * (gh % 2)
                        vector.wait_ge(am, am_silu(gh))
                        vector.tensor_tensor(hT[:, ic, :], s1[:, gh % 2, :],
                                             ph[:, bk + 1, :], ALU.mult).then_inc(vm, 1)
                    for j in range(8):
                        s, oc = divmod(j, 2)
                        gy = g * 8 + j
                        vector.wait_ge(tm, tm_w2(e, b, j))
                        if j == 0 and g >= 2:
                            bp, ep = (b - 2, e) if b >= 2 else (b + 14, e - 1)
                            vector.wait_ge(stb[bp], 16 * (ep + 1))
                        vector.tensor_scalar(ft[:, b % 2, s, oc * 512:(oc + 1) * 512],
                                             py[:, gy % 2, :],
                                             comb[:, 4 * b + s, e:e + 1], None,
                                             ALU.mult).then_inc(vm, 1)
    return nc



_ENG_OF = {"sy": "sync", "sc": "scalar", "scd": "scalar", "ve": "vector",
           "te": "tensor", "gp": "gpsimd", "gpd": "gpsimd"}
_IS_DMA = {"sy": True, "sc": False, "scd": True, "ve": False, "te": False,
           "gp": False, "gpd": True}


_SCHED_N = [0]


class Sched:
    def __init__(self):
        _SCHED_N[0] += 1
        self.uid = _SCHED_N[0]
        self.steps = []
        self.seq = []

    def step(self, eng, fn, deps=(), no_drain=False):
        sid = len(self.steps)
        self.steps.append((eng, fn, [d for d in deps if d is not None], no_drain))
        return sid

    def emit(self, nc, block):
        cnt = {}
        for eng, _, _, _ in self.steps:
            cnt[eng] = cnt.get(eng, 0) + 1
            self.seq.append(cnt[eng])
        sems = {eng: nc.alloc_semaphore(name=f"sch_{eng}_{self.uid}")
                for eng in cnt}
        phys = sorted({_ENG_OF[e] for e in cnt})

        def section(qname):
            def body(q):
                for sid, (eng, fn, deps, no_drain) in enumerate(self.steps):
                    if _ENG_OF[eng] != qname:
                        continue
                    need_drain = False
                    waits = {}
                    for d in deps:
                        peng = self.steps[d][0]
                        pseq = self.seq[d]
                        if peng == eng and not _IS_DMA[eng] and not _IS_DMA[peng]:
                            if not (no_drain or eng == "te"):
                                need_drain = True
                            continue
                        if _ENG_OF[peng] == qname and not _IS_DMA[peng] and _IS_DMA[eng]:
                            pass
                        v = pseq * (16 if _IS_DMA[peng] else 1)
                        waits[peng] = max(waits.get(peng, 0), v)
                    if _IS_DMA[eng] and self.seq[sid] > 1:
                        waits[eng] = max(waits.get(eng, 0), (self.seq[sid] - 1) * 16)
                    for peng, v in waits.items():
                        if peng == eng and not _IS_DMA[eng]:
                            continue
                        q.wait_ge(sems[peng], v)
                    if need_drain:
                        q.drain()
                    inst = fn(q)
                    assert inst is not None, f"step {sid} on {eng} returned no instruction"
                    inst.then_inc(sems[eng], 16 if _IS_DMA[eng] else 1)
            return body

        for qname in phys:
            getattr(block, qname)(section(qname))

        def final(q):
            for eng in sorted(cnt):
                if _IS_DMA[eng]:
                    q.wait_ge(sems[eng], cnt[eng] * 16)
        block.sync(final)
        return sems, cnt


class Res:
    __slots__ = ("w", "r")

    def __init__(self):
        self.w = None
        self.r = []


def mk_S(sch):
    def S(eng, fn, reads=(), writes=(), extra=(), no_drain=False):
        deps = list(extra)
        for r in reads:
            if r.w is not None:
                deps.append(r.w)
        for w in writes:
            deps.extend(w.r)
            if w.w is not None:
                deps.append(w.w)
        sid = sch.step(eng, fn, deps, no_drain=no_drain)
        for r in reads:
            r.r.append(sid)
        for w in writes:
            w.w = sid
            w.r = []
        return sid
    return S



def build_s3(nc, io):
    xnTb = io["xnT_bf"]
    NB = 16
    o_w, o_r, o_k, o_kk = io["w_d"], io["r_d"], io["k_d"], io["kk_d"]
    o_v, o_b, o_g = io["v_d"], io["b_d"], io["g_d"]
    o_bs, o_wlm, o_mcs = io["bsum_d"], io["wlm_d"], io["mcs_d"]
    G_ORDER = [0, 3, 5, 1, 4, 2]

    with ExitStack() as _es:
        ec = _es.enter_context
        sb = lambda *a: ec(nc.sbuf_tensor(*a))
        cc = sb("s3_cc", [128, 8], F32)
        xa = sb("s3s_xa", [128, 8, 192], BF16)
        xb = sb("s3s_xb", [128, 2, 1024], BF16)
        lam = sb("s3s_lam", [128, 48], F32)
        xmiu = sb("s3s_xmiu", [128, 8], F32)
        wa = sb("s3s_wa", [128, 8, 64], BF16)
        ka = sb("s3s_ka", [128, 8, 64], BF16)
        aa = sb("s3s_aa", [128, 8, 64], BF16)
        ga = sb("s3s_ga", [128, 8, 128], BF16)
        wbb = sb("s3s_wb", [64, 1024], BF16)
        kbb = sb("s3s_kb", [64, 1024], BF16)
        abb = sb("s3s_ab", [64, 1024], BF16)
        gbb = sb("s3s_gb", [128, 1024], BF16)
        wmiu = sb("s3s_wmiu", [128, 1024], F32)
        atim = sb("s3s_atim", [128, 1024], F32)
        xtim = sb("s3s_xtim", [128, 1024], F32)
        rwt = sb("s3s_rwt", [128, 8, 1024], BF16)
        kwt = sb("s3s_kwt", [128, 8, 1024], BF16)
        vwt = sb("s3s_vwt", [128, 8, 1024], BF16)
        c3 = sb("s3s_c3", [128, 4], F32)
        win = sb("s3_win", [128, 8, 513], BF16)
        dxt = sb("s3_dx", [128, 8, 512], BF16)
        xmt = sb("s3_xm", [128, 8, 512], BF16)
        mixT = sb("s3_mix", [128, 2, 512], BF16)
        xd = sb("s3_xd", [128, 8, 512], BF16)
        mt = sb("s3_mt", [128, 4, 512], F32)
        lr = sb("s3_lr", [128, 2, 512], BF16)
        rstg = sb("s3_rstg", [128, 4, 1024], BF16)
        kstg = sb("s3_kstg", [128, 4, 1024], BF16)
        wstg = sb("s3_wstg", [128, 4, 1024], F32)
        sig = sb("s3_sig", [128, 4, 1024], BF16)
        vstg = sb("s3_vstg", [128, 1024], BF16)
        gstg = sb("s3_gstg", [128, 1024], BF16)
        kkstg = sb("s3_kkstg", [128, 1024], BF16)
        bstg = sb("s3_bstg", [128, 1024], BF16)
        cs = sb("s3_cs", [128, 2, 1024], F32)
        cs2 = sb("s3_cs2", [128, 1024], F32)
        sqd = sb("s3_sqd", [128, 1024], F32)
        kcol = sb("s3_kcol", [128, 2, 8], F32)
        bscol = sb("s3_bs", [128, 2, 16], F32)
        auxs = sb("s3_aux", [4, 2, 1024], F32)
        zrow = sb("s3_zrow", [1, 1024], F32)
        pm = ec(nc.psum_tensor("s3_pm", [128, 2, 512], F32))
        pxd = ec(nc.psum_tensor("s3_pxd", [128, 4, 512], F32))
        ppj = ec(nc.psum_tensor("s3_ppj", [128, 2, 512], F32))

        sch = Sched()
        S = mk_S(sch)
        r_win, r_dx, r_xm, r_xd, r_sqd = Res(), Res(), Res(), Res(), Res()
        r_mix = [Res(), Res()]
        r_pm = [Res(), Res()]
        r_pxd = [Res() for _ in range(4)]
        r_ppj = [Res(), Res()]
        r_mt = [Res() for _ in range(4)]
        r_lr = [Res() for _ in range(4)]
        r_rstg, r_kstg, r_wstg, r_sig = Res(), Res(), Res(), Res()
        r_vstg, r_gstg, r_kkstg, r_bstg = Res(), Res(), Res(), Res()
        r_cs = [Res(), Res()]
        r_cs2 = [Res(), Res()]
        r_kcol = [Res(), Res()]
        r_bscol = [Res(), Res()]
        r_aux = [Res(), Res()]
        r_cst = Res()

        def ld(q, dst, src):
            return q.dma_start(out=dst, in_=src)

        S("gpd", lambda q: ld(q, xa[:], io["s3_xA"].rearrange("(fc f) j -> f fc j", f=128)),
          writes=[r_cst])
        for g in range(6):
            S("gpd", lambda q, g=g: ld(q, xb[32 * (g % 3):32 * (g % 3) + 32, g // 3, :],
                                       io["s3_xB"][g]), writes=[r_cst])
        for nm, dst in [("s3_wA", wa), ("s3_kA", ka), ("s3_aA", aa)]:
            S("gpd", lambda q, nm=nm, dst=dst: ld(
                q, dst[:], io[nm].rearrange("(fc f) j -> f fc j", f=128)), writes=[r_cst])
        S("gpd", lambda q: ld(q, ga[:], io["s3_gA"].rearrange("(fc f) j -> f fc j", f=128)),
          writes=[r_cst])
        for nm, dst in [("s3_wB", wbb), ("s3_kB", kbb), ("s3_aB", abb)]:
            S("gpd", lambda q, nm=nm, dst=dst: ld(q, dst[:], io[nm][:, :]), writes=[r_cst])
        S("gpd", lambda q: ld(q, gbb[:], io["s3_gB"][:, :]), writes=[r_cst])
        for nm, dst in [("s3_rWt", rwt), ("s3_kWt", kwt), ("s3_vWt", vwt)]:
            S("gpd", lambda q, nm=nm, dst=dst: ld(
                q, dst[:], io[nm].rearrange("(fc f) o -> f fc o", f=128)), writes=[r_cst])
        for nm, dst in [("s3_lam", lam), ("s3_xmiu", xmiu), ("s3_wmiu_t", wmiu),
                        ("s3_atime_t", atim), ("s3_xtime_t", xtim), ("s3_c3aux", c3),
                        ("c_consts", cc)]:
            S("sy", lambda q, nm=nm, dst=dst: ld(q, dst[:], io[nm][:, :]), writes=[r_cst])

        sz = S("ve", lambda q: q.memset(zrow[:], 0.0))
        S("scd", lambda q: ld(q, o_mcs[NCH - 1:NCH, :], zrow[:]), extra=[sz])
        S("ve", lambda q: q.memset(win[:, :, 0:1], 0.0), writes=[r_win])

        xnTr = xnTb.rearrange("(fc f) t -> f fc t", f=128)

        for b in range(NB):
            t0 = b * 512
            if b == 0:
                S("sy", lambda q: ld(q, win[:, :, 1:513], xnTr[:, :, 0:512]),
                  reads=[r_cst], writes=[r_win])
            else:
                S("sy", lambda q, t0=t0: ld(q, win[:, :, 0:513],
                                            xnTr[:, :, t0 - 1:t0 + 512]),
                  writes=[r_win])
            S("ve", lambda q: q.tensor_tensor(dxt[:, :, :], win[:, :, 0:512],
                                              win[:, :, 1:513], ALU.subtract),
              reads=[r_win], writes=[r_dx])

            def xm_fn(q):
                ii = None
                for fc in range(8):
                    ii = q.scalar_tensor_tensor(xmt[:, fc, :], dxt[:, fc, :],
                                                xmiu[:, fc:fc + 1], win[:, fc, 1:513],
                                                ALU.mult, ALU.add)
                return ii
            S("ve", xm_fn, reads=[r_dx, r_win, r_cst], writes=[r_xm])

            for cg in range(2):
                def mix_mm(q, cg=cg):
                    ii = None
                    for gi in range(3):
                        g = cg * 3 + gi
                        for fc in range(8):
                            ii = q.matmul(pm[32 * gi:32 * gi + 32, cg, :],
                                          xa[:, fc, 32 * g:32 * g + 32],
                                          xmt[:, fc, :],
                                          start=(fc == 0), stop=(fc == 7))
                    return ii
                S("te", mix_mm, reads=[r_xm, r_cst], writes=[r_pm[cg]])

                def mix_tanh(q, cg=cg):
                    ii = None
                    for gi in range(3):
                        ii = q.activation(mixT[32 * gi:32 * gi + 32, cg, :],
                                          pm[32 * gi:32 * gi + 32, cg, :], AF.Tanh)
                    return ii
                S("sc", mix_tanh, reads=[r_pm[cg]], writes=[r_mix[cg]])

            for g in G_ORDER:
                gi, cg = g % 3, g // 3
                for wv in range(2):
                    fcs = tuple(range(4 * wv, 4 * wv + 4))

                    def miu_mm(q, gi=gi, cg=cg, fcs=fcs):
                        ii = None
                        for fc in fcs:
                            ii = q.matmul(pxd[:, fc % 4, :],
                                          xb[32 * gi:32 * gi + 32, cg,
                                             fc * 128:(fc + 1) * 128],
                                          mixT[32 * gi:32 * gi + 32, cg, :],
                                          start=True, stop=True)
                        return ii
                    S("te", miu_mm, reads=[r_mix[cg], r_cst],
                      writes=[r_pxd[fc % 4] for fc in fcs])

                    def miu_stt(q, g=g, fcs=fcs):
                        ii = None
                        for fc in fcs:
                            ii = q.scalar_tensor_tensor(mt[:, fc % 4, :], pxd[:, fc % 4, :],
                                                        lam[:, g * 8 + fc:g * 8 + fc + 1],
                                                        dxt[:, fc, :], ALU.add, ALU.mult)
                        return ii
                    S("ve", miu_stt, reads=[r_pxd[fc % 4] for fc in fcs] + [r_dx, r_cst],
                      writes=[r_mt[fc % 4] for fc in fcs])

                    def xd_add(q, fcs=fcs):
                        ii = None
                        for fc in fcs:
                            ii = q.tensor_tensor(xd[:, fc, :], mt[:, fc % 4, :],
                                                 win[:, fc, 1:513], ALU.add)
                        return ii
                    S("ve", xd_add, reads=[r_mt[fc % 4] for fc in fcs] + [r_win],
                      writes=[r_xd])

                if g in (0, 3):
                    wt = rwt if g == 0 else vwt
                    for s in range(4):
                        def proj_mm(q, wt=wt, s=s):
                            ii = None
                            for oc in range(2):
                                for fc in range(8):
                                    ii = q.matmul(ppj[:, oc, :],
                                                  xd[:, fc, s * 128:(s + 1) * 128],
                                                  wt[:, fc, oc * 512:(oc + 1) * 512],
                                                  start=(fc == 0), stop=(fc == 7))
                            return ii
                        S("te", proj_mm, reads=[r_xd, r_cst], writes=[r_ppj[0], r_ppj[1]])
                        if g == 0:
                            def ev_r(q, s=s):
                                q.tensor_copy(rstg[:, s, 0:512], ppj[:, 0, :])
                                return q.tensor_copy(rstg[:, s, 512:1024], ppj[:, 1, :])
                            S("ve", ev_r, reads=[r_ppj[0], r_ppj[1]], writes=[r_rstg])
                            S("gpd", lambda q, t0=t0, s=s: ld(
                                q, o_r[t0 + s * 128:t0 + (s + 1) * 128, :], rstg[:, s, :]),
                                reads=[r_rstg])
                        else:
                            def ev_v(q):
                                q.tensor_copy(vstg[:, 0:512], ppj[:, 0, :])
                                return q.tensor_copy(vstg[:, 512:1024], ppj[:, 1, :])
                            S("ve", ev_v, reads=[r_ppj[0], r_ppj[1]], writes=[r_vstg])
                            S("scd", lambda q, t0=t0, s=s: ld(
                                q, o_v[t0 + s * 128:t0 + (s + 1) * 128, :], vstg[:]),
                                reads=[r_vstg])
                elif g == 5:
                    def g1_mm(q):
                        ii = None
                        for fc in range(8):
                            ii = q.matmul(pxd[:, 0, :], ga[:, fc, :], xd[:, fc, :],
                                          start=(fc == 0), stop=(fc == 7))
                        return ii
                    S("te", g1_mm, reads=[r_xd, r_cst], writes=[r_pxd[0]])
                    S("sc", lambda q: q.activation(lr[:, 0, :], pxd[:, 0, :], AF.Tanh),
                      reads=[r_pxd[0]], writes=[r_lr[0]])
                    for s in range(4):
                        def g2_mm(q, s=s):
                            ii = None
                            for oc in range(2):
                                ii = q.matmul(ppj[:, oc, :],
                                              lr[:, 0, s * 128:(s + 1) * 128],
                                              gbb[:, oc * 512:(oc + 1) * 512],
                                              start=True, stop=True)
                            return ii
                        S("te", g2_mm, reads=[r_lr[0], r_cst], writes=[r_ppj[0], r_ppj[1]])

                        def ev_g(q):
                            q.tensor_copy(gstg[:, 0:512], ppj[:, 0, :])
                            return q.tensor_copy(gstg[:, 512:1024], ppj[:, 1, :])
                        S("ve", ev_g, reads=[r_ppj[0], r_ppj[1]], writes=[r_gstg])
                        S("scd", lambda q, t0=t0, s=s: ld(
                            q, o_g[t0 + s * 128:t0 + (s + 1) * 128, :], gstg[:]),
                            reads=[r_gstg])
                elif g == 1:
                    def w1_mm(q):
                        ii = None
                        for fc in range(8):
                            ii = q.matmul(pxd[0:64, 1, :], wa[:, fc, :], xd[:, fc, :],
                                          start=(fc == 0), stop=(fc == 7))
                        return ii
                    S("te", w1_mm, reads=[r_xd, r_cst], writes=[r_pxd[1]])
                    S("sc", lambda q: q.activation(lr[0:64, 1, :], pxd[0:64, 1, :], AF.Tanh),
                      reads=[r_pxd[1]], writes=[r_lr[1]])
                    for s in range(4):
                        sp2 = s % 2

                        def w2_mm(q, s=s):
                            ii = None
                            for oc in range(2):
                                ii = q.matmul(ppj[:, oc, :],
                                              lr[0:64, 1, s * 128:(s + 1) * 128],
                                              wbb[:, oc * 512:(oc + 1) * 512],
                                              start=True, stop=True)
                            return ii
                        S("te", w2_mm, reads=[r_lr[1], r_cst], writes=[r_ppj[0], r_ppj[1]])

                        def w_u(q, sp2=sp2):
                            q.tensor_tensor(cs[:, sp2, 0:512], ppj[:, 0, :],
                                            wmiu[:, 0:512], ALU.add)
                            return q.tensor_tensor(cs[:, sp2, 512:1024], ppj[:, 1, :],
                                                   wmiu[:, 512:1024], ALU.add)
                        S("ve", w_u, reads=[r_ppj[0], r_ppj[1], r_cst], writes=[r_cs[sp2]])
                        S("sc", lambda q, sp2=sp2: q.activation(cs2[:],
                                                                cs[:, sp2, :], AF.Abs),
                          reads=[r_cs[sp2]], writes=[r_cs2[0]])
                        S("sc", lambda q, sp2=sp2: q.activation(sqd[:], cs2[:],
                                                                AF.Exp, scale=-1.0),
                          reads=[r_cs2[0]], writes=[r_sqd])
                        S("sc", lambda q, sp2=sp2: q.activation(cs2[:], sqd[:],
                                                                AF.Ln, bias=cc[:, 2:3]),
                          reads=[r_sqd, r_cst], writes=[r_cs2[0]])
                        S("ve", lambda q, sp2=sp2: q.tensor_scalar(
                            mt[:, 0:2, :].rearrange("p a b -> p (a b)"), cs[:, sp2, :],
                            0.0, None, ALU.min),
                          reads=[r_cs[sp2]], writes=[r_mt[0], r_mt[1]])
                        S("ve", lambda q, s=s, sp2=sp2: q.scalar_tensor_tensor(
                            wstg[:, s, :], mt[:, 0:2, :].rearrange("p a b -> p (a b)"),
                            -0.5, cs2[:], ALU.add, ALU.subtract),
                          reads=[r_mt[0], r_mt[1], r_cs2[0]], writes=[r_wstg])
                        S("scd", lambda q, t0=t0, s=s: ld(
                            q, o_w[t0 + s * 128:t0 + (s + 1) * 128, :], wstg[:, s, :]),
                            reads=[r_wstg])

                        def aux_mm(q, s=s):
                            q.matmul(pxd[0:4, 2, :], c3[:], wstg[:, s, 0:512],
                                     start=True, stop=True)
                            return q.matmul(pxd[0:4, 3, :], c3[:], wstg[:, s, 512:1024],
                                            start=True, stop=True)
                        S("te", aux_mm, reads=[r_wstg, r_cst],
                          writes=[r_pxd[2], r_pxd[3]])

                        def aux_ev(q, sp2=sp2):
                            q.tensor_copy(auxs[:, sp2, 0:512], pxd[0:4, 2, :])
                            return q.tensor_copy(auxs[:, sp2, 512:1024], pxd[0:4, 3, :])
                        S("ve", aux_ev, reads=[r_pxd[2], r_pxd[3]], writes=[r_aux[sp2]])
                        c0 = b * 8 + 2 * s
                        if c0 == 0:
                            S("scd", lambda q, sp2=sp2: ld(
                                q, o_mcs[0:1, :], auxs[1:2, sp2, :]), reads=[r_aux[sp2]])
                        else:
                            S("scd", lambda q, c0=c0, sp2=sp2: ld(
                                q, o_mcs[c0 - 1:c0 + 1, :], auxs[0:2, sp2, :]),
                                reads=[r_aux[sp2]])
                        S("scd", lambda q, c0=c0, sp2=sp2: ld(
                            q, o_wlm[c0:c0 + 2, :], auxs[2:4, sp2, :]),
                            reads=[r_aux[sp2]])
                elif g == 4:
                    def a1_mm(q):
                        ii = None
                        for fc in range(8):
                            ii = q.matmul(pxd[0:64, 1, :], aa[:, fc, :], xd[:, fc, :],
                                          start=(fc == 0), stop=(fc == 7))
                        return ii
                    S("te", a1_mm, reads=[r_xd, r_cst], writes=[r_pxd[1]])
                    S("ve", lambda q: q.tensor_copy(lr[0:64, 1, :], pxd[0:64, 1, :]),
                      reads=[r_pxd[1]], writes=[r_lr[1]])
                    for s in range(4):
                        sp2 = s % 2

                        def a2_mm(q, s=s):
                            ii = None
                            for oc in range(2):
                                ii = q.matmul(ppj[:, oc, :],
                                              lr[0:64, 1, s * 128:(s + 1) * 128],
                                              abb[:, oc * 512:(oc + 1) * 512],
                                              start=True, stop=True)
                            return ii
                        S("te", a2_mm, reads=[r_lr[1], r_cst], writes=[r_ppj[0], r_ppj[1]])

                        def a_pre(q, sp2=sp2):
                            q.tensor_tensor(cs[:, sp2, 0:512], ppj[:, 0, :],
                                            atim[:, 0:512], ALU.add)
                            return q.tensor_tensor(cs[:, sp2, 512:1024], ppj[:, 1, :],
                                                   atim[:, 512:1024], ALU.add)
                        S("ve", a_pre, reads=[r_ppj[0], r_ppj[1], r_cst], writes=[r_cs[sp2]])
                        S("sc", lambda q, s=s, sp2=sp2: q.activation(
                            sig[:, s, :], cs[:, sp2, :], AF.Sigmoid),
                          reads=[r_cs[sp2]], writes=[r_sig])
                else:
                    def k1_mm(q):
                        ii = None
                        for fc in range(8):
                            ii = q.matmul(pxd[0:64, 1, :], ka[:, fc, :], xd[:, fc, :],
                                          start=(fc == 0), stop=(fc == 7))
                        return ii
                    S("te", k1_mm, reads=[r_xd, r_cst], writes=[r_pxd[1]])
                    S("sc", lambda q: q.activation(lr[0:64, 1, :], pxd[0:64, 1, :], AF.Tanh),
                      reads=[r_pxd[1]], writes=[r_lr[1]])
                    for s in range(4):
                        sp2 = s % 2

                        def k_mm(q, s=s):
                            ii = None
                            for oc in range(2):
                                for fc in range(8):
                                    q.matmul(ppj[:, oc, :],
                                             xd[:, fc, s * 128:(s + 1) * 128],
                                             kwt[:, fc, oc * 512:(oc + 1) * 512],
                                             start=(fc == 0), stop=False)
                                ii = q.matmul(ppj[:, oc, :],
                                              lr[0:64, 1, s * 128:(s + 1) * 128],
                                              kbb[:, oc * 512:(oc + 1) * 512],
                                              start=False, stop=True)
                            return ii
                        S("te", k_mm, reads=[r_xd, r_lr[1], r_cst],
                          writes=[r_ppj[0], r_ppj[1]])

                        def k_ev(q, sp2=sp2):
                            q.tensor_copy(cs[:, sp2, 0:512], ppj[:, 0, :])
                            return q.tensor_copy(cs[:, sp2, 512:1024], ppj[:, 1, :])
                        S("ve", k_ev, reads=[r_ppj[0], r_ppj[1]], writes=[r_cs[sp2]])
                        S("sc", lambda q, sp2=sp2: q.activation(
                            sqd[:], cs[:, sp2, :], AF.Square, accum_out=kcol[:, sp2, 0:1]),
                          reads=[r_cs[sp2]], writes=[r_sqd, r_kcol[sp2]])
                        S("sc", lambda q, sp2=sp2: q.activation(
                            kcol[:, sp2, 1:2], kcol[:, sp2, 0:1], AF.Sqrt),
                          reads=[r_kcol[sp2]], writes=[r_kcol[sp2]])
                        S("ve", lambda q, sp2=sp2: q.tensor_scalar(
                            kcol[:, sp2, 2:3], kcol[:, sp2, 1:2], 1e-12, None, ALU.max),
                          reads=[r_kcol[sp2]], writes=[r_kcol[sp2]])
                        S("ve", lambda q, sp2=sp2: q.reciprocal(
                            kcol[:, sp2, 3:4], kcol[:, sp2, 2:3]),
                          reads=[r_kcol[sp2]], writes=[r_kcol[sp2]])
                        S("ve", lambda q, s=s, sp2=sp2: q.tensor_scalar(
                            kstg[:, s, :], cs[:, sp2, :], kcol[:, sp2, 3:4], None,
                            ALU.mult),
                          reads=[r_cs[sp2], r_kcol[sp2]], writes=[r_kstg])
                        S("gpd", lambda q, t0=t0, s=s: ld(
                            q, o_k[t0 + s * 128:t0 + (s + 1) * 128, :], kstg[:, s, :]),
                            reads=[r_kstg])
                        S("ve", lambda q, s=s: q.tensor_scalar(
                            cs2[:], wstg[:, s, :], 0.0, None, ALU.min),
                          reads=[r_wstg], writes=[r_cs2[0]])
                        S("sc", lambda q: q.activation(
                            sqd[:], cs2[:], AF.Exp, scale=0.5),
                          reads=[r_cs2[0]], writes=[r_sqd])
                        S("ve", lambda q, s=s: q.tensor_tensor(
                            kkstg[:], kstg[:, s, :], sqd[:], ALU.mult),
                          reads=[r_kstg, r_sqd], writes=[r_kkstg])
                        S("gpd", lambda q, t0=t0, s=s: ld(
                            q, o_kk[t0 + s * 128:t0 + (s + 1) * 128, :], kkstg[:]),
                            reads=[r_kkstg])
                        S("ve", lambda q, s=s: q.scalar_tensor_tensor(
                            bstg[:], kstg[:, s, :], 2.0, sig[:, s, :], ALU.mult, ALU.mult),
                          reads=[r_kstg, r_sig], writes=[r_bstg])
                        S("gpd", lambda q, t0=t0, s=s: ld(
                            q, o_b[t0 + s * 128:t0 + (s + 1) * 128, :], bstg[:]),
                            reads=[r_bstg])
                        S("ve", lambda q, s=s, sp2=sp2: q.tensor_tensor(
                            cs[:, sp2, :], rstg[:, s, :], kstg[:, s, :], ALU.mult),
                          reads=[r_rstg, r_kstg], writes=[r_cs[sp2]])
                        S("ve", lambda q, sp2=sp2: q.tensor_tensor(
                            cs2[:], cs[:, sp2, :], xtim[:], ALU.mult),
                          reads=[r_cs[sp2], r_cst], writes=[r_cs2[0]])
                        S("ve", lambda q, sp2=sp2: q.tensor_reduce(
                            bscol[:, sp2, :],
                            cs2[:].rearrange("p (h n) -> p h n", h=16),
                            AX.X, ALU.add),
                          reads=[r_cs2[0]], writes=[r_bscol[sp2]])
                        S("scd", lambda q, t0=t0, s=s, sp2=sp2: ld(
                            q, o_bs[t0 + s * 128:t0 + (s + 1) * 128, :], bscol[:, sp2, :]),
                            reads=[r_bscol[sp2]])

        with nc.Block() as block:
            sch.emit(nc, block)
    return nc




def build_s5(nc, io):
    o_x1, o_xn1, o_xn1T = io["x1_d"], io["xn1_d"], io["xn1T_bf"]
    with ExitStack() as _es:
        ec = _es.enter_context
        sb = lambda *a: ec(nc.sbuf_tensor(*a))
        cc = sb("s5_cc", [128, 8], F32)
        idb = sb("s5_idb", [128, 128], BF16)
        owt = sb("s5_owt", [128, 8, 1024], BF16)
        gng = sb("s5_gng", [128, 1024], F32)
        gnb = sb("s5_gnb", [128, 1024], F32)
        yt = sb("s5_y", [128, 2, 1024], F32)
        bst = sb("s5_bs", [128, 2, 16], F32)
        vt = sb("s5_v", [128, 2, 1024], BF16)
        gt = sb("s5_g", [128, 2, 1024], BF16)
        xt = sb("s5_x", [128, 2, 1024], F32)
        f0t = sb("s5_f0", [128, 2, 1024], F32)
        sq = sb("s5_sq", [128, 1024], F32)
        gcol = sb("s5_gcol", [128, 2, 96], F32)
        y2 = sb("s5_y2", [128, 1024], F32)
        zc = sb("s5_zc", [128, 2, 1024], BF16)
        zcT = sb("s5_zcT", [128, 2, 8, 128], BF16)
        x1s = sb("s5_x1", [128, 2, 1024], F32)
        ncol = sb("s5_ncol", [128, 2, 8], F32)
        xn1s = sb("s5_xn1", [128, 2, 1024], F32)
        xn1b = sb("s5_xn1b", [128, 1024], BF16)
        xn1Ts = sb("s5_xn1T", [128, 2, 8, 128], BF16)
        ptr = ec(nc.psum_tensor("s5_ptr", [128, 2, 1024], BF16))
        ppj = ec(nc.psum_tensor("s5_ppj", [128, 2, 512], F32))

        sch = Sched()
        S = mk_S(sch)
        r_cst = Res()
        r_y, r_bs, r_v, r_g, r_x, r_f0 = [[Res(), Res()] for _ in range(6)]
        r_sq, r_y2, r_xn1b = Res(), Res(), Res()
        r_gcol = [Res(), Res()]
        r_zc = [Res(), Res()]
        r_zcT = [Res(), Res()]
        r_x1 = [Res(), Res()]
        r_nc = [Res(), Res()]
        r_xn1 = [Res(), Res()]
        r_xn1T = [Res(), Res()]
        r_ptr = [Res(), Res()]
        r_ppj = [Res(), Res()]

        def ld(q, dst, src):
            return q.dma_start(out=dst, in_=src)

        S("gpd", lambda q: ld(q, owt[:], io["s5_oWt"].rearrange("(fc f) o -> f fc o", f=128)),
          writes=[r_cst])
        S("gpd", lambda q: ld(q, idb[:], io["c_ident_bf"][:, :]), writes=[r_cst])
        for nm, dst in [("s5_gng_t", gng), ("s5_gnb_t", gnb), ("c_consts", cc)]:
            S("sy", lambda q, nm=nm, dst=dst: ld(q, dst[:], io[nm][:, :]), writes=[r_cst])

        for i in range(NT):
            p = i % 2
            t0 = i * 128
            rows = slice(t0, t0 + 128)
            S("sy", lambda q, rows=rows, p=p: ld(q, yt[:, p, :], io["y_d"][rows, :]),
              reads=[r_cst], writes=[r_y[p]])
            S("sy", lambda q, rows=rows, p=p: ld(q, bst[:, p, :], io["bsum_d"][rows, :]),
              writes=[r_bs[p]])
            S("sy", lambda q, rows=rows, p=p: ld(q, vt[:, p, :], io["v_d"][rows, :]),
              writes=[r_v[p]])
            S("sy", lambda q, rows=rows, p=p: ld(q, gt[:, p, :], io["g_d"][rows, :]),
              writes=[r_g[p]])
            S("sy", lambda q, rows=rows, p=p: ld(q, xt[:, p, :], io["x"][rows, :]),
              writes=[r_x[p]])
            S("sy", lambda q, rows=rows, p=p: ld(q, f0t[:, p, :], io["f0"][rows, :]),
              writes=[r_f0[p]])
            S("ve", lambda q, p=p: q.tensor_reduce(
                gcol[:, p, 0:16], yt[:, p, :].rearrange("a (h n) -> a h n", h=16),
                AX.X, ALU.add), reads=[r_y[p]], writes=[r_gcol[p]])
            S("sc", lambda q, p=p: q.activation(sq[:], yt[:, p, :], AF.Square),
              reads=[r_y[p]], writes=[r_sq])
            S("ve", lambda q, p=p: q.tensor_reduce(
                gcol[:, p, 16:32], sq[:].rearrange("a (h n) -> a h n", h=16),
                AX.X, ALU.add), reads=[r_sq], writes=[r_gcol[p]])
            S("ve", lambda q, p=p: q.tensor_scalar(
                gcol[:, p, 32:48], gcol[:, p, 0:16], 1.0 / 64, None, ALU.mult),
              reads=[r_gcol[p]], writes=[r_gcol[p]])
            S("ve", lambda q, p=p: q.tensor_tensor(
                gcol[:, p, 48:64], gcol[:, p, 0:16], gcol[:, p, 32:48], ALU.mult),
              reads=[r_gcol[p]], writes=[r_gcol[p]])
            S("ve", lambda q, p=p: q.tensor_tensor(
                gcol[:, p, 64:80], gcol[:, p, 16:32], gcol[:, p, 48:64], ALU.subtract),
              reads=[r_gcol[p]], writes=[r_gcol[p]])
            S("ve", lambda q, p=p: q.tensor_scalar(
                gcol[:, p, 80:96], gcol[:, p, 64:80], 1.0 / 64, None, ALU.mult),
              reads=[r_gcol[p]], writes=[r_gcol[p]])
            S("sc", lambda q, p=p: q.activation(
                gcol[:, p, 48:64], gcol[:, p, 80:96], AF.Ln, bias=cc[:, 1:2]),
              reads=[r_gcol[p], r_cst], writes=[r_gcol[p]])
            S("sc", lambda q, p=p: q.activation(
                gcol[:, p, 64:80], gcol[:, p, 48:64], AF.Exp, scale=-0.5),
              reads=[r_gcol[p]], writes=[r_gcol[p]])

            def gnorm(q, p=p):
                ii = None
                for h in range(16):
                    sg = slice(h * 64, (h + 1) * 64)
                    ii = q.tensor_scalar(y2[:, sg], yt[:, p, sg],
                                         gcol[:, p, 32 + h:33 + h],
                                         gcol[:, p, 64 + h:65 + h],
                                         ALU.subtract, ALU.mult)
                return ii
            S("ve", gnorm, reads=[r_y[p], r_gcol[p]], writes=[r_y2])
            S("ve", lambda q: q.tensor_tensor(y2[:], y2[:], gng[:], ALU.mult),
              reads=[r_y2, r_cst], writes=[r_y2])
            S("ve", lambda q: q.tensor_tensor(y2[:], y2[:], gnb[:], ALU.add),
              reads=[r_y2], writes=[r_y2])

            def bonus(q, p=p):
                ii = None
                for h in range(16):
                    sg = slice(h * 64, (h + 1) * 64)
                    ii = q.scalar_tensor_tensor(y2[:, sg], vt[:, p, sg],
                                                bst[:, p, h:h + 1], y2[:, sg],
                                                ALU.mult, ALU.add)
                return ii
            S("ve", bonus, reads=[r_v[p], r_bs[p], r_y2], writes=[r_y2])
            S("ve", lambda q, p=p: q.tensor_tensor(zc[:, p, :], y2[:], gt[:, p, :],
                                                   ALU.mult),
              reads=[r_y2, r_g[p]], writes=[r_zc[p]])

            def tr_zc(q, p=p):
                ii = None
                for j in range(8):
                    ii = q.transpose(ptr[:, p, j * 128:(j + 1) * 128],
                                     zc[:, p, j * 128:(j + 1) * 128], idb[:])
                return ii
            S("te", tr_zc, reads=[r_zc[p], r_cst], writes=[r_ptr[p]], no_drain=True)
            S("ve", lambda q, p=p: q.tensor_copy(
                zcT[:, p, :, :].rearrange("a b c -> a (b c)"), ptr[:, p, :]),
              reads=[r_ptr[p]], writes=[r_zcT[p]])

            def ow_mm(q, p=p):
                ii = None
                for oc in range(2):
                    for fc in range(8):
                        ii = q.matmul(ppj[:, oc, :], zcT[:, p, fc, :],
                                      owt[:, fc, oc * 512:(oc + 1) * 512],
                                      start=(fc == 0), stop=(fc == 7))
                return ii
            S("te", ow_mm, reads=[r_zcT[p], r_cst], writes=[r_ppj[0], r_ppj[1]])

            def x1c(q, p=p):
                for oc in range(2):
                    cols = slice(oc * 512, (oc + 1) * 512)
                    q.tensor_tensor(x1s[:, p, cols], ppj[:, oc, :], xt[:, p, cols],
                                    ALU.add)
                ii = None
                for oc in range(2):
                    cols = slice(oc * 512, (oc + 1) * 512)
                    ii = q.tensor_tensor(x1s[:, p, cols], x1s[:, p, cols],
                                         f0t[:, p, cols], ALU.add)
                return ii
            S("ve", x1c, reads=[r_ppj[0], r_ppj[1], r_x[p], r_f0[p]], writes=[r_x1[p]])
            S("scd", lambda q, rows=rows, p=p: ld(q, o_x1[rows, :], x1s[:, p, :]),
              reads=[r_x1[p]])
            S("ve", lambda q, p=p: q.reduce_sum(ncol[:, p, 0:1], x1s[:, p, :], AX.X),
              reads=[r_x1[p]], writes=[r_nc[p]])
            S("sc", lambda q, p=p: q.activation(sq[:], x1s[:, p, :], AF.Square,
                                                accum_out=ncol[:, p, 1:2]),
              reads=[r_x1[p]], writes=[r_sq, r_nc[p]])
            S("ve", lambda q, p=p: q.tensor_scalar(ncol[:, p, 2:3], ncol[:, p, 0:1],
                                                   1.0 / C, None, ALU.mult),
              reads=[r_nc[p]], writes=[r_nc[p]])
            S("ve", lambda q, p=p: q.tensor_tensor(ncol[:, p, 3:4], ncol[:, p, 0:1],
                                                   ncol[:, p, 2:3], ALU.mult),
              reads=[r_nc[p]], writes=[r_nc[p]])
            S("ve", lambda q, p=p: q.tensor_tensor(ncol[:, p, 4:5], ncol[:, p, 1:2],
                                                   ncol[:, p, 3:4], ALU.subtract),
              reads=[r_nc[p]], writes=[r_nc[p]])
            S("ve", lambda q, p=p: q.tensor_scalar(ncol[:, p, 5:6], ncol[:, p, 4:5],
                                                   1.0 / C, None, ALU.mult),
              reads=[r_nc[p]], writes=[r_nc[p]])
            S("sc", lambda q, p=p: q.activation(ncol[:, p, 6:7], ncol[:, p, 5:6],
                                                AF.Ln, bias=cc[:, 0:1]),
              reads=[r_nc[p], r_cst], writes=[r_nc[p]])
            S("sc", lambda q, p=p: q.activation(ncol[:, p, 7:8], ncol[:, p, 6:7],
                                                AF.Exp, scale=-0.5),
              reads=[r_nc[p]], writes=[r_nc[p]])
            S("ve", lambda q, p=p: q.tensor_scalar(xn1s[:, p, :], x1s[:, p, :],
                                                   ncol[:, p, 2:3], ncol[:, p, 7:8],
                                                   ALU.subtract, ALU.mult),
              reads=[r_x1[p], r_nc[p]], writes=[r_xn1[p]])
            S("scd", lambda q, rows=rows, p=p: ld(q, o_xn1[rows, :], xn1s[:, p, :]),
              reads=[r_xn1[p]])
            S("ve", lambda q, p=p: q.tensor_copy(xn1b[:], xn1s[:, p, :]),
              reads=[r_xn1[p]], writes=[r_xn1b])

            def tr_xn1(q, p=p):
                ii = None
                for j in range(8):
                    ii = q.transpose(ptr[:, p, j * 128:(j + 1) * 128],
                                     xn1b[:, j * 128:(j + 1) * 128], idb[:])
                return ii
            S("te", tr_xn1, reads=[r_xn1b, r_cst], writes=[r_ptr[p]], no_drain=True)
            S("ve", lambda q, p=p: q.tensor_copy(
                xn1Ts[:, p, :, :].rearrange("a b c -> a (b c)"), ptr[:, p, :]),
              reads=[r_ptr[p]], writes=[r_xn1T[p]])
            S("gpd", lambda q, t0=t0, p=p: ld(
                q, o_xn1T.rearrange("(fc f) t -> f fc t", f=128)[:, :, t0:t0 + 128],
                xn1Ts[:, p, :, :]), reads=[r_xn1T[p]])

        with nc.Block() as block:
            sch.emit(nc, block)
    return nc




def build_s6(nc, io):
    xnTb = io["xn1T_bf"]
    o_f1 = io["f1_d"]
    NE, NB = 14, 16
    with ExitStack() as _es:
        ec = _es.enter_context
        sb = lambda *a: ec(nc.sbuf_tensor(*a))
        idb = sb("s6_idb", [128, 128], BF16)
        rw = sb("s6s_rw", [128, 8, 14], BF16)
        gs = sb("s6_gs", [128, NT, 48], F32)
        Mall = sb("s6_mall", [128, NT, 16], F32)
        Mb = sb("s6_mb", [128, 16], BF16)
        MT = sb("s6_mt", [14, T], BF16)
        xg = sb("s6_xg", [128, 2, 8, 128], BF16)
        xT = sb("s6_xT", [128, 2, 8, 512], BF16)
        w1s = sb("s6_w1s", [128, 8, 16, 128], BF16)
        w2s = sb("s6_w2s", [128, 16, 1024], BF16)
        b1c = sb("s6s_b1c", [128, 224], F32)
        b2s = sb("s6_b2s", [14, 1024], BF16)
        hT = sb("s6_hT", [128, 16, 512], BF16)
        ft = sb("s6_ft", [128, 2, 4, 1024], F32)
        pg = ec(nc.psum_tensor("s6_pg", [128, 2, 512], F32))
        pmt = ec(nc.psum_tensor("s6_pmt", [128, 1024], BF16))
        ph = ec(nc.psum_tensor("s6_ph", [128, 2, 512], F32))
        py = ec(nc.psum_tensor("s6_py", [128, 2, 512], F32))

        sch = Sched()
        S = mk_S(sch)
        r_cst, r_w1, r_w2 = Res(), Res(), Res()
        r_xg = [Res(), Res()]
        r_xT = [Res(), Res()]
        r_pg = [Res(), Res()]
        r_ph = [Res(), Res()]
        r_py = [Res(), Res()]
        r_pmt, r_mb, r_MT, r_gs, r_mall, r_hT = Res(), Res(), Res(), Res(), Res(), Res()
        r_ft = [Res(), Res()]
        r_f1b = [Res() for _ in range(NB)]

        def ld(q, dst, src):
            return q.dma_start(out=dst, in_=src)

        S("gpd", lambda q: ld(q, rw[:], io["s6_rw"].rearrange("(fc f) e -> f fc e", f=128)),
          writes=[r_cst])
        S("gpd", lambda q: ld(q, idb[:], io["c_ident_bf"][:, :]), writes=[r_cst])
        S("gpd", lambda q: ld(q, b2s[:], io["s6_b2"][:, :]), writes=[r_cst])
        S("sy", lambda q: ld(q, b1c[:], io["s6_b1c"][:, :]), writes=[r_cst])

        xr = xnTb.rearrange("(fc f) t -> f fc t", f=128)

        for i in range(NT):
            p = i % 2
            S("sy", lambda q, i=i, p=p: ld(q, xg[:, p, :, :],
                                           xr[:, :, i * 128:(i + 1) * 128]),
              reads=[r_cst], writes=[r_xg[p]])

            def g_mm(q, p=p):
                ii = None
                for fc in range(8):
                    ii = q.matmul(pg[:, p, 0:14], xg[:, p, fc, :], rw[:, fc, :],
                                  start=(fc == 0), stop=(fc == 7))
                return ii
            S("te", g_mm, reads=[r_xg[p], r_cst], writes=[r_pg[p]])
            S("ve", lambda q, i=i, p=p: q.reduce_max(gs[:, i, 0:1], pg[:, p, 0:14], AX.X),
              reads=[r_pg[p]], writes=[r_gs])
            S("ve", lambda q, i=i, p=p: q.tensor_scalar(
                gs[:, i, 16:30], pg[:, p, 0:14], gs[:, i, 0:1], None, ALU.is_ge),
              reads=[r_pg[p], r_gs], writes=[r_gs])
            S("ve", lambda q, i=i, p=p: q.tensor_scalar(
                gs[:, i, 32:46], pg[:, p, 0:14], gs[:, i, 0:1], None, ALU.subtract),
              reads=[r_pg[p], r_gs], writes=[r_gs])
            S("sc", lambda q, i=i: q.activation(gs[:, i, 2:16], gs[:, i, 32:46], AF.Exp),
              reads=[r_gs], writes=[r_gs])
            S("ve", lambda q, i=i: q.reduce_sum(gs[:, i, 1:2], gs[:, i, 2:16], AX.X),
              reads=[r_gs], writes=[r_gs])
            S("ve", lambda q, i=i: q.reciprocal(gs[:, i, 46:47], gs[:, i, 1:2]),
              reads=[r_gs], writes=[r_gs])
            S("ve", lambda q, i=i: q.tensor_scalar(
                Mall[:, i, 0:14], gs[:, i, 16:30], gs[:, i, 46:47], None, ALU.mult),
              reads=[r_gs], writes=[r_mall])
            S("ve", lambda q, i=i: q.tensor_copy(Mb[:, 0:14], Mall[:, i, 0:14]),
              reads=[r_mall], writes=[r_mb])
            S("te", lambda q: q.transpose(pmt[0:14, 0:128], Mb[:, 0:14], idb[:]),
              reads=[r_mb, r_cst], writes=[r_pmt], no_drain=True)
            S("ve", lambda q, i=i: q.tensor_copy(MT[0:14, i * 128:(i + 1) * 128],
                                                 pmt[0:14, 0:128]),
              reads=[r_pmt], writes=[r_MT])

        for e in range(NE):
            S("gpd", lambda q, e=e: ld(
                q, w1s[:], io["s6_w1"][e].rearrange("(fc f) (ic i) -> f fc ic i",
                                                    f=128, i=128)), writes=[r_w1])
            S("gpd", lambda q, e=e: ld(
                q, w2s[:], io["s6_w2"][e].rearrange("(ic i) o -> i ic o", i=128)),
              writes=[r_w2])
            for b in range(NB):
                p = (e * NB + b) % 2
                S("sy", lambda q, b=b, p=p: ld(q, xT[:, p, :, :],
                                               xr[:, :, b * 512:(b + 1) * 512]),
                  writes=[r_xT[p]])
                for ic in range(16):
                    hp = ic % 2

                    def h_mm(q, ic=ic, p=p, hp=hp):
                        ii = None
                        for fc in range(8):
                            ii = q.matmul(ph[:, hp, :], w1s[:, fc, ic, :],
                                          xT[:, p, fc, :],
                                          start=(fc == 0), stop=(fc == 7))
                        return ii
                    S("te", h_mm, reads=[r_xT[p], r_w1], writes=[r_ph[hp]])
                    S("sc", lambda q, e=e, ic=ic, hp=hp: q.activation(
                        hT[:, ic, :], ph[:, hp, :], AF.Gelu,
                        bias=b1c[:, e * 16 + ic:e * 16 + ic + 1]),
                      reads=[r_ph[hp], r_cst], writes=[r_hT])
                for j in range(8):
                    s, oc = divmod(j, 2)
                    yp = j % 2

                    def y_mm(q, s=s, oc=oc, yp=yp):
                        ii = None
                        for ic in range(16):
                            ii = q.matmul(py[:, yp, :], hT[:, ic, s * 128:(s + 1) * 128],
                                          w2s[:, ic, oc * 512:(oc + 1) * 512],
                                          start=(ic == 0), stop=(ic == 15))
                        return ii
                    S("te", y_mm, reads=[r_hT, r_w2], writes=[r_py[yp]])
                    S("ve", lambda q, e=e, b=b, s=s, oc=oc, yp=yp: q.tensor_scalar(
                        ft[:, b % 2, s, oc * 512:(oc + 1) * 512], py[:, yp, :],
                        Mall[:, 4 * b + s, e:e + 1], None, ALU.mult),
                      reads=[r_py[yp], r_mall], writes=[r_ft[b % 2]])
                dst = o_f1.rearrange("(bb q) c -> q bb c", q=128)
                if e == 0:
                    S("gpd", lambda q, b=b, dst=dst: ld(
                        q, dst[:, 4 * b:4 * b + 4, :], ft[:, b % 2, :, :]),
                      reads=[r_ft[b % 2]], writes=[r_f1b[b]])
                else:
                    S("gpd", lambda q, b=b, dst=dst: q.dma_start(
                        out=dst[:, 4 * b:4 * b + 4, :], in_=ft[:, b % 2, :, :],
                        accum_op=ALU.add),
                      reads=[r_ft[b % 2]], writes=[r_f1b[b]])
        for b in range(NB):
            for j in range(8):
                s, oc = divmod(j, 2)
                yp = j % 2

                def b2_mm(q, b=b, s=s, oc=oc, yp=yp):
                    return q.matmul(py[:, yp, :], MT[0:14, (4 * b + s) * 128:(4 * b + s + 1) * 128],
                                    b2s[:, oc * 512:(oc + 1) * 512],
                                    start=True, stop=True)
                S("te", b2_mm, reads=[r_MT, r_cst], writes=[r_py[yp]])
                S("ve", lambda q, b=b, s=s, oc=oc, yp=yp: q.tensor_copy(
                    ft[:, b % 2, s, oc * 512:(oc + 1) * 512], py[:, yp, :]),
                  reads=[r_py[yp]], writes=[r_ft[b % 2]])
            dst = o_f1.rearrange("(bb q) c -> q bb c", q=128)
            S("gpd", lambda q, b=b, dst=dst: q.dma_start(
                out=dst[:, 4 * b:4 * b + 4, :], in_=ft[:, b % 2, :, :],
                accum_op=ALU.add),
              reads=[r_ft[b % 2]], writes=[r_f1b[b]])

        with nc.Block() as block:
            sch.emit(nc, block)
    return nc



def build_s7a(nc, io):
    o_qfT, o_kvz = io["qfT_d"], io["kvz_d"]
    with ExitStack() as _es:
        ec = _es.enter_context
        sb = lambda *a: ec(nc.sbuf_tensor(*a))
        cc = sb("s7a_cc", [128, 8], F32)
        idb = sb("s7a_idb", [128, 128], BF16)
        ones = sb("s7a_ones", [128, 128], BF16)
        qw = sb("s7a_qw", [128, 8, 1024], BF16)
        kvw = sb("s7a_kvw", [128, 8, 2048], BF16)
        qbc = sb("s7a_qbc", [128, 8], F32)
        kvb = sb("s7a_kvb", [128, 2048], F32)
        g1 = sb("s7a_g1", [128, 1024], F32)
        be1 = sb("s7a_be1", [128, 1024], F32)
        xt = sb("s7a_x", [128, 2, 1024], F32)
        ncol = sb("s7a_ncol", [128, 2, 8], F32)
        sq = sb("s7a_sq", [128, 1024], F32)
        ln1 = sb("s7a_ln1", [128, 1024], F32)
        ln1b = sb("s7a_ln1b", [128, 1024], BF16)
        lnT = sb("s7a_lnT", [128, 2, 8, 128], BF16)
        qcs = sb("s7a_qcs", [128, 2, 512], F32)
        qfT = sb("s7a_qfT", [128, 8, 128], BF16)
        kvcs = sb("s7a_kvcs", [128, 2048], F32)
        kfb = sb("s7a_kfb", [128, 1024], BF16)
        vfb = sb("s7a_vfb", [128, 1024], BF16)
        kvzs = sb("s7a_kvz", [128, 8, 66], BF16)
        zrow = sb("s7a_zrow", [1, 1024], F32)
        zrowb = sb("s7a_zrowb", [1, 1024], BF16)
        zcs = sb("s7a_zc", [128, 8], F32)
        ptr = ec(nc.psum_tensor("s7a_ptr", [128, 2, 1024], BF16))
        pq = ec(nc.psum_tensor("s7a_pq", [128, 2, 512], F32))
        pkv = ec(nc.psum_tensor("s7a_pkv", [128, 2, 512], F32))
        pacc = ec(nc.psum_tensor("s7a_pacc", [128, 512], F32))
        pz = ec(nc.psum_tensor("s7a_pz", [128, 512], F32))

        sch = Sched()
        S = mk_S(sch)
        r_cst = Res()
        r_x = [Res(), Res()]
        r_nc = [Res(), Res()]
        r_sq, r_ln1, r_ln1b = Res(), Res(), Res()
        r_lnT = [Res(), Res()]
        r_ptr = [Res(), Res()]
        r_pq = [Res(), Res()]
        r_pkv = [Res(), Res()]
        r_qcs = [Res(), Res()]
        r_qfT, r_kvcs, r_kfb, r_vfb = Res(), Res(), Res(), Res()
        r_pacc, r_pz, r_kvz, r_zrow, r_zc = Res(), Res(), Res(), Res(), Res()

        def ld(q, dst, src):
            return q.dma_start(out=dst, in_=src)

        S("gpd", lambda q: ld(q, qw[:], io["s7_qw"].rearrange("(fc f) o -> f fc o", f=128)),
          writes=[r_cst])
        S("gpd", lambda q: ld(q, kvw[:], io["s7_kvw"].rearrange("(fc f) o -> f fc o", f=128)),
          writes=[r_cst])
        S("gpd", lambda q: ld(q, idb[:], io["c_ident_bf"][:, :]), writes=[r_cst])
        S("gpd", lambda q: ld(q, ones[:], io["c_ones_bf"][:, :]), writes=[r_cst])
        for nm, dst in [("s7_qbc", qbc), ("s7_kvb_t", kvb), ("s7_g1_t", g1),
                        ("s7_be1_t", be1), ("c_consts", cc)]:
            S("sy", lambda q, nm=nm, dst=dst: ld(q, dst[:], io[nm][:, :]), writes=[r_cst])

        for i in range(NT):
            p = i % 2
            rows = slice(i * 128, (i + 1) * 128)
            S("sy", lambda q, rows=rows, p=p: ld(q, xt[:, p, :], io["xn1_d"][rows, :]),
              reads=[r_cst], writes=[r_x[p]])
            S("ve", lambda q, p=p: q.reduce_sum(ncol[:, p, 0:1], xt[:, p, :], AX.X),
              reads=[r_x[p]], writes=[r_nc[p]])
            S("sc", lambda q, p=p: q.activation(sq[:], xt[:, p, :], AF.Square,
                                                accum_out=ncol[:, p, 1:2]),
              reads=[r_x[p]], writes=[r_sq, r_nc[p]])
            S("ve", lambda q, p=p: q.tensor_scalar(ncol[:, p, 2:3], ncol[:, p, 0:1],
                                                   1.0 / C, None, ALU.mult),
              reads=[r_nc[p]], writes=[r_nc[p]])
            S("ve", lambda q, p=p: q.tensor_tensor(ncol[:, p, 3:4], ncol[:, p, 0:1],
                                                   ncol[:, p, 2:3], ALU.mult),
              reads=[r_nc[p]], writes=[r_nc[p]])
            S("ve", lambda q, p=p: q.tensor_tensor(ncol[:, p, 4:5], ncol[:, p, 1:2],
                                                   ncol[:, p, 3:4], ALU.subtract),
              reads=[r_nc[p]], writes=[r_nc[p]])
            S("ve", lambda q, p=p: q.tensor_scalar(ncol[:, p, 5:6], ncol[:, p, 4:5],
                                                   1.0 / C, None, ALU.mult),
              reads=[r_nc[p]], writes=[r_nc[p]])
            S("sc", lambda q, p=p: q.activation(ncol[:, p, 6:7], ncol[:, p, 5:6],
                                                AF.Ln, bias=cc[:, 0:1]),
              reads=[r_nc[p], r_cst], writes=[r_nc[p]])
            S("sc", lambda q, p=p: q.activation(ncol[:, p, 7:8], ncol[:, p, 6:7],
                                                AF.Exp, scale=-0.5),
              reads=[r_nc[p]], writes=[r_nc[p]])
            S("ve", lambda q, p=p: q.tensor_scalar(ln1[:], xt[:, p, :],
                                                   ncol[:, p, 2:3], ncol[:, p, 7:8],
                                                   ALU.subtract, ALU.mult),
              reads=[r_x[p], r_nc[p]], writes=[r_ln1])
            S("ve", lambda q: q.tensor_tensor(ln1[:], ln1[:], g1[:], ALU.mult),
              reads=[r_ln1, r_cst], writes=[r_ln1])
            S("ve", lambda q: q.tensor_tensor(ln1b[:], ln1[:], be1[:], ALU.add),
              reads=[r_ln1], writes=[r_ln1b])

            def tr_ln(q, p=p):
                ii = None
                for j in range(8):
                    ii = q.transpose(ptr[:, p, j * 128:(j + 1) * 128],
                                     ln1b[:, j * 128:(j + 1) * 128], idb[:])
                return ii
            S("te", tr_ln, reads=[r_ln1b, r_cst], writes=[r_ptr[p]], no_drain=True)
            S("ve", lambda q, p=p: q.tensor_copy(
                lnT[:, p, :, :].rearrange("a b c -> a (b c)"), ptr[:, p, :]),
              reads=[r_ptr[p]], writes=[r_lnT[p]])

            for hb in range(2):
                def q_mm(q, p=p, hb=hb):
                    ii = None
                    for oj in range(4):
                        oc = hb * 4 + oj
                        for fc in range(8):
                            ii = q.matmul(pq[:, hb, oj * 128:(oj + 1) * 128],
                                          qw[:, fc, oc * 128:(oc + 1) * 128],
                                          lnT[:, p, fc, :],
                                          start=(fc == 0), stop=(fc == 7))
                    return ii
                S("te", q_mm, reads=[r_lnT[p], r_cst], writes=[r_pq[hb]])

                def q_elu(q, hb=hb):
                    for oj in range(4):
                        oc = hb * 4 + oj
                        cl = slice(oj * 128, (oj + 1) * 128)
                        q.tensor_scalar(qcs[:, hb, cl], pq[:, hb, cl],
                                        qbc[:, oc:oc + 1], None, ALU.add)
                    return q.tensor_scalar(qcs[:, hb, 0:1], qcs[:, hb, 0:1], 0.0, None,
                                           ALU.add)
                S("ve", q_elu, reads=[r_pq[hb], r_cst], writes=[r_qcs[hb]])
                S("ve", lambda q, hb=hb: q.tensor_scalar(
                    pq[:, hb, :], qcs[:, hb, :], 0.0, None, ALU.min),
                  reads=[r_qcs[hb]], writes=[r_pq[hb]])
                S("sc", lambda q, hb=hb: q.activation(sq[:, hb * 512:(hb + 1) * 512],
                                                      pq[:, hb, :], AF.Exp),
                  reads=[r_pq[hb]], writes=[r_sq])
                S("ve", lambda q, hb=hb: q.tensor_relu(qcs[:, hb, :], qcs[:, hb, :]),
                  reads=[r_qcs[hb]], writes=[r_qcs[hb]])
                S("ve", lambda q, hb=hb: q.tensor_tensor(
                    qfT[:, :, :].rearrange("a b c -> a (b c)")[:, hb * 512:(hb + 1) * 512],
                    qcs[:, hb, :], sq[:, hb * 512:(hb + 1) * 512], ALU.add),
                  reads=[r_qcs[hb], r_sq], writes=[r_qfT])
            S("gpd", lambda q, i=i: ld(
                q, o_qfT.rearrange("(fc f) t -> f fc t", f=128)[:, :, i * 128:(i + 1) * 128],
                qfT[:, :, :]), reads=[r_qfT])

            for ob in range(4):
                def kv_mm(q, p=p, ob=ob):
                    ii = None
                    for fc in range(8):
                        ii = q.matmul(pkv[:, ob % 2, :], lnT[:, p, fc, :],
                                      kvw[:, fc, ob * 512:(ob + 1) * 512],
                                      start=(fc == 0), stop=(fc == 7))
                    return ii
                S("te", kv_mm, reads=[r_lnT[p], r_cst], writes=[r_pkv[ob % 2]])
                S("ve", lambda q, ob=ob: q.tensor_tensor(
                    kvcs[:, ob * 512:(ob + 1) * 512], pkv[:, ob % 2, :],
                    kvb[:, ob * 512:(ob + 1) * 512], ALU.add),
                  reads=[r_pkv[ob % 2], r_cst], writes=[r_kvcs])
            S("ve", lambda q: q.tensor_scalar(sq[:], kvcs[:, 0:1024], 0.0, None, ALU.min),
              reads=[r_kvcs], writes=[r_sq])
            S("sc", lambda q: q.activation(sq[:], sq[:], AF.Exp),
              reads=[r_sq], writes=[r_sq])
            S("ve", lambda q: q.tensor_relu(kvcs[:, 0:1024], kvcs[:, 0:1024]),
              reads=[r_kvcs], writes=[r_kvcs])
            S("ve", lambda q: q.tensor_tensor(kfb[:], kvcs[:, 0:1024], sq[:], ALU.add),
              reads=[r_kvcs, r_sq], writes=[r_kfb])
            S("ve", lambda q: q.tensor_copy(vfb[:], kvcs[:, 1024:2048]),
              reads=[r_kvcs], writes=[r_vfb])

            def kvacc(q, i=i):
                ii = None
                for h in range(16):
                    rr = slice((h % 2) * 64, (h % 2) * 64 + 64)
                    ii = q.matmul(pacc[rr, (h // 2) * 64:(h // 2) * 64 + 64],
                                  kfb[:, h * 64:(h + 1) * 64],
                                  vfb[:, h * 64:(h + 1) * 64],
                                  start=(i == 0), stop=(i == NT - 1))
                return ii
            S("te", kvacc, reads=[r_kfb, r_vfb], writes=[r_pacc], no_drain=True)

            def zacc(q, i=i):
                q.matmul(pz[0:1, 0:512], ones[:, 0:1], kfb[:, 0:512],
                         start=(i == 0), stop=(i == NT - 1))
                return q.matmul(pz[32:33, 0:512], ones[:, 0:1], kfb[:, 512:1024],
                                start=(i == 0), stop=(i == NT - 1))
            S("te", zacc, reads=[r_kfb, r_cst], writes=[r_pz], no_drain=True)

        def kv_ev(q):
            ii = None
            for h2 in range(8):
                ii = q.tensor_copy(kvzs[:, h2, 0:64], pacc[:, h2 * 64:(h2 + 1) * 64])
            return ii
        S("ve", kv_ev, reads=[r_pacc], writes=[r_kvz])
        S("ve", lambda q: q.tensor_copy(zrow[0:1, 0:512], pz[0:1, 0:512]),
          reads=[r_pz], writes=[r_zrow])
        S("ve", lambda q: q.tensor_copy(zrow[0:1, 512:1024], pz[32:33, 0:512]),
          reads=[r_pz], writes=[r_zrow])

        S("ve", lambda q: q.tensor_copy(zrowb[:], zrow[:]), reads=[r_zrow],
          writes=[r_zrow])

        def z_tr(q):
            ii = None
            for j in range(8):
                ii = q.transpose(ptr[:, 0, j * 128:(j + 1) * 128][:, 0:1],
                                 zrowb[0:1, j * 128:(j + 1) * 128], idb[0:1, 0:1])
            return ii
        S("te", z_tr, reads=[r_zrow, r_cst], writes=[r_ptr[0]], no_drain=True)

        def z_ev(q):
            ii = None
            for j in range(8):
                ii = q.tensor_copy(zcs[:, j:j + 1], ptr[:, 0, j * 128:j * 128 + 1])
            return ii
        S("ve", z_ev, reads=[r_ptr[0]], writes=[r_zc])

        def z_fill(q):
            ii = None
            for h2 in range(8):
                ii = q.tensor_copy(kvzs[:, h2, 64:65], zcs[:, h2:h2 + 1])
            return ii
        S("ve", z_fill, reads=[r_zc], writes=[r_kvz])
        S("ve", lambda q: q.memset(kvzs[:, :, 65:66], 0.0), writes=[r_kvz])
        S("gpd", lambda q: ld(q, o_kvz[:, :],
                              kvzs[:, :, :].rearrange("a b c -> a (b c)")),
          reads=[r_kvz])

        with nc.Block() as block:
            sch.emit(nc, block)
    return nc



def build_s7b(nc, io):
    o_out = io["out_d"]
    with ExitStack() as _es:
        ec = _es.enter_context
        sb = lambda *a: ec(nc.sbuf_tensor(*a))
        cc = sb("s7b_cc", [128, 8], F32)
        idb = sb("s7b_idb", [128, 128], BF16)
        ones = sb("s7b_ones", [128, 128], BF16)
        ow = sb("s7b_ow", [128, 8, 1024], BF16)
        obt = sb("s7b_obt", [128, 1024], BF16)
        w1 = sb("s7b_w1", [128, 8, 4096], BF16)
        b1c = sb("s7b_b1c", [128, 32], F32)
        w2 = sb("s7b_w2", [128, 32, 1024], BF16)
        b2t = sb("s7b_b2t", [128, 1024], BF16)
        g2 = sb("s7b_g2", [128, 1024], BF16)
        be2 = sb("s7b_be2", [128, 1024], BF16)
        kvzs = sb("s7b_kvz", [128, 8, 66], BF16)
        qfb = sb("s7b_qfb", [128, 8, 512], BF16)
        den = sb("s7b_den", [1, 1024], F32)
        bcs = sb("s7b_bc", [64, 512], BF16)
        ysb = sb("s7b_y", [128, 8, 512], BF16)
        xn1t = sb("s7b_xn1", [128, 1024], F32)
        x1t = sb("s7b_x1", [128, 1024], F32)
        xxs = sb("s7b_xx", [128, 1, 1024], F32)
        ncol = sb("s7b_ncol", [128, 2, 8], F32)
        sq = sb("s7b_sq", [128, 1024], F32)
        ln2b = sb("s7b_ln2b", [128, 1024], BF16)
        lnT = sb("s7b_lnT", [128, 8, 128], BF16)
        hms = sb("s7b_hm", [128, 32, 128], BF16)
        outs = sb("s7b_out", [128, 1024], F32)
        pA = ec(nc.psum_tensor("s7b_pA", [128, 2, 512], F32))
        pB = ec(nc.psum_tensor("s7b_pB", [128, 2, 512], F32))
        pC = ec(nc.psum_tensor("s7b_pC", [128, 2, 512], F32))
        ptr = ec(nc.psum_tensor("s7b_ptr", [128, 1024], BF16))
        pD = ec(nc.psum_tensor("s7b_pD", [128, 512], F32))

        sch = Sched()
        S = mk_S(sch)
        r_cst = Res()
        r_qfb, r_kvz = Res(), Res()
        r_pA = [Res(), Res()]
        r_pB = [Res(), Res()]
        r_pC = [Res(), Res()]
        r_pD, r_ptr = Res(), Res()
        r_den = [Res(), Res()]
        r_rdb = [Res(), Res()]
        r_bc = [Res(), Res()]
        r_y, r_xn1, r_x1 = Res(), Res(), Res()
        r_xx = [Res(), Res()]
        r_nc = [Res(), Res()]
        r_sq, r_ln2b, r_lnT, r_hm, r_out = Res(), Res(), Res(), Res(), Res()

        def ld(q, dst, src):
            return q.dma_start(out=dst, in_=src)

        S("gpd", lambda q: ld(q, ow[:], io["s7_ow"].rearrange("(fc f) o -> f fc o", f=128)),
          writes=[r_cst])
        S("gpd", lambda q: ld(q, w1[:], io["s7_w1"].rearrange("(fc f) o -> f fc o", f=128)),
          writes=[r_cst])
        S("gpd", lambda q: ld(q, w2[:], io["s7_w2"].rearrange("(ic i) o -> i ic o", i=128)),
          writes=[r_cst])
        S("gpd", lambda q: ld(q, idb[:], io["c_ident_bf"][:, :]), writes=[r_cst])
        S("gpd", lambda q: ld(q, ones[:], io["c_ones_bf"][:, :]), writes=[r_cst])
        S("gpd", lambda q: ld(q, kvzs[:].rearrange("a b c -> a (b c)"), io["kvz_d"][:, :]),
          writes=[r_kvz])
        for nm, dst in [("s7_ob_t", obt), ("s7_b2_t", b2t), ("s7_g2_t", g2),
                        ("s7_be2_t", be2)]:
            S("gpd", lambda q, nm=nm, dst=dst: ld(q, dst[:], io[nm][:, :]), writes=[r_cst])
        for nm, dst in [("s7_b1c", b1c), ("c_consts", cc)]:
            S("sy", lambda q, nm=nm, dst=dst: ld(q, dst[:], io[nm][:, :]), writes=[r_cst])

        qfr = io["qfT_d"].rearrange("(fc f) t -> f fc t", f=128)

        for bb in range(16):
            tb = bb * 512
            S("sy", lambda q, tb=tb: ld(q, qfb[:, :, :], qfr[:, :, tb:tb + 512]),
              reads=[r_cst], writes=[r_qfb])
            for h in range(16):
                rr = slice((h % 2) * 64, (h % 2) * 64 + 64)
                hp = h % 2

                def aug_mm(q, h=h, rr=rr, hp=hp):
                    return q.matmul(pA[0:65, hp, :], kvzs[rr, h // 2, 0:65],
                                    qfb[rr, h // 2, :], start=True, stop=True)
                S("te", aug_mm, reads=[r_kvz, r_qfb], writes=[r_pA[hp]])
                S("ve", lambda q, hp=hp: q.tensor_scalar(
                    den[0:1, 0:512], pA[64:65, hp, :], 1e-6, None, ALU.max),
                  reads=[r_pA[hp]], writes=[r_den[0]])
                S("ve", lambda q: q.reciprocal(den[0:1, 512:1024], den[0:1, 0:512]),
                  reads=[r_den[0]], writes=[r_den[0]])
                S("ve", lambda q: q.tensor_copy(bcs[0:1, :], den[0:1, 512:1024]),
                  reads=[r_den[0]], writes=[r_rdb[0]])

                def bc_mm(q, hp=hp):
                    return q.matmul(pB[0:64, hp, :], ones[0:1, 0:64], bcs[0:1, :],
                                    start=True, stop=True)
                S("te", bc_mm, reads=[r_rdb[0], r_cst], writes=[r_pB[hp]])
                S("ve", lambda q, hp=hp: q.tensor_copy(bcs[:, :], pB[0:64, hp, :]),
                  reads=[r_pB[hp]], writes=[r_bc[0]])
                S("ve", lambda q, h=h, rr=rr, hp=hp: q.tensor_tensor(
                    ysb[rr, h // 2, :], pA[0:64, hp, :], bcs[:, :], ALU.mult),
                  reads=[r_pA[hp], r_bc[0]], writes=[r_y])

            for s in range(4):
                sp = s % 2
                rows = slice(tb + s * 128, tb + (s + 1) * 128)

                def proj_mm(q, s=s):
                    ii = None
                    for oc in range(2):
                        for fc in range(8):
                            ii = q.matmul(pC[:, oc, :], ysb[:, fc, s * 128:(s + 1) * 128],
                                          ow[:, fc, oc * 512:(oc + 1) * 512],
                                          start=(fc == 0), stop=(fc == 7))
                    return ii
                S("te", proj_mm, reads=[r_y, r_cst], writes=[r_pC[0], r_pC[1]])
                S("sy", lambda q, rows=rows: ld(q, xn1t[:], io["xn1_d"][rows, :]),
                  writes=[r_xn1])

                def xx_c(q, sp=sp):
                    for oc in range(2):
                        cl = slice(oc * 512, (oc + 1) * 512)
                        q.tensor_tensor(xxs[:, 0, cl], pC[:, oc, :], obt[:, cl], ALU.add)
                    ii = None
                    for oc in range(2):
                        cl = slice(oc * 512, (oc + 1) * 512)
                        ii = q.tensor_tensor(xxs[:, 0, cl], xxs[:, 0, cl],
                                             xn1t[:, cl], ALU.add)
                    return ii
                S("ve", xx_c, reads=[r_pC[0], r_pC[1], r_xn1, r_cst], writes=[r_xx[sp]])
                S("ve", lambda q, sp=sp: q.reduce_sum(ncol[:, sp, 0:1], xxs[:, 0, :], AX.X),
                  reads=[r_xx[sp]], writes=[r_nc[sp]])
                S("sc", lambda q, sp=sp: q.activation(sq[:], xxs[:, 0, :], AF.Square,
                                                      accum_out=ncol[:, sp, 1:2]),
                  reads=[r_xx[sp]], writes=[r_sq, r_nc[sp]])
                S("ve", lambda q, sp=sp: q.tensor_scalar(ncol[:, sp, 2:3], ncol[:, sp, 0:1],
                                                         1.0 / C, None, ALU.mult),
                  reads=[r_nc[sp]], writes=[r_nc[sp]])
                S("ve", lambda q, sp=sp: q.tensor_tensor(ncol[:, sp, 3:4], ncol[:, sp, 0:1],
                                                         ncol[:, sp, 2:3], ALU.mult),
                  reads=[r_nc[sp]], writes=[r_nc[sp]])
                S("ve", lambda q, sp=sp: q.tensor_tensor(ncol[:, sp, 4:5], ncol[:, sp, 1:2],
                                                         ncol[:, sp, 3:4], ALU.subtract),
                  reads=[r_nc[sp]], writes=[r_nc[sp]])
                S("ve", lambda q, sp=sp: q.tensor_scalar(ncol[:, sp, 5:6], ncol[:, sp, 4:5],
                                                         1.0 / C, None, ALU.mult),
                  reads=[r_nc[sp]], writes=[r_nc[sp]])
                S("sc", lambda q, sp=sp: q.activation(ncol[:, sp, 6:7], ncol[:, sp, 5:6],
                                                      AF.Ln, bias=cc[:, 0:1]),
                  reads=[r_nc[sp], r_cst], writes=[r_nc[sp]])
                S("sc", lambda q, sp=sp: q.activation(ncol[:, sp, 7:8], ncol[:, sp, 6:7],
                                                      AF.Exp, scale=-0.5),
                  reads=[r_nc[sp]], writes=[r_nc[sp]])
                S("ve", lambda q, sp=sp: q.tensor_scalar(sq[:], xxs[:, 0, :],
                                                         ncol[:, sp, 2:3], ncol[:, sp, 7:8],
                                                         ALU.subtract, ALU.mult),
                  reads=[r_xx[sp], r_nc[sp]], writes=[r_sq])
                S("ve", lambda q: q.tensor_tensor(ln2b[:], sq[:], g2[:], ALU.mult),
                  reads=[r_sq, r_cst], writes=[r_ln2b])
                S("ve", lambda q: q.scalar_tensor_tensor(ln2b[:], ln2b[:], 1.0, be2[:],
                                                         ALU.mult, ALU.add),
                  reads=[r_ln2b, r_cst], writes=[r_ln2b])

                def tr_ln2(q):
                    ii = None
                    for j in range(8):
                        ii = q.transpose(ptr[:, j * 128:(j + 1) * 128],
                                         ln2b[:, j * 128:(j + 1) * 128], idb[:])
                    return ii
                S("te", tr_ln2, reads=[r_ln2b, r_cst], writes=[r_ptr], no_drain=True)
                S("ve", lambda q: q.tensor_copy(
                    lnT[:, :, :].rearrange("a b c -> a (b c)"), ptr[:, :]),
                  reads=[r_ptr], writes=[r_lnT])

                for hb in range(8):
                    def h_mm(q, hb=hb):
                        ii = None
                        for oj in range(4):
                            oc = hb * 4 + oj
                            for fc in range(8):
                                ii = q.matmul(pC[:, hb % 2, oj * 128:(oj + 1) * 128],
                                              w1[:, fc, oc * 128:(oc + 1) * 128],
                                              lnT[:, fc, :],
                                              start=(fc == 0), stop=(fc == 7))
                        return ii
                    S("te", h_mm, reads=[r_lnT, r_cst], writes=[r_pC[hb % 2]])

                    def h_act(q, hb=hb):
                        ii = None
                        for oj in range(4):
                            oc = hb * 4 + oj
                            ii = q.activation(hms[:, oc, :],
                                              pC[:, hb % 2, oj * 128:(oj + 1) * 128],
                                              AF.Gelu, bias=b1c[:, oc:oc + 1])
                        return ii
                    S("sc", h_act, reads=[r_pC[hb % 2], r_cst], writes=[r_hm])

                S("sy", lambda q, rows=rows: ld(q, x1t[:], io["x1_d"][rows, :]),
                  writes=[r_x1])
                S("gpd", lambda q, rows=rows: q.dma_start(
                    out=x1t[:], in_=io["f1_d"][rows, :], accum_op=ALU.add),
                  reads=[r_x1], writes=[r_x1])
                for oc in range(2):
                    def o_mm(q, oc=oc):
                        ii = None
                        for ic in range(32):
                            ii = q.matmul(pD[:, :], hms[:, ic, :],
                                          w2[:, ic, oc * 512:(oc + 1) * 512],
                                          start=(ic == 0), stop=(ic == 31))
                        return ii
                    S("te", o_mm, reads=[r_hm, r_cst], writes=[r_pD])

                    def fin(q, sp=sp, oc=oc):
                        cl = slice(oc * 512, (oc + 1) * 512)
                        q.tensor_tensor(outs[:, cl], pD[:, :], b2t[:, cl], ALU.add)
                        q.drain()
                        q.tensor_tensor(outs[:, cl], outs[:, cl], xxs[:, 0, cl], ALU.add)
                        q.drain()
                        return q.tensor_tensor(outs[:, cl], outs[:, cl], x1t[:, cl], ALU.add)
                    S("ve", fin, reads=[r_pD, r_xx[sp], r_x1, r_cst], writes=[r_out])
                S("scd", lambda q, rows=rows: ld(q, o_out[rows, :], outs[:]),
                  reads=[r_out])

        with nc.Block() as block:
            sch.emit(nc, block)
    return nc



S4_TILES = NT
S4_PHASE = 5
S4_APPS = 6
S4_CPRE = 3


def build_s4(nc, io):
    o_y = io["y_d"]
    f_int = nc.dram_tensor("s4_fint", [128, 1024], BF16, kind="Internal").ap()
    with ExitStack() as _es:
        ec = _es.enter_context
        sb = lambda *a: ec(nc.sbuf_tensor(*a))
        dmM = sb("s4_dmM", [128, 128], F32)
        dmP = sb("s4_dmP", [128, 128], F32)
        msk = sb("s4_msk", [64, 320], F32)
        idb = sb("s4_idb", [128, 128], BF16)
        F_sb = sb("s4_F", [128, 8, 128], BF16)
        Ff = sb("s4_Ff", [128, 8, 128], F32)
        fsum = sb("s4_fsum", [128, 1024], F32)
        fb = sb("s4_fb", [128, 1024], BF16)
        wt = sb("s4_w", [128, 2, 1024], F32)
        rt = sb("s4_r", [128, 2, 1024], BF16)
        kt = sb("s4_k", [128, 2, 1024], BF16)
        kkt = sb("s4_kk", [128, 2, 1024], BF16)
        bt = sb("s4_b", [128, 2, 1024], BF16)
        vt = sb("s4_v", [64, 2, 2, 1024], BF16)
        vt2 = sb("s4_v2", [128, 2, 2, 1024], BF16)
        ers = sb("s4_er", [128, 1024], F32)
        eas = sb("s4_ea", [128, 1024], F32)
        ebs = sb("s4_eb", [128, 1024], F32)
        rts = sb("s4_rt", [64, 2, 1024], BF16)
        ats = sb("s4_at", [64, 2, 1024], BF16)
        bhs = sb("s4_bh", [64, 2, 1024], BF16)
        khs = sb("s4_kh", [64, 2, 1024], BF16)
        chT = sb("s4_chT", [128, 2, 4, 8, 64], BF16)
        TmA = sb("s4_TmA", [128, 3, 64], BF16)
        TmB = sb("s4_TmB", [128, 2, 64], BF16)
        mskd = sb("s4_mskd", [128, 320], F32)
        Qp = sb("s4_Qp", [128, 5, 64], BF16)
        Qu = sb("s4_Qu", [128, 2, 64], BF16)
        Xs = sb("s4_X", [128, 2, 64], BF16)
        Cs = sb("s4_C", [128, 2, 64], BF16)
        Gs = sb("s4_G", [128, 8, 64], BF16)
        yts = sb("s4_y", [128, 2, 1024], F32)
        pcs = ec(nc.psum_tensor("s4_pcs", [128, 2, 512], F32))
        pcp = ec(nc.psum_tensor("s4_pcp", [128, 2, 512], F32))
        ptr = ec(nc.psum_tensor("s4_ptr", [128, 2, 1024], BF16))
        pT = ec(nc.psum_tensor("s4_pT", [128, 2, 512], F32))

        sch = Sched()
        S = mk_S(sch)
        r_cst, r_fsum, r_fb, r_fint, r_F = Res(), Res(), Res(), Res(), Res()
        r_w = [Res(), Res()]
        r_r = [Res(), Res()]
        r_k = [Res(), Res()]
        r_kk = [Res(), Res()]
        r_b = [Res(), Res()]
        r_v = [Res(), Res()]
        r_pcs = [Res(), Res()]
        r_pcp = [Res(), Res()]
        r_ptr = [Res(), Res()]
        r_pT = [Res(), Res()]
        r_er, r_ea, r_eb = Res(), Res(), Res()
        r_rts, r_ats, r_bhs, r_khs = Res(), Res(), Res(), Res()
        r_chT = [Res(), Res()]
        r_Tm, r_Qp, r_Cs = Res(), Res(), Res()
        r_Qu = [Res(), Res()]
        r_X = [Res(), Res()]
        r_G = [Res() for _ in range(8)]
        r_y = [Res(), Res()]

        def ld(q, dst, src):
            return q.dma_start(out=dst, in_=src)

        for nm, dst in [("s4_dmatM", dmM), ("s4_dmatP", dmP), ("s4_masks", msk)]:
            S("sy", lambda q, nm=nm, dst=dst: ld(q, dst[:], io[nm][:, :]), writes=[r_cst])
        S("sy", lambda q: ld(q, mskd[0:64, :], io["s4_masks"][:, :]), writes=[r_cst])
        S("sy", lambda q: ld(q, mskd[64:128, :], io["s4_masks"][:, :]), writes=[r_cst])
        S("gpd", lambda q: ld(q, idb[:], io["c_ident_bf"][:, :]), writes=[r_cst])
        gz = S("ve", lambda q: q.memset(Gs[:].rearrange("a b c -> a (b c)"), 0.0))
        for h2 in range(8):
            r_G[h2].w = gz

        S("sy", lambda q: ld(q, fsum[:], io["wlm_d"][:, :]), writes=[r_fsum])
        S("gpd", lambda q: q.dma_start(out=fsum[:], in_=io["mcs_d"][:, :],
                                       accum_op=ALU.add),
          reads=[r_fsum], writes=[r_fsum])
        S("sc", lambda q: q.activation(fb[:], fsum[:], AF.Exp),
          reads=[r_fsum], writes=[r_fb])
        S("gpd", lambda q: ld(q, f_int[:, :], fb[:]), reads=[r_fb], writes=[r_fint])

        for fc in range(8):
            S("sy", lambda q, fc=fc: q.dma_start_transpose(
                F_sb[:, fc, :], f_int[:, fc * 128:(fc + 1) * 128]),
              reads=[r_fint], writes=[r_F])
        S("ve", lambda q: q.tensor_copy(Ff[:].rearrange("a b c -> a (b c)"),
                                        F_sb[:].rearrange("a b c -> a (b c)")),
          reads=[r_F], writes=[r_F])

        for i in range(S4_TILES):
            p = i % 2
            rows = slice(i * 128, (i + 1) * 128)
            S("sy", lambda q, rows=rows, p=p: ld(q, wt[:, p, :], io["w_d"][rows, :]),
              reads=[r_cst], writes=[r_w[p]])
            S("sy", lambda q, rows=rows, p=p: ld(q, rt[:, p, :], io["r_d"][rows, :]),
              writes=[r_r[p]])
            S("sy", lambda q, rows=rows, p=p: ld(q, kt[:, p, :], io["k_d"][rows, :]),
              writes=[r_k[p]])
            S("scd", lambda q, rows=rows, p=p: ld(q, kkt[:, p, :], io["kk_d"][rows, :]),
              writes=[r_kk[p]])
            S("scd", lambda q, rows=rows, p=p: ld(q, bt[:, p, :], io["b_d"][rows, :]),
              writes=[r_b[p]])
            S("scd", lambda q, rows=rows, p=p: ld(
                q, vt[:, p, :, :],
                io["v_d"][rows, :].rearrange("(cj t) c -> t cj c", cj=2)),
              writes=[r_v[p]])
            S("scd", lambda q, rows=rows, p=p: ld(
                q, vt2[0:64, p, :, :],
                io["v_d"][rows, :].rearrange("(cj t) c -> t cj c", cj=2)),
              writes=[r_v[p]])
            S("scd", lambda q, rows=rows, p=p: ld(
                q, vt2[64:128, p, :, :],
                io["v_d"][rows, :].rearrange("(cj t) c -> t cj c", cj=2)),
              writes=[r_v[p]])

            def cs_mm(q, p=p):
                q.matmul(pcs[:, 0, :], dmM[:], wt[:, p, 0:512], start=True, stop=True)
                q.matmul(pcs[:, 1, :], dmM[:], wt[:, p, 512:1024], start=True, stop=True)
                q.matmul(pcp[:, 0, :], dmP[:], wt[:, p, 0:512], start=True, stop=True)
                return q.matmul(pcp[:, 1, :], dmP[:], wt[:, p, 512:1024],
                                start=True, stop=True)
            S("te", cs_mm, reads=[r_w[p], r_cst],
              writes=[r_pcs[0], r_pcs[1], r_pcp[0], r_pcp[1]])
            def exps(q):
                q.activation(ers[:, 0:512], pcs[:, 0, :], AF.Exp)
                q.drain()
                q.activation(ers[:, 512:1024], pcs[:, 1, :], AF.Exp)
                q.drain()
                q.activation(ebs[:, 0:512], pcs[:, 0, :], AF.Exp, scale=-1.0)
                q.drain()
                q.activation(ebs[:, 512:1024], pcs[:, 1, :], AF.Exp, scale=-1.0)
                q.drain()
                q.activation(eas[:, 0:512], pcp[:, 0, :], AF.Exp)
                q.drain()
                return q.activation(eas[:, 512:1024], pcp[:, 1, :], AF.Exp)
            S("sc", exps, reads=[r_pcs[0], r_pcs[1], r_pcp[0], r_pcp[1]],
              writes=[r_er, r_eb, r_ea])

            def scl(q, p=p):
                ii = None
                for cj in range(2):
                    rsl = slice(cj * 64, (cj + 1) * 64)
                    q.tensor_tensor(rts[:, cj, :], rt[rsl, p, :], ers[rsl, :], ALU.mult)
                    q.scalar_tensor_tensor(ats[:, cj, :], kt[rsl, p, :], -1.0,
                                           eas[rsl, :], ALU.mult, ALU.mult)
                    q.tensor_tensor(bhs[:, cj, :], bt[rsl, p, :], ebs[rsl, :], ALU.mult)
                    ii = q.tensor_tensor(khs[:, cj, :], kkt[rsl, p, :], ebs[rsl, :],
                                         ALU.mult)
                return ii
            S("ve", scl, reads=[r_r[p], r_k[p], r_b[p], r_kk[p], r_er, r_ea, r_eb],
              writes=[r_rts, r_ats, r_bhs, r_khs])

            for cj in range(2):
                def tr_w1(q, cj=cj):
                    ii = None
                    for ai, arr in enumerate((ats, bhs)):
                        for fc in range(8):
                            ii = q.transpose(
                                ptr[:, cj, (ai * 8 + fc) * 64:(ai * 8 + fc + 1) * 64],
                                arr[:, cj, fc * 128:(fc + 1) * 128], idb[0:64, 0:64])
                    return ii
                S("te", tr_w1, reads=[r_ats, r_bhs, r_cst], writes=[r_ptr[cj]],
                  no_drain=True)
                S("ve", lambda q, cj=cj: q.tensor_copy(
                    chT[:, cj, 0:2, :, :].rearrange("a b c d -> a (b c d)"),
                    ptr[:, cj, :]),
                  reads=[r_ptr[cj]], writes=[r_chT[cj]])

                def tr_w2(q, cj=cj):
                    ii = None
                    for ai, arr in enumerate((khs, rts)):
                        for fc in range(8):
                            ii = q.transpose(
                                ptr[:, cj, (ai * 8 + fc) * 64:(ai * 8 + fc + 1) * 64],
                                arr[:, cj, fc * 128:(fc + 1) * 128], idb[0:64, 0:64])
                    return ii
                S("te", tr_w2, reads=[r_khs, r_rts, r_cst], writes=[r_ptr[cj]],
                  no_drain=True)
                S("ve", lambda q, cj=cj: q.tensor_copy(
                    chT[:, cj, 2:4, :, :].rearrange("a b c d -> a (b c d)"),
                    ptr[:, cj, :]),
                  reads=[r_ptr[cj]], writes=[r_chT[cj]])

            for cj in range(2):
                for h2 in range(8):
                    hcs = [slice((2 * h2 + hf) * 64, (2 * h2 + hf + 1) * 64)
                           for hf in range(2)]

                    def tf_mm(q, cj=cj, h2=h2):
                        ii = None
                        for hf in range(2):
                            rr = slice(hf * 64, hf * 64 + 64)
                            a_ = chT[rr, cj, 0, h2, :]
                            b_ = chT[rr, cj, 1, h2, :]
                            k_ = chT[rr, cj, 2, h2, :]
                            r_ = chT[rr, cj, 3, h2, :]
                            q.matmul(pT[rr, 0, 0:64], a_, b_, start=True, stop=True)
                            q.matmul(pT[rr, 1, 0:64], b_, a_, start=True, stop=True)
                            q.matmul(pT[rr, 0, 64:128], k_, a_, start=True, stop=True)
                            q.matmul(pT[rr, 1, 64:128], b_, r_, start=True, stop=True)
                            ii = q.matmul(pT[rr, 0, 128:192], k_, r_, start=True,
                                          stop=True)
                        return ii
                    S("te", tf_mm, reads=[r_chT[cj], r_cst], writes=[r_pT[0], r_pT[1]])

                    def tf_msk(q):
                        q.tensor_tensor(TmA[:, :, :].rearrange("a b c -> a (b c)"),
                                        pT[:, 0, 0:192], mskd[:, 0:192], ALU.mult)
                        return q.tensor_tensor(TmB[:, :, :].rearrange("a b c -> a (b c)"),
                                               pT[:, 1, 0:128], mskd[:, 192:320],
                                               ALU.mult)
                    S("ve", tf_msk, reads=[r_pT[0], r_pT[1], r_cst], writes=[r_Tm])

                    for li in range(5):
                        bk = li % 2

                        def sq_mm(q, li=li, bk=bk):
                            ii = None
                            for hf in range(2):
                                rr = slice(hf * 64, hf * 64 + 64)
                                if li == 0:
                                    qu_, qp_ = TmA[rr, 0, :], TmB[rr, 0, :]
                                else:
                                    qu_ = Qu[rr, (li - 1) % 2, :]
                                    qp_ = Qp[rr, li - 1, :]
                                if li < 4:
                                    q.matmul(pT[rr, bk, 256:320], qp_, qu_,
                                             start=True, stop=True)
                                ii = q.matmul(pT[rr, 1 - bk, 256:320], qu_, qp_,
                                              start=True, stop=True)
                            return ii
                        S("te", sq_mm,
                          reads=[r_Tm, r_Qu[(li - 1) % 2] if li > 0 else r_Tm, r_Qp],
                          writes=[r_pT[0], r_pT[1]])

                        def sq_ev(q, li=li, bk=bk):
                            if li < 4:
                                q.tensor_copy(Qu[:, li % 2, :], pT[:, bk, 256:320])
                                q.drain()
                            return q.tensor_copy(Qp[:, li, :], pT[:, 1 - bk, 256:320])
                        S("ve", sq_ev, reads=[r_pT[0], r_pT[1]],
                          writes=[r_Qu[li % 2], r_Qp])

                    def cpre_mm(q, cj=cj, h2=h2, p=p):
                        ii = None
                        for hf in range(2):
                            hcf = slice((2 * h2 + hf) * 64, (2 * h2 + hf + 1) * 64)
                            rr = slice(hf * 64, hf * 64 + 64)
                            q.matmul(pcs[rr, 0, 0:64], chT[rr, cj, 0, h2, :],
                                     Gs[rr, h2, :], start=True, stop=True)
                            ii = q.matmul(pcs[rr, 1, 0:64], TmA[rr, 1, :],
                                          vt2[rr, p, cj, hcf], start=True, stop=True)
                        return ii
                    S("te", cpre_mm, reads=[r_chT[cj], r_G[h2], r_Tm, r_v[p]],
                      writes=[r_pcs[0], r_pcs[1]])
                    S("ve", lambda q: q.tensor_copy(Xs[:, 1, :], pcs[:, 0, 0:64]),
                      reads=[r_pcs[0]], writes=[r_X[1]])
                    S("ve", lambda q: q.tensor_tensor(Xs[:, 0, :], pcs[:, 1, 0:64],
                                                      Xs[:, 1, :], ALU.add),
                      reads=[r_pcs[1], r_X[1]], writes=[r_X[0]])
                    for ai2 in range(6):
                        xp = ai2 % 2
                        bk2 = ai2 % 2

                        def app_mm(q, ai2=ai2, xp=xp, bk2=bk2):
                            ii = None
                            for hf in range(2):
                                rr = slice(hf * 64, hf * 64 + 64)
                                qq = TmB[rr, 0, :] if ai2 == 5 else Qp[rr, 4 - ai2, :]
                                ii = q.matmul(pcs[rr, bk2, 0:64], qq, Xs[rr, xp, :],
                                              start=True, stop=True)
                            return ii
                        S("te", app_mm, reads=[r_Qp, r_Tm, r_X[xp]], writes=[r_pcs[bk2]])
                        if ai2 < 5:
                            S("ve", lambda q, xp=xp, bk2=bk2: q.tensor_tensor(
                                Xs[:, 1 - xp, :], pcs[:, bk2, 0:64], Xs[:, xp, :],
                                ALU.add),
                              reads=[r_pcs[bk2], r_X[xp]], writes=[r_X[1 - xp]])
                        else:
                            def c_ev(q, xp=xp, bk2=bk2):
                                q.tensor_tensor(Cs[:, 0, :], pcs[:, bk2, 0:64],
                                                Xs[:, xp, :], ALU.add)
                                q.drain()
                                return q.tensor_copy(Cs[0:64, 1, :], Cs[64:128, 0, :])
                            S("ve", c_ev, reads=[r_pcs[bk2], r_X[xp]], writes=[r_Cs])

                    def y_mm(q, cj=cj, h2=h2, p=p):
                        ii = None
                        for hf in range(2):
                            hcf = slice((2 * h2 + hf) * 64, (2 * h2 + hf + 1) * 64)
                            rr = slice(hf * 64, hf * 64 + 64)
                            q.matmul(pT[rr, 0, 192:256], chT[rr, cj, 3, h2, :],
                                     Gs[rr, h2, :], start=True, stop=True)
                            q.matmul(pT[rr, 1, 192:256], TmB[rr, 1, :], Cs[rr, 0, :],
                                     start=True, stop=False)
                            ii = q.matmul(pT[rr, 1, 192:256], TmA[rr, 2, :],
                                          vt2[rr, p, cj, hcf], start=False,
                                          stop=True)
                        return ii
                    S("te", y_mm, reads=[r_chT[cj], r_G[h2], r_Tm, r_Cs, r_v[p]],
                      writes=[r_pT[0], r_pT[1]])

                    def y_ev(q, cj=cj, p=p, h2=h2):
                        rows = slice(cj * 64, (cj + 1) * 64)
                        hc0 = slice(2 * h2 * 64, (2 * h2 + 1) * 64)
                        hc1 = slice((2 * h2 + 1) * 64, (2 * h2 + 2) * 64)
                        q.tensor_copy(yts[rows, p, hc0], pT[0:64, 0, 192:256])
                        q.tensor_copy(yts[rows, p, hc1], pT[64:128, 0, 192:256])
                        q.drain()
                        q.tensor_tensor(yts[rows, p, hc0], pT[0:64, 1, 192:256],
                                        yts[rows, p, hc0], ALU.add)
                        return q.tensor_tensor(yts[rows, p, hc1],
                                               pT[64:128, 1, 192:256],
                                               yts[rows, p, hc1], ALU.add)
                    S("ve", y_ev, reads=[r_pT[0], r_pT[1], r_y[p]], writes=[r_y[p]])

                    def st_mm(q, cj=cj, p=p, h2=h2):
                        ii = None
                        for hf in range(2):
                            hcf = slice((2 * h2 + hf) * 64, (2 * h2 + hf + 1) * 64)
                            rr = slice(hf * 64, hf * 64 + 64)
                            cs_src = Cs[0:64, hf, :]
                            q.matmul(pcp[rr, 0, 0:64], bhs[:, cj, hcf], cs_src,
                                     start=True, stop=False)
                            ii = q.matmul(pcp[rr, 0, 0:64], khs[:, cj, hcf],
                                          vt[:, p, cj, hcf], start=False, stop=True)
                        return ii
                    S("te", st_mm, reads=[r_bhs, r_khs, r_Cs, r_v[p]], writes=[r_pcp[0]])

                    def st_up(q, cj=cj, h2=h2, i=i):
                        ci = 2 * i + cj
                        q.tensor_tensor(Xs[:, 0, :], pcp[:, 0, 0:64], Gs[:, h2, :],
                                        ALU.add)
                        q.drain()
                        return q.tensor_scalar(Gs[:, h2, :], Xs[:, 0, :],
                                               Ff[:, h2, ci:ci + 1], None, ALU.mult)
                    S("ve", st_up, reads=[r_pcp[0], r_G[h2], r_F, r_X[0]],
                      writes=[r_G[h2], r_X[0]])

            S("scd", lambda q, rows=rows, p=p: ld(q, o_y[rows, :], yts[:, p, :]),
              reads=[r_y[p]])

        with nc.Block() as block:
            sch.emit(nc, block)
    return nc


BUILDERS = {"s1": build_s1, "s2": build_s2, "s3": build_s3, "s5": build_s5,
            "s6": build_s6, "s7a": build_s7a, "s7b": build_s7b, "s4": build_s4}


def build_program(stages):
    nc = bass.Bass("TRN2", target_bir_lowering=False)
    needed_in, needed_out = [], []
    for s in stages:
        for n in STAGE_IN[s]:
            if n not in needed_in and not any(n in STAGE_OUT[t] for t in stages[:stages.index(s)]):
                needed_in.append(n)
        for n in STAGE_OUT[s]:
            needed_out.append(n)
    io = {}
    for n in needed_in:
        shape, dt, _ = IO_SPECS[n]
        io[n] = nc.dram_tensor(n, shape, dt, kind="ExternalInput").ap()
    for n in needed_out:
        shape, dt, _ = IO_SPECS[n]
        io[n] = nc.dram_tensor(n, shape, dt, kind="ExternalOutput").ap()
    for s in stages:
        BUILDERS[s](nc, io)
    return nc, needed_in, needed_out


_NP_DT = {F32: np.float32, BF16: BF}


def run_stages(stage_groups, host_shared, core_vals):
    for stages in stage_groups:
        nc, ins, outs = build_program(stages)
        in_maps = []
        for cv in core_vals:
            m = {}
            for n in ins:
                m[n] = cv[n] if IO_SPECS[n][2] else host_shared[n]
            in_maps.append(m)
        res = run_bass_kernel_spmd(nc, in_maps, core_ids=list(range(8)))
        for c, r in enumerate(res.results):
            for n in outs:
                core_vals[c][n] = r[n]
    return core_vals


ALL_STAGES = ["s1", "s2", "s3", "s4", "s5", "s6", "s7a", "s7b"]


def build_fused(stages=None):
    stages = stages or ALL_STAGES
    nc = bass.Bass("TRN2", target_bir_lowering=False)
    needed = []
    for s in stages:
        for n in STAGE_IN[s] + STAGE_OUT[s]:
            if n not in needed:
                needed.append(n)
    produced = {n for s in stages for n in STAGE_OUT[s]}
    io, ext_in = {}, []
    for n in needed:
        shape, dt, per_core = IO_SPECS[n]
        if n == "out_d":
            kind = "ExternalOutput"
        elif n in produced:
            kind = "Internal"
        else:
            kind = "ExternalInput"
            ext_in.append(n)
        io[n] = nc.dram_tensor(n, shape, dt, kind=kind).ap()
    bar = nc.alloc_semaphore(name="g_bar")
    for k, s in enumerate(stages):
        BUILDERS[s](nc, io)
        with nc.Block() as bb:
            for eng in ["sync", "vector", "scalar", "tensor", "gpsimd"]:
                def mk(eng=eng, k=k):
                    def f(q):
                        q.drain().then_inc(bar, 1)
                        q.wait_ge(bar, 5 * (k + 1))
                    return f
                getattr(bb, eng)(mk())
    return nc, ext_in


def kernel(**inputs):
    host_shared = _prep_host(inputs)
    core_vals = [_prep_core(inputs, b) for b in range(8)]
    nc, ext_in = build_fused()
    in_maps = []
    for cv in core_vals:
        m = {}
        for n in ext_in:
            m[n] = cv[n] if IO_SPECS[n][2] else host_shared[n]
        in_maps.append(m)
    res = run_bass_kernel_spmd(nc, in_maps, core_ids=list(range(8)))
    return np.stack([res.results[b]["out_d"] for b in range(8)], axis=0)


# version 04 (final)
# speedup vs baseline: 1.3343x; 1.1065x over previous
from contextlib import ExitStack
import numpy as np
import ml_dtypes
import concourse.bass as bass
import concourse.mybir as mybir
from concourse.bass_utils import run_bass_kernel_spmd

F32 = mybir.dt.float32
BF16 = mybir.dt.bfloat16
AF = mybir.ActivationFunctionType
ALU = mybir.AluOpType
AX = mybir.AxisListType

T = 8192
C = 1024
NT = T // 128
H, N = 16, 64
LCH = 64
NCH = T // LCH
BF = ml_dtypes.bfloat16



def _prep_host(inputs):
    g = {k: np.asarray(v) for k, v in inputs.items()}
    d = {}
    f32 = np.float32
    ct = np.ascontiguousarray

    d["c_ident_f32"] = np.eye(128, dtype=f32)
    d["c_ident_bf"] = np.eye(128, dtype=f32).astype(BF)
    consts = np.zeros((128, 8), f32)
    consts[:, 0] = 1e-5
    consts[:, 1] = 1.6e-4
    consts[:, 2] = 1.0
    consts[:, 3] = 1e-6
    consts[:, 4] = 0.5
    consts[:, 5] = -0.5
    consts[:, 6] = 2.0
    consts[:, 7] = -1e30
    d["c_consts"] = consts

    d["s2_gate_wt"] = ct(g["L0_ffn_gate_w"].T.astype(f32))
    d["s2_gate_bias"] = ct(np.tile(g["L0_ffn_gate_bias"].astype(f32)[None, :], (128, 1)))
    w1 = np.concatenate([g["L0_ffn_w1"], g["L0_ffn_sw1"][None]], 0)
    w3 = np.concatenate([g["L0_ffn_w3"], g["L0_ffn_sw3"][None]], 0)
    w2 = np.concatenate([g["L0_ffn_w2"], g["L0_ffn_sw2"][None]], 0)
    d["s2_w1t"] = ct(np.transpose(w1, (0, 2, 1)).astype(f32))
    d["s2_w3t"] = ct(np.transpose(w3, (0, 2, 1)).astype(f32))
    d["s2_w2t"] = ct(np.transpose(w2, (0, 2, 1)).astype(f32))

    d["s3_xA"] = ct(g["L0_attn_x_A"].astype(f32))
    d["s3_xB"] = ct(g["L0_attn_x_B"].astype(f32))
    lam = g["L0_attn_lambdas"].astype(f32).reshape(6, 1024)
    lamc = np.zeros((128, 48), f32)
    for gg in range(6):
        for fc in range(8):
            lamc[:, gg * 8 + fc] = lam[gg, fc * 128:(fc + 1) * 128]
    d["s3_lam"] = lamc
    xm = g["L0_attn_x_miu"].astype(f32).reshape(1024)
    d["s3_xmiu"] = ct(xm.reshape(8, 128).T)
    d["s3_wA"] = ct(g["L0_attn_w_A"].astype(f32))
    d["s3_kA"] = ct(g["L0_attn_k_A"].astype(f32))
    d["s3_aA"] = ct(g["L0_attn_a_A"].astype(f32))
    d["s3_gA"] = ct(g["L0_attn_g_A"].astype(f32))
    d["s3_wB"] = ct(g["L0_attn_w_B"].astype(f32))
    d["s3_kB"] = ct(g["L0_attn_k_B"].astype(f32))
    d["s3_aB"] = ct(g["L0_attn_a_B"].astype(f32))
    d["s3_gB"] = ct(g["L0_attn_g_B"].astype(f32))
    d["s3_wmiu_t"] = ct(np.tile(g["L0_attn_w_miu"].astype(f32).reshape(1, 1024), (128, 1)))
    d["s3_atime_t"] = ct(np.tile(g["L0_attn_a_time"].astype(f32).reshape(1, 1024), (128, 1)))
    d["s3_xtime_t"] = ct(np.tile(g["L0_attn_x_time"].astype(f32).reshape(1, 1024), (128, 1)))
    d["s3_rWt"] = ct(g["L0_attn_rW"].T.astype(f32))
    d["s3_kWt"] = ct(g["L0_attn_kW"].T.astype(f32))
    d["s3_vWt"] = ct(g["L0_attn_vW"].T.astype(f32))
    c3 = np.zeros((128, 4), f32)
    c3[0:32, 0] = 1.0
    c3[64:96, 1] = 1.0
    c3[32:64, 2] = 1.0
    c3[96:128, 3] = 1.0
    d["s3_c3aux"] = c3

    d["s5_oWt"] = ct(g["L0_attn_oW"].T.astype(f32))
    d["s5_gng_t"] = ct(np.tile(g["L0_attn_gn_gamma"].astype(f32)[None, :], (128, 1)))
    d["s5_gnb_t"] = ct(np.tile(g["L0_attn_gn_beta"].astype(f32)[None, :], (128, 1)))

    d["s6_rw"] = ct(g["L1_ffn_router_w"].astype(f32))
    d["s6_w1"] = ct(g["L1_ffn_w1"].astype(f32))
    b1 = g["L1_ffn_b1"].astype(f32)
    b1c = np.zeros((128, 224), f32)
    for e in range(14):
        for ic in range(16):
            b1c[:, e * 16 + ic] = b1[e, ic * 128:(ic + 1) * 128]
    d["s6_b1c"] = b1c
    d["s6_w2"] = ct(g["L1_ffn_w2"].astype(f32))
    d["s6_b2"] = ct(g["L1_ffn_b2"].astype(f32))

    d["c_ones_bf"] = np.ones((128, 128), f32).astype(BF)
    qkvw = g["L1_attn_qkv_w"].astype(f32)
    qkvb = g["L1_attn_qkv_b"].astype(f32)
    d["s7_qw"] = ct(qkvw[:, 0:1024])
    d["s7_kvw"] = ct(qkvw[:, 1024:3072])
    d["s7_qbc"] = ct(qkvb[0:1024].reshape(8, 128).T)
    d["s7_kvb_t"] = ct(np.tile(qkvb[1024:3072][None, :], (128, 1)))
    d["s7_ow"] = ct(g["L1_attn_out_w"].astype(f32))
    d["s7_ob_t"] = ct(np.tile(g["L1_attn_out_b"].astype(f32)[None, :], (128, 1)))
    d["s7_w1"] = ct(g["L1_attn_w1"].astype(f32))
    d["s7_b1c"] = ct(g["L1_attn_b1"].astype(f32).reshape(32, 128).T)
    d["s7_w2"] = ct(g["L1_attn_w2"].astype(f32))
    d["s7_b2_t"] = ct(np.tile(g["L1_attn_b2"].astype(f32)[None, :], (128, 1)))
    for nm, key in [("s7_g1_t", "L1_attn_g1"), ("s7_be1_t", "L1_attn_be1"),
                    ("s7_g2_t", "L1_attn_g2"), ("s7_be2_t", "L1_attn_be2")]:
        d[nm] = ct(np.tile(g[key].astype(f32)[None, :], (128, 1)))

    dm = np.zeros((128, 128), f32)
    dp = np.zeros((128, 128), f32)
    for cj in range(2):
        for s_ in range(64):
            j = cj * 64 + s_
            for ul in range(64):
                u = cj * 64 + ul
                dm[u, j] = (1.0 if ul <= s_ else 0.0) - (1.0 if ul <= 31 else 0.0)
                dp[u, j] = (1.0 if ul <= s_ - 1 else 0.0) - (1.0 if ul <= 31 else 0.0)
    d["s4_dmatM"] = dm
    d["s4_dmatP"] = dp
    mlo = np.tril(np.ones((64, 64), f32), -1)
    mupS = np.triu(np.ones((64, 64), f32), 1)
    mupI = np.triu(np.ones((64, 64), f32), 0)
    masks = np.zeros((64, 320), f32)
    masks[:, 0:64] = mlo
    masks[:, 64:128] = mupS
    masks[:, 128:192] = mupI
    masks[:, 192:256] = mupS
    masks[:, 256:320] = mupI
    d["s4_masks"] = masks
    mq = np.zeros((128, 640), f32)
    for tI, m in enumerate([mlo, mupS, mupI, mupS, mupI]):
        mq[:, tI * 128:tI * 128 + 128] = np.tile(m, (2, 2))
    d["s4_maskq"] = mq
    return d


def _prep_core(inputs, b):
    x = np.ascontiguousarray(np.asarray(inputs["L0_attn_x"])[b].astype(np.float32))
    return {"x": x}



IO_SPECS = {
    "x": ([T, C], F32, True),
    "xnT_f32": ([C, T], F32, True),
    "dbg_xn": ([T, C], F32, True),
    "dbg_cols": ([NT * 128, 12], F32, True),
    "xnT_bf": ([C, T], BF16, True),
    "f0": ([T, C], F32, True),
    "dbg_comb": ([128, NT, 12], F32, True),
    "c_ident_f32": ([128, 128], F32, False),
    "c_consts": ([128, 8], F32, False),
    "c_ident_bf": ([128, 128], BF16, False),
    "s2_gate_wt": ([C, 8], F32, False),
    "s2_gate_bias": ([128, 8], F32, False),
    "s2_w1t": ([9, C, 2048], F32, False),
    "s2_w3t": ([9, C, 2048], F32, False),
    "s2_w2t": ([9, 2048, C], F32, False),
}

for _n in ["s3_xA", "s3_wA", "s3_kA", "s3_aA", "s3_gA", "s3_wB", "s3_kB", "s3_aB",
           "s3_gB", "s3_rWt", "s3_kWt", "s3_vWt"]:
    pass
IO_SPECS.update({
    "s3_xA": ([C, 192], F32, False),
    "s3_xB": ([6, 32, C], F32, False),
    "s3_lam": ([128, 48], F32, False),
    "s3_xmiu": ([128, 8], F32, False),
    "s3_wA": ([C, 64], F32, False),
    "s3_kA": ([C, 64], F32, False),
    "s3_aA": ([C, 64], F32, False),
    "s3_gA": ([C, 128], F32, False),
    "s3_wB": ([64, C], F32, False),
    "s3_kB": ([64, C], F32, False),
    "s3_aB": ([64, C], F32, False),
    "s3_gB": ([128, C], F32, False),
    "s3_wmiu_t": ([128, C], F32, False),
    "s3_atime_t": ([128, C], F32, False),
    "s3_xtime_t": ([128, C], F32, False),
    "s3_rWt": ([C, C], F32, False),
    "s3_kWt": ([C, C], F32, False),
    "s3_vWt": ([C, C], F32, False),
    "s3_c3aux": ([128, 4], F32, False),
    "w_d": ([T, C], F32, True),
    "r_d": ([T, C], BF16, True),
    "k_d": ([T, C], BF16, True),
    "kk_d": ([T, C], BF16, True),
    "v_d": ([T, C], BF16, True),
    "b_d": ([T, C], BF16, True),
    "g_d": ([T, C], BF16, True),
    "bsum_d": ([T, 16], F32, True),
    "wlm_d": ([NCH, C], F32, True),
    "mcs_d": ([NCH, C], F32, True),
    "y_d": ([T, C], F32, True),
    "x1_d": ([T, C], F32, True),
    "xn1_d": ([T, C], F32, True),
    "xn1T_bf": ([C, T], BF16, True),
    "s5_oWt": ([C, C], F32, False),
    "s5_gng_t": ([128, C], F32, False),
    "s5_gnb_t": ([128, C], F32, False),
    "f1_d": ([T, C], F32, True),
    "s6_rw": ([C, 14], F32, False),
    "s6_w1": ([14, C, 2048], F32, False),
    "s6_b1c": ([128, 224], F32, False),
    "s6_w2": ([14, 2048, C], F32, False),
    "s6_b2": ([14, C], F32, False),
    "qfT_d": ([C, T], BF16, True),
    "kvz_d": ([128, 8 * 66], BF16, True),
    "out_d": ([T, C], F32, True),
    "c_ones_bf": ([128, 128], BF16, False),
    "s7_qw": ([C, C], F32, False),
    "s7_kvw": ([C, 2048], F32, False),
    "s7_qbc": ([128, 8], F32, False),
    "s7_kvb_t": ([128, 2048], F32, False),
    "s7_ow": ([C, C], F32, False),
    "s7_ob_t": ([128, C], F32, False),
    "s7_w1": ([C, 4096], F32, False),
    "s7_b1c": ([128, 32], F32, False),
    "s7_w2": ([4096, C], F32, False),
    "s7_b2_t": ([128, C], F32, False),
    "s7_g1_t": ([128, C], F32, False),
    "s7_be1_t": ([128, C], F32, False),
    "s7_g2_t": ([128, C], F32, False),
    "s7_be2_t": ([128, C], F32, False),
    "s4_dmatM": ([128, 128], F32, False),
    "s4_dmatP": ([128, 128], F32, False),
    "s4_masks": ([64, 320], F32, False),
    "s4_maskq": ([128, 640], F32, False),
})

STAGE_IN = {
    "s1": ["x", "c_ident_f32", "c_ident_bf", "c_consts"],
    "s2": ["xnT_bf", "s2_gate_wt", "s2_gate_bias", "s2_w1t", "s2_w3t", "s2_w2t"],
    "s3": ["xnT_bf", "c_consts", "s3_xA", "s3_xB", "s3_lam", "s3_xmiu", "s3_wA",
           "s3_kA", "s3_aA", "s3_gA", "s3_wB", "s3_kB", "s3_aB", "s3_gB",
           "s3_wmiu_t", "s3_atime_t", "s3_xtime_t", "s3_rWt", "s3_kWt", "s3_vWt",
           "s3_c3aux"],
    "s5": ["y_d", "bsum_d", "v_d", "g_d", "x", "f0", "c_consts", "c_ident_bf",
           "s5_oWt", "s5_gng_t", "s5_gnb_t"],
    "s6": ["xn1T_bf", "c_ident_bf", "s6_rw", "s6_w1", "s6_b1c", "s6_w2", "s6_b2"],
    "s4": ["w_d", "r_d", "k_d", "kk_d", "v_d", "b_d", "wlm_d", "mcs_d",
           "c_ident_bf", "s4_dmatM", "s4_dmatP", "s4_masks", "s4_maskq"],
    "s7a": ["xn1_d", "c_consts", "c_ident_bf", "c_ones_bf", "s7_qw", "s7_kvw",
            "s7_qbc", "s7_kvb_t", "s7_g1_t", "s7_be1_t"],
    "s7b": ["qfT_d", "kvz_d", "xn1_d", "x1_d", "f1_d", "c_consts", "c_ident_bf",
            "c_ones_bf", "s7_ow", "s7_ob_t", "s7_w1", "s7_b1c", "s7_w2", "s7_b2_t",
            "s7_g2_t", "s7_be2_t"],
}
STAGE_OUT = {
    "s1": ["xnT_f32", "xnT_bf"],
    "s2": ["f0"],
    "s3": ["w_d", "r_d", "k_d", "kk_d", "v_d", "b_d", "g_d", "bsum_d",
           "wlm_d", "mcs_d"],
    "s5": ["x1_d", "xn1_d", "xn1T_bf"],
    "s6": ["f1_d"],
    "s4": ["y_d"],
    "s7a": ["qfT_d", "kvz_d"],
    "s7b": ["out_d"],
}



def build_s1(nc, io):
    x = io["x"]
    xnT = io["xnT_f32"]
    xnTb = io["xnT_bf"]
    idf = io["c_ident_f32"]
    idb = io["c_ident_bf"]
    ccd = io["c_consts"]

    with (
        nc.sbuf_tensor("s1_cc", [128, 8], F32) as cc,
        nc.sbuf_tensor("s1_xin", [128, 2, C], F32) as xin,
        nc.sbuf_tensor("s1_xn", [128, 2, C], F32) as xn,
        nc.sbuf_tensor("s1_xnb", [128, 2, C], BF16) as xnb,
        nc.sbuf_tensor("s1_sq", [128, C], F32) as sq,
        nc.sbuf_tensor("s1_cols", [128, 2, 8], F32) as cols,
        nc.sbuf_tensor("s1_scol", [128, 2, 4], F32) as scol,
        nc.sbuf_tensor("s1_otf", [128, 2, 8, 128], F32) as otf,
        nc.sbuf_tensor("s1_otb", [128, 2, 8, 128], BF16) as otb,
        nc.sbuf_tensor("s1_idf", [128, 128], F32) as sidf,
        nc.sbuf_tensor("s1_idb", [128, 128], BF16) as sidb,
        nc.psum_tensor("s1_ptf", [128, 2, 2, 512], F32) as ptf,
        nc.psum_tensor("s1_ptb", [128, 2, 1024], BF16) as ptb,
    ):
        ld = nc.alloc_semaphore(name="s1_ld")
        vm = nc.alloc_semaphore(name="s1_vm")
        am = nc.alloc_semaphore(name="s1_am")
        tm = nc.alloc_semaphore(name="s1_tm")
        st = nc.alloc_semaphore(name="s1_st")
        blkctx = nc.Block()
        block = blkctx.__enter__()
        def vm_v(i, k):
            return 4 * i + k

        def am_v(i, k):
            return 2 * i + k

        @block.sync
        def _(sync):
            sync.dma_start(out=sidf[:], in_=idf[:, :]).then_inc(ld, 16)
            sync.dma_start(out=sidb[:], in_=idb[:, :]).then_inc(ld, 16)
            sync.dma_start(out=cc[:], in_=ccd[:, :]).then_inc(ld, 16)
            for i in range(NT):
                if i >= 2:
                    sync.wait_ge(vm, vm_v(i - 2, 3))
                sync.dma_start(out=xin[:, i % 2, :],
                               in_=x[i * 128:(i + 1) * 128, :]).then_inc(ld, 16)

        @block.scalar
        def _(scalar):
            for i in range(NT):
                p = i % 2
                scalar.wait_ge(ld, 48 + 16 * (i + 1))
                if i >= 1:
                    scalar.wait_ge(vm, vm_v(i - 1, 2))
                scalar.activation(sq[:, :], xin[:, p, :], AF.Square,
                                  accum_out=scol[:, p, 0:1])
                scalar.drain().then_inc(am, 1)
                scalar.wait_ge(vm, vm_v(i, 2))
                scalar.activation(scol[:, p, 2:3], cols[:, p, 5:6], AF.Ln, bias=cc[:, 0:1])
                scalar.drain()
                scalar.activation(scol[:, p, 1:2], scol[:, p, 2:3], AF.Exp, scale=-0.5)
                scalar.drain().then_inc(am, 1)

        @block.vector
        def _(vector):
            for i in range(NT):
                p = i % 2
                vector.wait_ge(ld, 48 + 16 * (i + 1))
                xi = xin[:, p, :]
                vector.reduce_sum(cols[:, p, 0:1], xi, AX.X).then_inc(vm, 1)
                vector.wait_ge(am, am_v(i, 1))
                vector.drain()
                vector.tensor_scalar(cols[:, p, 2:3], cols[:, p, 0:1], 1.0 / C, None, ALU.mult)
                vector.drain()
                vector.tensor_tensor(cols[:, p, 4:5], cols[:, p, 0:1], cols[:, p, 2:3], ALU.mult)
                vector.drain()
                vector.tensor_tensor(cols[:, p, 3:4], scol[:, p, 0:1], cols[:, p, 4:5], ALU.subtract)
                vector.drain()
                vector.tensor_scalar(cols[:, p, 5:6], cols[:, p, 3:4], 1.0 / C, None,
                                     ALU.mult).then_inc(vm, 1)
                vector.wait_ge(am, am_v(i, 2))
                vector.drain()
                vector.tensor_scalar(xn[:, p, :], xi, cols[:, p, 2:3], scol[:, p, 1:2],
                                     ALU.subtract, ALU.mult)
                vector.drain()
                vector.tensor_copy(xnb[:, p, :], xn[:, p, :]).then_inc(vm, 1)
                vector.wait_ge(tm, i + 1)
                if i >= 2:
                    vector.wait_ge(st, 32 * (i - 1))
                vector.tensor_copy(otf[:, p, 0:4, :].rearrange("p a b -> p (a b)"),
                                   ptf[:, p, 0, :])
                vector.tensor_copy(otf[:, p, 4:8, :].rearrange("p a b -> p (a b)"),
                                   ptf[:, p, 1, :])
                vector.tensor_copy(otb[:, p, :, :].rearrange("p a b -> p (a b)"),
                                   ptb[:, p, :]).then_inc(vm, 1)

        @block.tensor
        def _(tensor):
            tensor.wait_ge(ld, 48)
            for i in range(NT):
                p = i % 2
                tensor.wait_ge(vm, vm_v(i, 3))
                if i >= 2:
                    tensor.wait_ge(vm, vm_v(i - 2, 4))
                for j in range(8):
                    tensor.transpose(ptf[:, p, j // 4, (j % 4) * 128:(j % 4 + 1) * 128],
                                     xn[:, p, j * 128:(j + 1) * 128], sidf[:])
                mm = None
                for j in range(8):
                    mm = tensor.transpose(ptb[:, p, j * 128:(j + 1) * 128],
                                          xnb[:, p, j * 128:(j + 1) * 128], sidb[:])
                mm.then_inc(tm, 1)

        @block.gpsimd
        def _(gpsimd):
            for i in range(NT):
                p = i % 2
                gpsimd.wait_ge(vm, vm_v(i, 4))
                gpsimd.dma_start(
                    out=xnT.rearrange("(fc f) t -> f fc t", f=128)[:, :, i * 128:(i + 1) * 128],
                    in_=otf[:, p, :, :]).then_inc(st, 16)
                gpsimd.dma_start(
                    out=xnTb.rearrange("(fc f) t -> f fc t", f=128)[:, :, i * 128:(i + 1) * 128],
                    in_=otb[:, p, :, :]).then_inc(st, 16)
            gpsimd.wait_ge(st, 32 * NT)
        blkctx.__exit__(None, None, None)
    return nc



def build_s2(nc, io):
    xnTb = io["xnT_bf"]
    f0 = io["f0"]
    gate_wt = io["s2_gate_wt"]
    gate_bias = io["s2_gate_bias"]
    w1t = io["s2_w1t"]
    w3t = io["s2_w3t"]
    w2t = io["s2_w2t"]
    NE = 9
    NB = 16

    with ExitStack() as _es:
        ec = _es.enter_context
        xg = ec(nc.sbuf_tensor("s2_xg", [128, 2, 8, 128], BF16))
        gw = ec(nc.sbuf_tensor("s2_gw", [128, 8, 8], BF16))
        gb = ec(nc.sbuf_tensor("s2_gb", [128, 8], F32))
        gsc = ec(nc.sbuf_tensor("s2_gsc", [128, NT, 48], F32))
        comb = ec(nc.sbuf_tensor("s2_comb", [128, NT, 12], F32))
        xT = ec(nc.sbuf_tensor("s2_xT", [128, 2, 8, 512], BF16))
        w13 = ec(nc.sbuf_tensor("s2_w13", [128, 2, 8, 16, 128], BF16))
        w2s = ec(nc.sbuf_tensor("s2_w2", [128, 16, 1024], BF16))
        s1 = ec(nc.sbuf_tensor("s2_s1", [128, 2, 512], F32))
        hT = ec(nc.sbuf_tensor("s2_hT", [128, 16, 512], BF16))
        ft = ec(nc.sbuf_tensor("s2_ft", [128, 2, 4, 1024], F32))
        ph = ec(nc.psum_tensor("s2_ph", [128, 4, 512], F32))
        py = ec(nc.psum_tensor("s2_py", [128, 2, 512], F32))
        pg = ec(nc.psum_tensor("s2_pg", [128, 2, 512], F32))
        lgp = [nc.alloc_semaphore(name=f"s2_lg{p}") for p in range(2)]
        ltp = [nc.alloc_semaphore(name=f"s2_lt{p}") for p in range(2)]
        ldc = nc.alloc_semaphore(name="s2_ldc")
        lw = nc.alloc_semaphore(name="s2_lw")
        tm = nc.alloc_semaphore(name="s2_tm")
        vm = nc.alloc_semaphore(name="s2_vm")
        am = nc.alloc_semaphore(name="s2_am")
        st = nc.alloc_semaphore(name="s2_st")
        stb = [nc.alloc_semaphore(name=f"s2_stb{b}") for b in range(NB)]
        block = ec(nc.Block())
        TM0, VM0, AM0 = NT, 2 * NT, NT

        def tm_h(e, b, ic):
            return TM0 + e * 384 + b * 24 + ic + 1

        def tm_w2(e, b, j):
            return TM0 + e * 384 + b * 24 + 16 + j + 1

        def vm_hmult(gh):
            eb, ic = divmod(gh, 16)
            return VM0 + eb * 24 + ic + 1

        def vm_evac(gy):
            eb, j = divmod(gy, 8)
            return VM0 + eb * 24 + 16 + j + 1

        def am_silu(gh):
            return AM0 + gh + 1

        @block.sync
        def _(sync):
            sync.dma_start(out=gb[:], in_=gate_bias[:, :]).then_inc(ldc, 16)
            for i in range(NT):
                if i >= 2:
                    sync.wait_ge(tm, i - 1)
                sync.dma_start(out=xg[:, i % 2, :, :],
                               in_=xnTb.rearrange("(fc f) t -> f fc t", f=128)
                               [:, :, i * 128:(i + 1) * 128]).then_inc(lgp[i % 2], 16)
            for e in range(NE):
                for b in range(NB):
                    g = e * NB + b
                    if g >= 2:
                        sync.wait_ge(tm, TM0 + (g - 1) * 24)
                    sync.dma_start(out=xT[:, g % 2, :, :],
                                   in_=xnTb.rearrange("(fc f) t -> f fc t", f=128)
                                   [:, :, b * 512:(b + 1) * 512]).then_inc(ltp[g % 2], 16)

        @block.gpsimd
        def _(gpsimd):
            gpsimd.memset(comb[:, :, 8:9], 1.0)
            gpsimd.dma_start(out=gw[:],
                             in_=gate_wt.rearrange("(fc f) e -> f fc e", f=128)).then_inc(lw, 16)
            for e in range(NE):
                if e >= 1:
                    gpsimd.wait_ge(tm, TM0 + e * 384)
                gpsimd.dma_start(out=w13[:, 0, :, :, :],
                                 in_=w1t[e].rearrange("(fc f) (ic i) -> f fc ic i",
                                                      f=128, i=128)).then_inc(lw, 16)
                gpsimd.dma_start(out=w13[:, 1, :, :, :],
                                 in_=w3t[e].rearrange("(fc f) (ic i) -> f fc ic i",
                                                      f=128, i=128)).then_inc(lw, 16)
                gpsimd.dma_start(out=w2s[:, :, :],
                                 in_=w2t[e].rearrange("(ic i) o -> i ic o",
                                                      i=128)).then_inc(lw, 16)
                for b in range(NB):
                    gst = e * NB + b
                    gpsimd.wait_ge(vm, vm_evac(gst * 8 + 7))
                    if e >= 1:
                        gpsimd.wait_ge(stb[b], 16 * e)
                    dst = f0.rearrange("(bb p) c -> p bb c", p=128)[:, 4 * b:4 * b + 4, :]
                    if e == 0:
                        gpsimd.dma_start(out=dst, in_=ft[:, b % 2, :, :]).then_inc(stb[b], 16)
                    else:
                        gpsimd.dma_start(out=dst, in_=ft[:, b % 2, :, :],
                                         accum_op=ALU.add).then_inc(stb[b], 16)
            for b in range(NB):
                gpsimd.wait_ge(stb[b], 16 * NE)

        @block.tensor
        def _(tensor):
            tensor.wait_ge(ldc, 16)
            tensor.wait_ge(lw, 16)
            for i in range(NT):
                tensor.wait_ge(lgp[i % 2], 16 * (i // 2 + 1))
                if i >= 2:
                    tensor.wait_ge(vm, 2 * (i - 2) + 1)
                mm = None
                for k in range(8):
                    mm = tensor.matmul(pg[:, i % 2, 0:8], xg[:, i % 2, k, :], gw[:, k, :],
                                       start=(k == 0), stop=(k == 7))
                mm.then_inc(tm, 1)
            for e in range(NE):
                tensor.wait_ge(lw, 16 + 48 * (e + 1))
                for b in range(NB):
                    g = e * NB + b
                    p = g % 2
                    tensor.wait_ge(ltp[p], 16 * (g // 2 + 1))
                    for ic in range(16):
                        gh = g * 16 + ic
                        bk = 2 * (gh % 2)
                        if gh >= 2:
                            tensor.wait_ge(am, am_silu(gh - 2))
                            tensor.wait_ge(vm, vm_hmult(gh - 2))
                        for fc in range(8):
                            tensor.matmul(ph[:, bk, :], w13[:, 0, fc, ic, :],
                                          xT[:, p, fc, :], start=(fc == 0), stop=(fc == 7))
                        mm = None
                        for fc in range(8):
                            mm = tensor.matmul(ph[:, bk + 1, :], w13[:, 1, fc, ic, :],
                                               xT[:, p, fc, :], start=(fc == 0), stop=(fc == 7))
                        mm.then_inc(tm, 1)
                    tensor.wait_ge(vm, vm_hmult(g * 16 + 15))
                    for j in range(8):
                        s, oc = divmod(j, 2)
                        gy = g * 8 + j
                        if gy >= 2:
                            tensor.wait_ge(vm, vm_evac(gy - 2))
                        mm = None
                        for ic in range(16):
                            mm = tensor.matmul(py[:, gy % 2, :],
                                               hT[:, ic, s * 128:(s + 1) * 128],
                                               w2s[:, ic, oc * 512:(oc + 1) * 512],
                                               start=(ic == 0), stop=(ic == 15))
                        mm.then_inc(tm, 1)

        @block.scalar
        def _(scalar):
            for i in range(NT):
                scalar.wait_ge(vm, 2 * i + 1)
                scalar.activation(gsc[:, i, 16:24], gsc[:, i, 1:9], AF.Exp)
                scalar.drain().then_inc(am, 1)
            for e in range(NE):
                for b in range(NB):
                    g = e * NB + b
                    for ic in range(16):
                        gh = g * 16 + ic
                        bk = 2 * (gh % 2)
                        scalar.wait_ge(tm, tm_h(e, b, ic))
                        if gh >= 2:
                            scalar.wait_ge(vm, vm_hmult(gh - 2))
                        scalar.activation(s1[:, gh % 2, :], ph[:, bk, :], AF.Silu)
                        scalar.drain().then_inc(am, 1)

        @block.vector
        def _(vector):
            for i in range(NT):
                vector.wait_ge(tm, i + 1)
                pgi = pg[:, i % 2, 0:8]
                vector.reduce_max(gsc[:, i, 0:1], pgi, AX.X)
                vector.drain()
                vector.tensor_scalar(gsc[:, i, 1:9], pgi, gsc[:, i, 0:1], None,
                                     ALU.subtract).then_inc(vm, 1)
                vector.wait_ge(am, i + 1)
                exe = gsc[:, i, 16:24]
                vector.reduce_sum(gsc[:, i, 9:10], exe, AX.X)
                vector.drain()
                vector.reciprocal(gsc[:, i, 10:11], gsc[:, i, 9:10])
                vector.drain()
                vector.tensor_scalar(gsc[:, i, 24:32], exe, gsc[:, i, 10:11], None, ALU.mult)
                vector.drain()
                vector.tensor_tensor(gsc[:, i, 32:40], gsc[:, i, 24:32], gb[:, :], ALU.add)
                vector.drain()
                vector.reduce_max(gsc[:, i, 11:12], gsc[:, i, 32:40], AX.X)
                vector.drain()
                vector.tensor_scalar(gsc[:, i, 1:9], gsc[:, i, 32:40], gsc[:, i, 11:12], None, ALU.is_ge)
                vector.drain()
                vector.scalar_tensor_tensor(gsc[:, i, 16:24], gsc[:, i, 1:9], -1e30,
                                            gsc[:, i, 32:40], ALU.mult, ALU.add)
                vector.drain()
                vector.reduce_max(gsc[:, i, 12:13], gsc[:, i, 16:24], AX.X)
                vector.drain()
                vector.tensor_scalar(gsc[:, i, 32:40], gsc[:, i, 16:24], gsc[:, i, 12:13], None, ALU.is_ge)
                vector.drain()
                vector.tensor_tensor(gsc[:, i, 1:9], gsc[:, i, 1:9], gsc[:, i, 32:40], ALU.add)
                vector.drain()
                vector.tensor_tensor(comb[:, i, 0:8], gsc[:, i, 24:32], gsc[:, i, 1:9],
                                     ALU.mult).then_inc(vm, 1)
            for e in range(NE):
                for b in range(NB):
                    g = e * NB + b
                    for ic in range(16):
                        gh = g * 16 + ic
                        bk = 2 * (gh % 2)
                        vector.wait_ge(am, am_silu(gh))
                        vector.tensor_tensor(hT[:, ic, :], s1[:, gh % 2, :],
                                             ph[:, bk + 1, :], ALU.mult).then_inc(vm, 1)
                    for j in range(8):
                        s, oc = divmod(j, 2)
                        gy = g * 8 + j
                        vector.wait_ge(tm, tm_w2(e, b, j))
                        if j == 0 and g >= 2:
                            bp, ep = (b - 2, e) if b >= 2 else (b + 14, e - 1)
                            vector.wait_ge(stb[bp], 16 * (ep + 1))
                        vector.tensor_scalar(ft[:, b % 2, s, oc * 512:(oc + 1) * 512],
                                             py[:, gy % 2, :],
                                             comb[:, 4 * b + s, e:e + 1], None,
                                             ALU.mult).then_inc(vm, 1)
    return nc



_ENG_OF = {"sy": "sync", "sc": "scalar", "scd": "scalar", "ve": "vector",
           "te": "tensor", "gp": "gpsimd", "gpd": "gpsimd"}
_IS_DMA = {"sy": True, "sc": False, "scd": True, "ve": False, "te": False,
           "gp": False, "gpd": True}


_SCHED_N = [0]


class Sched:
    def __init__(self):
        _SCHED_N[0] += 1
        self.uid = _SCHED_N[0]
        self.steps = []
        self.seq = []

    def step(self, eng, fn, deps=(), no_drain=False):
        sid = len(self.steps)
        self.steps.append((eng, fn, [d for d in deps if d is not None], no_drain))
        return sid

    def emit(self, nc, block):
        cnt = {}
        for eng, _, _, _ in self.steps:
            cnt[eng] = cnt.get(eng, 0) + 1
            self.seq.append(cnt[eng])
        sems = {eng: nc.alloc_semaphore(name=f"sch_{eng}_{self.uid}")
                for eng in cnt}
        phys = sorted({_ENG_OF[e] for e in cnt})

        def section(qname):
            def body(q):
                for sid, (eng, fn, deps, no_drain) in enumerate(self.steps):
                    if _ENG_OF[eng] != qname:
                        continue
                    need_drain = False
                    waits = {}
                    for d in deps:
                        peng = self.steps[d][0]
                        pseq = self.seq[d]
                        if peng == eng and not _IS_DMA[eng] and not _IS_DMA[peng]:
                            if not (no_drain or eng == "te"):
                                need_drain = True
                            continue
                        if _ENG_OF[peng] == qname and not _IS_DMA[peng] and _IS_DMA[eng]:
                            pass
                        v = pseq * (16 if _IS_DMA[peng] else 1)
                        waits[peng] = max(waits.get(peng, 0), v)
                    if _IS_DMA[eng] and self.seq[sid] > 1:
                        waits[eng] = max(waits.get(eng, 0), (self.seq[sid] - 1) * 16)
                    for peng, v in waits.items():
                        if peng == eng and not _IS_DMA[eng]:
                            continue
                        q.wait_ge(sems[peng], v)
                    if need_drain:
                        q.drain()
                    inst = fn(q)
                    assert inst is not None, f"step {sid} on {eng} returned no instruction"
                    inst.then_inc(sems[eng], 16 if _IS_DMA[eng] else 1)
            return body

        for qname in phys:
            getattr(block, qname)(section(qname))

        def final(q):
            for eng in sorted(cnt):
                if _IS_DMA[eng]:
                    q.wait_ge(sems[eng], cnt[eng] * 16)
        block.sync(final)
        return sems, cnt


class Res:
    __slots__ = ("w", "r")

    def __init__(self):
        self.w = None
        self.r = []


def mk_S(sch):
    def S(eng, fn, reads=(), writes=(), extra=(), no_drain=False):
        deps = list(extra)
        for r in reads:
            if r.w is not None:
                deps.append(r.w)
        for w in writes:
            deps.extend(w.r)
            if w.w is not None:
                deps.append(w.w)
        sid = sch.step(eng, fn, deps, no_drain=no_drain)
        for r in reads:
            r.r.append(sid)
        for w in writes:
            w.w = sid
            w.r = []
        return sid
    return S



def build_s3(nc, io):
    xnTb = io["xnT_bf"]
    NB = 16
    o_w, o_r, o_k, o_kk = io["w_d"], io["r_d"], io["k_d"], io["kk_d"]
    o_v, o_b, o_g = io["v_d"], io["b_d"], io["g_d"]
    o_bs, o_wlm, o_mcs = io["bsum_d"], io["wlm_d"], io["mcs_d"]
    G_ORDER = [0, 3, 5, 1, 4, 2]

    with ExitStack() as _es:
        ec = _es.enter_context
        sb = lambda *a: ec(nc.sbuf_tensor(*a))
        cc = sb("s3_cc", [128, 8], F32)
        xa = sb("s3s_xa", [128, 8, 192], BF16)
        xb = sb("s3s_xb", [128, 2, 1024], BF16)
        lam = sb("s3s_lam", [128, 48], F32)
        xmiu = sb("s3s_xmiu", [128, 8], F32)
        wa = sb("s3s_wa", [128, 8, 64], BF16)
        ka = sb("s3s_ka", [128, 8, 64], BF16)
        aa = sb("s3s_aa", [128, 8, 64], BF16)
        ga = sb("s3s_ga", [128, 8, 128], BF16)
        wbb = sb("s3s_wb", [64, 1024], BF16)
        kbb = sb("s3s_kb", [64, 1024], BF16)
        abb = sb("s3s_ab", [64, 1024], BF16)
        gbb = sb("s3s_gb", [128, 1024], BF16)
        wmiu = sb("s3s_wmiu", [128, 1024], F32)
        atim = sb("s3s_atim", [128, 1024], F32)
        xtim = sb("s3s_xtim", [128, 1024], F32)
        rwt = sb("s3s_rwt", [128, 8, 1024], BF16)
        kwt = sb("s3s_kwt", [128, 8, 1024], BF16)
        vwt = sb("s3s_vwt", [128, 8, 1024], BF16)
        c3 = sb("s3s_c3", [128, 4], F32)
        win = sb("s3_win", [128, 8, 513], BF16)
        dxt = sb("s3_dx", [128, 8, 512], BF16)
        xmt = sb("s3_xm", [128, 8, 512], BF16)
        mixT = sb("s3_mix", [128, 2, 512], BF16)
        xd = sb("s3_xd", [128, 8, 512], BF16)
        mt = sb("s3_mt", [128, 4, 512], F32)
        lr = sb("s3_lr", [128, 2, 512], BF16)
        rstg = sb("s3_rstg", [128, 4, 1024], BF16)
        kstg = sb("s3_kstg", [128, 4, 1024], BF16)
        wstg = sb("s3_wstg", [128, 4, 1024], F32)
        sig = sb("s3_sig", [128, 4, 1024], BF16)
        vstg = sb("s3_vstg", [128, 1024], BF16)
        gstg = sb("s3_gstg", [128, 1024], BF16)
        kkstg = sb("s3_kkstg", [128, 1024], BF16)
        bstg = sb("s3_bstg", [128, 1024], BF16)
        cs = sb("s3_cs", [128, 2, 1024], F32)
        cs2 = sb("s3_cs2", [128, 1024], F32)
        sqd = sb("s3_sqd", [128, 1024], F32)
        kcol = sb("s3_kcol", [128, 2, 8], F32)
        bscol = sb("s3_bs", [128, 2, 16], F32)
        auxs = sb("s3_aux", [4, 2, 1024], F32)
        zrow = sb("s3_zrow", [1, 1024], F32)
        pm = ec(nc.psum_tensor("s3_pm", [128, 2, 512], F32))
        pxd = ec(nc.psum_tensor("s3_pxd", [128, 4, 512], F32))
        ppj = ec(nc.psum_tensor("s3_ppj", [128, 2, 512], F32))

        sch = Sched()
        S = mk_S(sch)
        r_win, r_dx, r_xm, r_xd, r_sqd = Res(), Res(), Res(), Res(), Res()
        r_mix = [Res(), Res()]
        r_pm = [Res(), Res()]
        r_pxd = [Res() for _ in range(4)]
        r_ppj = [Res(), Res()]
        r_mt = [Res() for _ in range(4)]
        r_lr = [Res() for _ in range(4)]
        r_rstg, r_kstg, r_wstg, r_sig = Res(), Res(), Res(), Res()
        r_vstg, r_gstg, r_kkstg, r_bstg = Res(), Res(), Res(), Res()
        r_cs = [Res(), Res()]
        r_cs2 = [Res(), Res()]
        r_kcol = [Res(), Res()]
        r_bscol = [Res(), Res()]
        r_aux = [Res(), Res()]
        r_cst = Res()

        def ld(q, dst, src):
            return q.dma_start(out=dst, in_=src)

        S("gpd", lambda q: ld(q, xa[:], io["s3_xA"].rearrange("(fc f) j -> f fc j", f=128)),
          writes=[r_cst])
        for g in range(6):
            S("gpd", lambda q, g=g: ld(q, xb[32 * (g % 3):32 * (g % 3) + 32, g // 3, :],
                                       io["s3_xB"][g]), writes=[r_cst])
        for nm, dst in [("s3_wA", wa), ("s3_kA", ka), ("s3_aA", aa)]:
            S("gpd", lambda q, nm=nm, dst=dst: ld(
                q, dst[:], io[nm].rearrange("(fc f) j -> f fc j", f=128)), writes=[r_cst])
        S("gpd", lambda q: ld(q, ga[:], io["s3_gA"].rearrange("(fc f) j -> f fc j", f=128)),
          writes=[r_cst])
        for nm, dst in [("s3_wB", wbb), ("s3_kB", kbb), ("s3_aB", abb)]:
            S("gpd", lambda q, nm=nm, dst=dst: ld(q, dst[:], io[nm][:, :]), writes=[r_cst])
        S("gpd", lambda q: ld(q, gbb[:], io["s3_gB"][:, :]), writes=[r_cst])
        for nm, dst in [("s3_rWt", rwt), ("s3_kWt", kwt), ("s3_vWt", vwt)]:
            S("gpd", lambda q, nm=nm, dst=dst: ld(
                q, dst[:], io[nm].rearrange("(fc f) o -> f fc o", f=128)), writes=[r_cst])
        for nm, dst in [("s3_lam", lam), ("s3_xmiu", xmiu), ("s3_wmiu_t", wmiu),
                        ("s3_atime_t", atim), ("s3_xtime_t", xtim), ("s3_c3aux", c3),
                        ("c_consts", cc)]:
            S("sy", lambda q, nm=nm, dst=dst: ld(q, dst[:], io[nm][:, :]), writes=[r_cst])

        sz = S("ve", lambda q: q.memset(zrow[:], 0.0))
        S("scd", lambda q: ld(q, o_mcs[NCH - 1:NCH, :], zrow[:]), extra=[sz])
        S("ve", lambda q: q.memset(win[:, :, 0:1], 0.0), writes=[r_win])

        xnTr = xnTb.rearrange("(fc f) t -> f fc t", f=128)

        for b in range(NB):
            t0 = b * 512
            if b == 0:
                S("sy", lambda q: ld(q, win[:, :, 1:513], xnTr[:, :, 0:512]),
                  reads=[r_cst], writes=[r_win])
            else:
                S("sy", lambda q, t0=t0: ld(q, win[:, :, 0:513],
                                            xnTr[:, :, t0 - 1:t0 + 512]),
                  writes=[r_win])
            S("ve", lambda q: q.tensor_tensor(dxt[:, :, :], win[:, :, 0:512],
                                              win[:, :, 1:513], ALU.subtract),
              reads=[r_win], writes=[r_dx])

            def xm_fn(q):
                ii = None
                for fc in range(8):
                    ii = q.scalar_tensor_tensor(xmt[:, fc, :], dxt[:, fc, :],
                                                xmiu[:, fc:fc + 1], win[:, fc, 1:513],
                                                ALU.mult, ALU.add)
                return ii
            S("ve", xm_fn, reads=[r_dx, r_win, r_cst], writes=[r_xm])

            for cg in range(2):
                def mix_mm(q, cg=cg):
                    ii = None
                    for gi in range(3):
                        g = cg * 3 + gi
                        for fc in range(8):
                            ii = q.matmul(pm[32 * gi:32 * gi + 32, cg, :],
                                          xa[:, fc, 32 * g:32 * g + 32],
                                          xmt[:, fc, :],
                                          start=(fc == 0), stop=(fc == 7))
                    return ii
                S("te", mix_mm, reads=[r_xm, r_cst], writes=[r_pm[cg]])

                def mix_tanh(q, cg=cg):
                    ii = None
                    for gi in range(3):
                        ii = q.activation(mixT[32 * gi:32 * gi + 32, cg, :],
                                          pm[32 * gi:32 * gi + 32, cg, :], AF.Tanh)
                    return ii
                S("sc", mix_tanh, reads=[r_pm[cg]], writes=[r_mix[cg]])

            for g in G_ORDER:
                gi, cg = g % 3, g // 3
                for wv in range(2):
                    fcs = tuple(range(4 * wv, 4 * wv + 4))

                    def miu_mm(q, gi=gi, cg=cg, fcs=fcs):
                        ii = None
                        for fc in fcs:
                            ii = q.matmul(pxd[:, fc % 4, :],
                                          xb[32 * gi:32 * gi + 32, cg,
                                             fc * 128:(fc + 1) * 128],
                                          mixT[32 * gi:32 * gi + 32, cg, :],
                                          start=True, stop=True)
                        return ii
                    S("te", miu_mm, reads=[r_mix[cg], r_cst],
                      writes=[r_pxd[fc % 4] for fc in fcs])

                    def miu_stt(q, g=g, fcs=fcs):
                        ii = None
                        for fc in fcs:
                            ii = q.scalar_tensor_tensor(mt[:, fc % 4, :], pxd[:, fc % 4, :],
                                                        lam[:, g * 8 + fc:g * 8 + fc + 1],
                                                        dxt[:, fc, :], ALU.add, ALU.mult)
                        return ii
                    S("ve", miu_stt, reads=[r_pxd[fc % 4] for fc in fcs] + [r_dx, r_cst],
                      writes=[r_mt[fc % 4] for fc in fcs])

                    def xd_add(q, fcs=fcs):
                        ii = None
                        for fc in fcs:
                            ii = q.tensor_tensor(xd[:, fc, :], mt[:, fc % 4, :],
                                                 win[:, fc, 1:513], ALU.add)
                        return ii
                    S("ve", xd_add, reads=[r_mt[fc % 4] for fc in fcs] + [r_win],
                      writes=[r_xd])

                if g in (0, 3):
                    wt = rwt if g == 0 else vwt
                    for s in range(4):
                        def proj_mm(q, wt=wt, s=s):
                            ii = None
                            for oc in range(2):
                                for fc in range(8):
                                    ii = q.matmul(ppj[:, oc, :],
                                                  xd[:, fc, s * 128:(s + 1) * 128],
                                                  wt[:, fc, oc * 512:(oc + 1) * 512],
                                                  start=(fc == 0), stop=(fc == 7))
                            return ii
                        S("te", proj_mm, reads=[r_xd, r_cst], writes=[r_ppj[0], r_ppj[1]])
                        if g == 0:
                            def ev_r(q, s=s):
                                q.tensor_copy(rstg[:, s, 0:512], ppj[:, 0, :])
                                return q.tensor_copy(rstg[:, s, 512:1024], ppj[:, 1, :])
                            S("ve", ev_r, reads=[r_ppj[0], r_ppj[1]], writes=[r_rstg])
                            S("gpd", lambda q, t0=t0, s=s: ld(
                                q, o_r[t0 + s * 128:t0 + (s + 1) * 128, :], rstg[:, s, :]),
                                reads=[r_rstg])
                        else:
                            def ev_v(q):
                                q.tensor_copy(vstg[:, 0:512], ppj[:, 0, :])
                                return q.tensor_copy(vstg[:, 512:1024], ppj[:, 1, :])
                            S("ve", ev_v, reads=[r_ppj[0], r_ppj[1]], writes=[r_vstg])
                            S("scd", lambda q, t0=t0, s=s: ld(
                                q, o_v[t0 + s * 128:t0 + (s + 1) * 128, :], vstg[:]),
                                reads=[r_vstg])
                elif g == 5:
                    def g1_mm(q):
                        ii = None
                        for fc in range(8):
                            ii = q.matmul(pxd[:, 0, :], ga[:, fc, :], xd[:, fc, :],
                                          start=(fc == 0), stop=(fc == 7))
                        return ii
                    S("te", g1_mm, reads=[r_xd, r_cst], writes=[r_pxd[0]])
                    S("sc", lambda q: q.activation(lr[:, 0, :], pxd[:, 0, :], AF.Tanh),
                      reads=[r_pxd[0]], writes=[r_lr[0]])
                    for s in range(4):
                        def g2_mm(q, s=s):
                            ii = None
                            for oc in range(2):
                                ii = q.matmul(ppj[:, oc, :],
                                              lr[:, 0, s * 128:(s + 1) * 128],
                                              gbb[:, oc * 512:(oc + 1) * 512],
                                              start=True, stop=True)
                            return ii
                        S("te", g2_mm, reads=[r_lr[0], r_cst], writes=[r_ppj[0], r_ppj[1]])

                        def ev_g(q):
                            q.tensor_copy(gstg[:, 0:512], ppj[:, 0, :])
                            return q.tensor_copy(gstg[:, 512:1024], ppj[:, 1, :])
                        S("ve", ev_g, reads=[r_ppj[0], r_ppj[1]], writes=[r_gstg])
                        S("scd", lambda q, t0=t0, s=s: ld(
                            q, o_g[t0 + s * 128:t0 + (s + 1) * 128, :], gstg[:]),
                            reads=[r_gstg])
                elif g == 1:
                    def w1_mm(q):
                        ii = None
                        for fc in range(8):
                            ii = q.matmul(pxd[0:64, 1, :], wa[:, fc, :], xd[:, fc, :],
                                          start=(fc == 0), stop=(fc == 7))
                        return ii
                    S("te", w1_mm, reads=[r_xd, r_cst], writes=[r_pxd[1]])
                    S("sc", lambda q: q.activation(lr[0:64, 1, :], pxd[0:64, 1, :], AF.Tanh),
                      reads=[r_pxd[1]], writes=[r_lr[1]])
                    for s in range(4):
                        sp2 = s % 2

                        def w2_mm(q, s=s):
                            ii = None
                            for oc in range(2):
                                ii = q.matmul(ppj[:, oc, :],
                                              lr[0:64, 1, s * 128:(s + 1) * 128],
                                              wbb[:, oc * 512:(oc + 1) * 512],
                                              start=True, stop=True)
                            return ii
                        S("te", w2_mm, reads=[r_lr[1], r_cst], writes=[r_ppj[0], r_ppj[1]])

                        def w_u(q, sp2=sp2):
                            q.tensor_tensor(cs[:, sp2, 0:512], ppj[:, 0, :],
                                            wmiu[:, 0:512], ALU.add)
                            return q.tensor_tensor(cs[:, sp2, 512:1024], ppj[:, 1, :],
                                                   wmiu[:, 512:1024], ALU.add)
                        S("ve", w_u, reads=[r_ppj[0], r_ppj[1], r_cst], writes=[r_cs[sp2]])
                        S("sc", lambda q, sp2=sp2: q.activation(cs2[:],
                                                                cs[:, sp2, :], AF.Abs),
                          reads=[r_cs[sp2]], writes=[r_cs2[0]])
                        S("sc", lambda q, sp2=sp2: q.activation(sqd[:], cs2[:],
                                                                AF.Exp, scale=-1.0),
                          reads=[r_cs2[0]], writes=[r_sqd])
                        S("sc", lambda q, sp2=sp2: q.activation(cs2[:], sqd[:],
                                                                AF.Ln, bias=cc[:, 2:3]),
                          reads=[r_sqd, r_cst], writes=[r_cs2[0]])
                        S("ve", lambda q, sp2=sp2: q.tensor_scalar(
                            mt[:, 0:2, :].rearrange("p a b -> p (a b)"), cs[:, sp2, :],
                            0.0, None, ALU.min),
                          reads=[r_cs[sp2]], writes=[r_mt[0], r_mt[1]])
                        S("ve", lambda q, s=s, sp2=sp2: q.scalar_tensor_tensor(
                            wstg[:, s, :], mt[:, 0:2, :].rearrange("p a b -> p (a b)"),
                            -0.5, cs2[:], ALU.add, ALU.subtract),
                          reads=[r_mt[0], r_mt[1], r_cs2[0]], writes=[r_wstg])
                        S("scd", lambda q, t0=t0, s=s: ld(
                            q, o_w[t0 + s * 128:t0 + (s + 1) * 128, :], wstg[:, s, :]),
                            reads=[r_wstg])

                        def aux_mm(q, s=s):
                            q.matmul(pxd[0:4, 2, :], c3[:], wstg[:, s, 0:512],
                                     start=True, stop=True)
                            return q.matmul(pxd[0:4, 3, :], c3[:], wstg[:, s, 512:1024],
                                            start=True, stop=True)
                        S("te", aux_mm, reads=[r_wstg, r_cst],
                          writes=[r_pxd[2], r_pxd[3]])

                        def aux_ev(q, sp2=sp2):
                            q.tensor_copy(auxs[:, sp2, 0:512], pxd[0:4, 2, :])
                            return q.tensor_copy(auxs[:, sp2, 512:1024], pxd[0:4, 3, :])
                        S("ve", aux_ev, reads=[r_pxd[2], r_pxd[3]], writes=[r_aux[sp2]])
                        c0 = b * 8 + 2 * s
                        if c0 == 0:
                            S("scd", lambda q, sp2=sp2: ld(
                                q, o_mcs[0:1, :], auxs[1:2, sp2, :]), reads=[r_aux[sp2]])
                        else:
                            S("scd", lambda q, c0=c0, sp2=sp2: ld(
                                q, o_mcs[c0 - 1:c0 + 1, :], auxs[0:2, sp2, :]),
                                reads=[r_aux[sp2]])
                        S("scd", lambda q, c0=c0, sp2=sp2: ld(
                            q, o_wlm[c0:c0 + 2, :], auxs[2:4, sp2, :]),
                            reads=[r_aux[sp2]])
                elif g == 4:
                    def a1_mm(q):
                        ii = None
                        for fc in range(8):
                            ii = q.matmul(pxd[0:64, 1, :], aa[:, fc, :], xd[:, fc, :],
                                          start=(fc == 0), stop=(fc == 7))
                        return ii
                    S("te", a1_mm, reads=[r_xd, r_cst], writes=[r_pxd[1]])
                    S("ve", lambda q: q.tensor_copy(lr[0:64, 1, :], pxd[0:64, 1, :]),
                      reads=[r_pxd[1]], writes=[r_lr[1]])
                    for s in range(4):
                        sp2 = s % 2

                        def a2_mm(q, s=s):
                            ii = None
                            for oc in range(2):
                                ii = q.matmul(ppj[:, oc, :],
                                              lr[0:64, 1, s * 128:(s + 1) * 128],
                                              abb[:, oc * 512:(oc + 1) * 512],
                                              start=True, stop=True)
                            return ii
                        S("te", a2_mm, reads=[r_lr[1], r_cst], writes=[r_ppj[0], r_ppj[1]])

                        def a_pre(q, sp2=sp2):
                            q.tensor_tensor(cs[:, sp2, 0:512], ppj[:, 0, :],
                                            atim[:, 0:512], ALU.add)
                            return q.tensor_tensor(cs[:, sp2, 512:1024], ppj[:, 1, :],
                                                   atim[:, 512:1024], ALU.add)
                        S("ve", a_pre, reads=[r_ppj[0], r_ppj[1], r_cst], writes=[r_cs[sp2]])
                        S("sc", lambda q, s=s, sp2=sp2: q.activation(
                            sig[:, s, :], cs[:, sp2, :], AF.Sigmoid),
                          reads=[r_cs[sp2]], writes=[r_sig])
                else:
                    def k1_mm(q):
                        ii = None
                        for fc in range(8):
                            ii = q.matmul(pxd[0:64, 1, :], ka[:, fc, :], xd[:, fc, :],
                                          start=(fc == 0), stop=(fc == 7))
                        return ii
                    S("te", k1_mm, reads=[r_xd, r_cst], writes=[r_pxd[1]])
                    S("sc", lambda q: q.activation(lr[0:64, 1, :], pxd[0:64, 1, :], AF.Tanh),
                      reads=[r_pxd[1]], writes=[r_lr[1]])
                    for s in range(4):
                        sp2 = s % 2

                        def k_mm(q, s=s):
                            ii = None
                            for oc in range(2):
                                for fc in range(8):
                                    q.matmul(ppj[:, oc, :],
                                             xd[:, fc, s * 128:(s + 1) * 128],
                                             kwt[:, fc, oc * 512:(oc + 1) * 512],
                                             start=(fc == 0), stop=False)
                                ii = q.matmul(ppj[:, oc, :],
                                              lr[0:64, 1, s * 128:(s + 1) * 128],
                                              kbb[:, oc * 512:(oc + 1) * 512],
                                              start=False, stop=True)
                            return ii
                        S("te", k_mm, reads=[r_xd, r_lr[1], r_cst],
                          writes=[r_ppj[0], r_ppj[1]])

                        def k_ev(q, sp2=sp2):
                            q.tensor_copy(cs[:, sp2, 0:512], ppj[:, 0, :])
                            return q.tensor_copy(cs[:, sp2, 512:1024], ppj[:, 1, :])
                        S("ve", k_ev, reads=[r_ppj[0], r_ppj[1]], writes=[r_cs[sp2]])
                        S("sc", lambda q, sp2=sp2: q.activation(
                            sqd[:], cs[:, sp2, :], AF.Square, accum_out=kcol[:, sp2, 0:1]),
                          reads=[r_cs[sp2]], writes=[r_sqd, r_kcol[sp2]])
                        S("sc", lambda q, sp2=sp2: q.activation(
                            kcol[:, sp2, 1:2], kcol[:, sp2, 0:1], AF.Sqrt),
                          reads=[r_kcol[sp2]], writes=[r_kcol[sp2]])
                        S("ve", lambda q, sp2=sp2: q.tensor_scalar(
                            kcol[:, sp2, 2:3], kcol[:, sp2, 1:2], 1e-12, None, ALU.max),
                          reads=[r_kcol[sp2]], writes=[r_kcol[sp2]])
                        S("ve", lambda q, sp2=sp2: q.reciprocal(
                            kcol[:, sp2, 3:4], kcol[:, sp2, 2:3]),
                          reads=[r_kcol[sp2]], writes=[r_kcol[sp2]])
                        S("ve", lambda q, s=s, sp2=sp2: q.tensor_scalar(
                            kstg[:, s, :], cs[:, sp2, :], kcol[:, sp2, 3:4], None,
                            ALU.mult),
                          reads=[r_cs[sp2], r_kcol[sp2]], writes=[r_kstg])
                        S("gpd", lambda q, t0=t0, s=s: ld(
                            q, o_k[t0 + s * 128:t0 + (s + 1) * 128, :], kstg[:, s, :]),
                            reads=[r_kstg])
                        S("ve", lambda q, s=s: q.tensor_scalar(
                            cs2[:], wstg[:, s, :], 0.0, None, ALU.min),
                          reads=[r_wstg], writes=[r_cs2[0]])
                        S("sc", lambda q: q.activation(
                            sqd[:], cs2[:], AF.Exp, scale=0.5),
                          reads=[r_cs2[0]], writes=[r_sqd])
                        S("ve", lambda q, s=s: q.tensor_tensor(
                            kkstg[:], kstg[:, s, :], sqd[:], ALU.mult),
                          reads=[r_kstg, r_sqd], writes=[r_kkstg])
                        S("gpd", lambda q, t0=t0, s=s: ld(
                            q, o_kk[t0 + s * 128:t0 + (s + 1) * 128, :], kkstg[:]),
                            reads=[r_kkstg])
                        S("ve", lambda q, s=s: q.scalar_tensor_tensor(
                            bstg[:], kstg[:, s, :], 2.0, sig[:, s, :], ALU.mult, ALU.mult),
                          reads=[r_kstg, r_sig], writes=[r_bstg])
                        S("gpd", lambda q, t0=t0, s=s: ld(
                            q, o_b[t0 + s * 128:t0 + (s + 1) * 128, :], bstg[:]),
                            reads=[r_bstg])
                        S("ve", lambda q, s=s, sp2=sp2: q.tensor_tensor(
                            cs[:, sp2, :], rstg[:, s, :], kstg[:, s, :], ALU.mult),
                          reads=[r_rstg, r_kstg], writes=[r_cs[sp2]])
                        S("ve", lambda q, sp2=sp2: q.tensor_tensor(
                            cs2[:], cs[:, sp2, :], xtim[:], ALU.mult),
                          reads=[r_cs[sp2], r_cst], writes=[r_cs2[0]])
                        S("ve", lambda q, sp2=sp2: q.tensor_reduce(
                            bscol[:, sp2, :],
                            cs2[:].rearrange("p (h n) -> p h n", h=16),
                            AX.X, ALU.add),
                          reads=[r_cs2[0]], writes=[r_bscol[sp2]])
                        S("scd", lambda q, t0=t0, s=s, sp2=sp2: ld(
                            q, o_bs[t0 + s * 128:t0 + (s + 1) * 128, :], bscol[:, sp2, :]),
                            reads=[r_bscol[sp2]])

        with nc.Block() as block:
            sch.emit(nc, block)
    return nc




def build_s5(nc, io):
    o_x1, o_xn1, o_xn1T = io["x1_d"], io["xn1_d"], io["xn1T_bf"]
    with ExitStack() as _es:
        ec = _es.enter_context
        sb = lambda *a: ec(nc.sbuf_tensor(*a))
        cc = sb("s5_cc", [128, 8], F32)
        idb = sb("s5_idb", [128, 128], BF16)
        owt = sb("s5_owt", [128, 8, 1024], BF16)
        gng = sb("s5_gng", [128, 1024], F32)
        gnb = sb("s5_gnb", [128, 1024], F32)
        yt = sb("s5_y", [128, 2, 1024], F32)
        bst = sb("s5_bs", [128, 2, 16], F32)
        vt = sb("s5_v", [128, 2, 1024], BF16)
        gt = sb("s5_g", [128, 2, 1024], BF16)
        xt = sb("s5_x", [128, 2, 1024], F32)
        f0t = sb("s5_f0", [128, 2, 1024], F32)
        sq = sb("s5_sq", [128, 1024], F32)
        gcol = sb("s5_gcol", [128, 2, 96], F32)
        y2 = sb("s5_y2", [128, 1024], F32)
        zc = sb("s5_zc", [128, 2, 1024], BF16)
        zcT = sb("s5_zcT", [128, 2, 8, 128], BF16)
        x1s = sb("s5_x1", [128, 2, 1024], F32)
        ncol = sb("s5_ncol", [128, 2, 8], F32)
        xn1s = sb("s5_xn1", [128, 2, 1024], F32)
        xn1b = sb("s5_xn1b", [128, 1024], BF16)
        xn1Ts = sb("s5_xn1T", [128, 2, 8, 128], BF16)
        ptr = ec(nc.psum_tensor("s5_ptr", [128, 2, 1024], BF16))
        ppj = ec(nc.psum_tensor("s5_ppj", [128, 2, 512], F32))

        sch = Sched()
        S = mk_S(sch)
        r_cst = Res()
        r_y, r_bs, r_v, r_g, r_x, r_f0 = [[Res(), Res()] for _ in range(6)]
        r_sq, r_y2, r_xn1b = Res(), Res(), Res()
        r_gcol = [Res(), Res()]
        r_zc = [Res(), Res()]
        r_zcT = [Res(), Res()]
        r_x1 = [Res(), Res()]
        r_nc = [Res(), Res()]
        r_xn1 = [Res(), Res()]
        r_xn1T = [Res(), Res()]
        r_ptr = [Res(), Res()]
        r_ppj = [Res(), Res()]

        def ld(q, dst, src):
            return q.dma_start(out=dst, in_=src)

        S("gpd", lambda q: ld(q, owt[:], io["s5_oWt"].rearrange("(fc f) o -> f fc o", f=128)),
          writes=[r_cst])
        S("gpd", lambda q: ld(q, idb[:], io["c_ident_bf"][:, :]), writes=[r_cst])
        for nm, dst in [("s5_gng_t", gng), ("s5_gnb_t", gnb), ("c_consts", cc)]:
            S("sy", lambda q, nm=nm, dst=dst: ld(q, dst[:], io[nm][:, :]), writes=[r_cst])

        for i in range(NT):
            p = i % 2
            t0 = i * 128
            rows = slice(t0, t0 + 128)
            S("sy", lambda q, rows=rows, p=p: ld(q, yt[:, p, :], io["y_d"][rows, :]),
              reads=[r_cst], writes=[r_y[p]])
            S("sy", lambda q, rows=rows, p=p: ld(q, bst[:, p, :], io["bsum_d"][rows, :]),
              writes=[r_bs[p]])
            S("sy", lambda q, rows=rows, p=p: ld(q, vt[:, p, :], io["v_d"][rows, :]),
              writes=[r_v[p]])
            S("sy", lambda q, rows=rows, p=p: ld(q, gt[:, p, :], io["g_d"][rows, :]),
              writes=[r_g[p]])
            S("sy", lambda q, rows=rows, p=p: ld(q, xt[:, p, :], io["x"][rows, :]),
              writes=[r_x[p]])
            S("sy", lambda q, rows=rows, p=p: ld(q, f0t[:, p, :], io["f0"][rows, :]),
              writes=[r_f0[p]])
            S("ve", lambda q, p=p: q.tensor_reduce(
                gcol[:, p, 0:16], yt[:, p, :].rearrange("a (h n) -> a h n", h=16),
                AX.X, ALU.add), reads=[r_y[p]], writes=[r_gcol[p]])
            S("sc", lambda q, p=p: q.activation(sq[:], yt[:, p, :], AF.Square),
              reads=[r_y[p]], writes=[r_sq])
            S("ve", lambda q, p=p: q.tensor_reduce(
                gcol[:, p, 16:32], sq[:].rearrange("a (h n) -> a h n", h=16),
                AX.X, ALU.add), reads=[r_sq], writes=[r_gcol[p]])
            S("ve", lambda q, p=p: q.tensor_scalar(
                gcol[:, p, 32:48], gcol[:, p, 0:16], 1.0 / 64, None, ALU.mult),
              reads=[r_gcol[p]], writes=[r_gcol[p]])
            S("ve", lambda q, p=p: q.tensor_tensor(
                gcol[:, p, 48:64], gcol[:, p, 0:16], gcol[:, p, 32:48], ALU.mult),
              reads=[r_gcol[p]], writes=[r_gcol[p]])
            S("ve", lambda q, p=p: q.tensor_tensor(
                gcol[:, p, 64:80], gcol[:, p, 16:32], gcol[:, p, 48:64], ALU.subtract),
              reads=[r_gcol[p]], writes=[r_gcol[p]])
            S("ve", lambda q, p=p: q.tensor_scalar(
                gcol[:, p, 80:96], gcol[:, p, 64:80], 1.0 / 64, None, ALU.mult),
              reads=[r_gcol[p]], writes=[r_gcol[p]])
            S("sc", lambda q, p=p: q.activation(
                gcol[:, p, 48:64], gcol[:, p, 80:96], AF.Ln, bias=cc[:, 1:2]),
              reads=[r_gcol[p], r_cst], writes=[r_gcol[p]])
            S("sc", lambda q, p=p: q.activation(
                gcol[:, p, 64:80], gcol[:, p, 48:64], AF.Exp, scale=-0.5),
              reads=[r_gcol[p]], writes=[r_gcol[p]])

            def gnorm(q, p=p):
                ii = None
                for h in range(16):
                    sg = slice(h * 64, (h + 1) * 64)
                    ii = q.tensor_scalar(y2[:, sg], yt[:, p, sg],
                                         gcol[:, p, 32 + h:33 + h],
                                         gcol[:, p, 64 + h:65 + h],
                                         ALU.subtract, ALU.mult)
                return ii
            S("ve", gnorm, reads=[r_y[p], r_gcol[p]], writes=[r_y2])
            S("ve", lambda q: q.tensor_tensor(y2[:], y2[:], gng[:], ALU.mult),
              reads=[r_y2, r_cst], writes=[r_y2])
            S("ve", lambda q: q.tensor_tensor(y2[:], y2[:], gnb[:], ALU.add),
              reads=[r_y2], writes=[r_y2])

            def bonus(q, p=p):
                ii = None
                for h in range(16):
                    sg = slice(h * 64, (h + 1) * 64)
                    ii = q.scalar_tensor_tensor(y2[:, sg], vt[:, p, sg],
                                                bst[:, p, h:h + 1], y2[:, sg],
                                                ALU.mult, ALU.add)
                return ii
            S("ve", bonus, reads=[r_v[p], r_bs[p], r_y2], writes=[r_y2])
            S("ve", lambda q, p=p: q.tensor_tensor(zc[:, p, :], y2[:], gt[:, p, :],
                                                   ALU.mult),
              reads=[r_y2, r_g[p]], writes=[r_zc[p]])

            def tr_zc(q, p=p):
                ii = None
                for j in range(8):
                    ii = q.transpose(ptr[:, p, j * 128:(j + 1) * 128],
                                     zc[:, p, j * 128:(j + 1) * 128], idb[:])
                return ii
            S("te", tr_zc, reads=[r_zc[p], r_cst], writes=[r_ptr[p]], no_drain=True)
            S("ve", lambda q, p=p: q.tensor_copy(
                zcT[:, p, :, :].rearrange("a b c -> a (b c)"), ptr[:, p, :]),
              reads=[r_ptr[p]], writes=[r_zcT[p]])

            def ow_mm(q, p=p):
                ii = None
                for oc in range(2):
                    for fc in range(8):
                        ii = q.matmul(ppj[:, oc, :], zcT[:, p, fc, :],
                                      owt[:, fc, oc * 512:(oc + 1) * 512],
                                      start=(fc == 0), stop=(fc == 7))
                return ii
            S("te", ow_mm, reads=[r_zcT[p], r_cst], writes=[r_ppj[0], r_ppj[1]])

            def x1c(q, p=p):
                for oc in range(2):
                    cols = slice(oc * 512, (oc + 1) * 512)
                    q.tensor_tensor(x1s[:, p, cols], ppj[:, oc, :], xt[:, p, cols],
                                    ALU.add)
                ii = None
                for oc in range(2):
                    cols = slice(oc * 512, (oc + 1) * 512)
                    ii = q.tensor_tensor(x1s[:, p, cols], x1s[:, p, cols],
                                         f0t[:, p, cols], ALU.add)
                return ii
            S("ve", x1c, reads=[r_ppj[0], r_ppj[1], r_x[p], r_f0[p]], writes=[r_x1[p]])
            S("scd", lambda q, rows=rows, p=p: ld(q, o_x1[rows, :], x1s[:, p, :]),
              reads=[r_x1[p]])
            S("ve", lambda q, p=p: q.reduce_sum(ncol[:, p, 0:1], x1s[:, p, :], AX.X),
              reads=[r_x1[p]], writes=[r_nc[p]])
            S("sc", lambda q, p=p: q.activation(sq[:], x1s[:, p, :], AF.Square,
                                                accum_out=ncol[:, p, 1:2]),
              reads=[r_x1[p]], writes=[r_sq, r_nc[p]])
            S("ve", lambda q, p=p: q.tensor_scalar(ncol[:, p, 2:3], ncol[:, p, 0:1],
                                                   1.0 / C, None, ALU.mult),
              reads=[r_nc[p]], writes=[r_nc[p]])
            S("ve", lambda q, p=p: q.tensor_tensor(ncol[:, p, 3:4], ncol[:, p, 0:1],
                                                   ncol[:, p, 2:3], ALU.mult),
              reads=[r_nc[p]], writes=[r_nc[p]])
            S("ve", lambda q, p=p: q.tensor_tensor(ncol[:, p, 4:5], ncol[:, p, 1:2],
                                                   ncol[:, p, 3:4], ALU.subtract),
              reads=[r_nc[p]], writes=[r_nc[p]])
            S("ve", lambda q, p=p: q.tensor_scalar(ncol[:, p, 5:6], ncol[:, p, 4:5],
                                                   1.0 / C, None, ALU.mult),
              reads=[r_nc[p]], writes=[r_nc[p]])
            S("sc", lambda q, p=p: q.activation(ncol[:, p, 6:7], ncol[:, p, 5:6],
                                                AF.Ln, bias=cc[:, 0:1]),
              reads=[r_nc[p], r_cst], writes=[r_nc[p]])
            S("sc", lambda q, p=p: q.activation(ncol[:, p, 7:8], ncol[:, p, 6:7],
                                                AF.Exp, scale=-0.5),
              reads=[r_nc[p]], writes=[r_nc[p]])
            S("ve", lambda q, p=p: q.tensor_scalar(xn1s[:, p, :], x1s[:, p, :],
                                                   ncol[:, p, 2:3], ncol[:, p, 7:8],
                                                   ALU.subtract, ALU.mult),
              reads=[r_x1[p], r_nc[p]], writes=[r_xn1[p]])
            S("scd", lambda q, rows=rows, p=p: ld(q, o_xn1[rows, :], xn1s[:, p, :]),
              reads=[r_xn1[p]])
            S("ve", lambda q, p=p: q.tensor_copy(xn1b[:], xn1s[:, p, :]),
              reads=[r_xn1[p]], writes=[r_xn1b])

            def tr_xn1(q, p=p):
                ii = None
                for j in range(8):
                    ii = q.transpose(ptr[:, p, j * 128:(j + 1) * 128],
                                     xn1b[:, j * 128:(j + 1) * 128], idb[:])
                return ii
            S("te", tr_xn1, reads=[r_xn1b, r_cst], writes=[r_ptr[p]], no_drain=True)
            S("ve", lambda q, p=p: q.tensor_copy(
                xn1Ts[:, p, :, :].rearrange("a b c -> a (b c)"), ptr[:, p, :]),
              reads=[r_ptr[p]], writes=[r_xn1T[p]])
            S("gpd", lambda q, t0=t0, p=p: ld(
                q, o_xn1T.rearrange("(fc f) t -> f fc t", f=128)[:, :, t0:t0 + 128],
                xn1Ts[:, p, :, :]), reads=[r_xn1T[p]])

        with nc.Block() as block:
            sch.emit(nc, block)
    return nc




def build_s6(nc, io):
    xnTb = io["xn1T_bf"]
    o_f1 = io["f1_d"]
    NE, NB = 14, 16
    with ExitStack() as _es:
        ec = _es.enter_context
        sb = lambda *a: ec(nc.sbuf_tensor(*a))
        idb = sb("s6_idb", [128, 128], BF16)
        rw = sb("s6s_rw", [128, 8, 14], BF16)
        gs = sb("s6_gs", [128, NT, 48], F32)
        Mall = sb("s6_mall", [128, NT, 16], F32)
        Mb = sb("s6_mb", [128, 16], BF16)
        MT = sb("s6_mt", [14, T], BF16)
        xg = sb("s6_xg", [128, 2, 8, 128], BF16)
        xT = sb("s6_xT", [128, 2, 8, 512], BF16)
        w1s = sb("s6_w1s", [128, 8, 16, 128], BF16)
        w2s = sb("s6_w2s", [128, 16, 1024], BF16)
        b1c = sb("s6s_b1c", [128, 224], F32)
        b2s = sb("s6_b2s", [14, 1024], BF16)
        hT = sb("s6_hT", [128, 16, 512], BF16)
        ft = sb("s6_ft", [128, 2, 4, 1024], F32)
        pg = ec(nc.psum_tensor("s6_pg", [128, 2, 512], F32))
        pmt = ec(nc.psum_tensor("s6_pmt", [128, 1024], BF16))
        ph = ec(nc.psum_tensor("s6_ph", [128, 2, 512], F32))
        py = ec(nc.psum_tensor("s6_py", [128, 2, 512], F32))

        sch = Sched()
        S = mk_S(sch)
        r_cst, r_w1, r_w2 = Res(), Res(), Res()
        r_xg = [Res(), Res()]
        r_xT = [Res(), Res()]
        r_pg = [Res(), Res()]
        r_ph = [Res(), Res()]
        r_py = [Res(), Res()]
        r_pmt, r_mb, r_MT, r_gs, r_mall, r_hT = Res(), Res(), Res(), Res(), Res(), Res()
        r_ft = [Res(), Res()]
        r_f1b = [Res() for _ in range(NB)]

        def ld(q, dst, src):
            return q.dma_start(out=dst, in_=src)

        S("gpd", lambda q: ld(q, rw[:], io["s6_rw"].rearrange("(fc f) e -> f fc e", f=128)),
          writes=[r_cst])
        S("gpd", lambda q: ld(q, idb[:], io["c_ident_bf"][:, :]), writes=[r_cst])
        S("gpd", lambda q: ld(q, b2s[:], io["s6_b2"][:, :]), writes=[r_cst])
        S("sy", lambda q: ld(q, b1c[:], io["s6_b1c"][:, :]), writes=[r_cst])

        xr = xnTb.rearrange("(fc f) t -> f fc t", f=128)

        for i in range(NT):
            p = i % 2
            S("sy", lambda q, i=i, p=p: ld(q, xg[:, p, :, :],
                                           xr[:, :, i * 128:(i + 1) * 128]),
              reads=[r_cst], writes=[r_xg[p]])

            def g_mm(q, p=p):
                ii = None
                for fc in range(8):
                    ii = q.matmul(pg[:, p, 0:14], xg[:, p, fc, :], rw[:, fc, :],
                                  start=(fc == 0), stop=(fc == 7))
                return ii
            S("te", g_mm, reads=[r_xg[p], r_cst], writes=[r_pg[p]])
            S("ve", lambda q, i=i, p=p: q.reduce_max(gs[:, i, 0:1], pg[:, p, 0:14], AX.X),
              reads=[r_pg[p]], writes=[r_gs])
            S("ve", lambda q, i=i, p=p: q.tensor_scalar(
                gs[:, i, 16:30], pg[:, p, 0:14], gs[:, i, 0:1], None, ALU.is_ge),
              reads=[r_pg[p], r_gs], writes=[r_gs])
            S("ve", lambda q, i=i, p=p: q.tensor_scalar(
                gs[:, i, 32:46], pg[:, p, 0:14], gs[:, i, 0:1], None, ALU.subtract),
              reads=[r_pg[p], r_gs], writes=[r_gs])
            S("sc", lambda q, i=i: q.activation(gs[:, i, 2:16], gs[:, i, 32:46], AF.Exp),
              reads=[r_gs], writes=[r_gs])
            S("ve", lambda q, i=i: q.reduce_sum(gs[:, i, 1:2], gs[:, i, 2:16], AX.X),
              reads=[r_gs], writes=[r_gs])
            S("ve", lambda q, i=i: q.reciprocal(gs[:, i, 46:47], gs[:, i, 1:2]),
              reads=[r_gs], writes=[r_gs])
            S("ve", lambda q, i=i: q.tensor_scalar(
                Mall[:, i, 0:14], gs[:, i, 16:30], gs[:, i, 46:47], None, ALU.mult),
              reads=[r_gs], writes=[r_mall])
            S("ve", lambda q, i=i: q.tensor_copy(Mb[:, 0:14], Mall[:, i, 0:14]),
              reads=[r_mall], writes=[r_mb])
            S("te", lambda q: q.transpose(pmt[0:14, 0:128], Mb[:, 0:14], idb[:]),
              reads=[r_mb, r_cst], writes=[r_pmt], no_drain=True)
            S("ve", lambda q, i=i: q.tensor_copy(MT[0:14, i * 128:(i + 1) * 128],
                                                 pmt[0:14, 0:128]),
              reads=[r_pmt], writes=[r_MT])

        for e in range(NE):
            S("gpd", lambda q, e=e: ld(
                q, w1s[:], io["s6_w1"][e].rearrange("(fc f) (ic i) -> f fc ic i",
                                                    f=128, i=128)), writes=[r_w1])
            S("gpd", lambda q, e=e: ld(
                q, w2s[:], io["s6_w2"][e].rearrange("(ic i) o -> i ic o", i=128)),
              writes=[r_w2])
            for b in range(NB):
                p = (e * NB + b) % 2
                S("sy", lambda q, b=b, p=p: ld(q, xT[:, p, :, :],
                                               xr[:, :, b * 512:(b + 1) * 512]),
                  writes=[r_xT[p]])
                for ic in range(16):
                    hp = ic % 2

                    def h_mm(q, ic=ic, p=p, hp=hp):
                        ii = None
                        for fc in range(8):
                            ii = q.matmul(ph[:, hp, :], w1s[:, fc, ic, :],
                                          xT[:, p, fc, :],
                                          start=(fc == 0), stop=(fc == 7))
                        return ii
                    S("te", h_mm, reads=[r_xT[p], r_w1], writes=[r_ph[hp]])
                    S("sc", lambda q, e=e, ic=ic, hp=hp: q.activation(
                        hT[:, ic, :], ph[:, hp, :], AF.Gelu,
                        bias=b1c[:, e * 16 + ic:e * 16 + ic + 1]),
                      reads=[r_ph[hp], r_cst], writes=[r_hT])
                for j in range(8):
                    s, oc = divmod(j, 2)
                    yp = j % 2

                    def y_mm(q, s=s, oc=oc, yp=yp):
                        ii = None
                        for ic in range(16):
                            ii = q.matmul(py[:, yp, :], hT[:, ic, s * 128:(s + 1) * 128],
                                          w2s[:, ic, oc * 512:(oc + 1) * 512],
                                          start=(ic == 0), stop=(ic == 15))
                        return ii
                    S("te", y_mm, reads=[r_hT, r_w2], writes=[r_py[yp]])
                    S("ve", lambda q, e=e, b=b, s=s, oc=oc, yp=yp: q.tensor_scalar(
                        ft[:, b % 2, s, oc * 512:(oc + 1) * 512], py[:, yp, :],
                        Mall[:, 4 * b + s, e:e + 1], None, ALU.mult),
                      reads=[r_py[yp], r_mall], writes=[r_ft[b % 2]])
                dst = o_f1.rearrange("(bb q) c -> q bb c", q=128)
                if e == 0:
                    S("gpd", lambda q, b=b, dst=dst: ld(
                        q, dst[:, 4 * b:4 * b + 4, :], ft[:, b % 2, :, :]),
                      reads=[r_ft[b % 2]], writes=[r_f1b[b]])
                else:
                    S("gpd", lambda q, b=b, dst=dst: q.dma_start(
                        out=dst[:, 4 * b:4 * b + 4, :], in_=ft[:, b % 2, :, :],
                        accum_op=ALU.add),
                      reads=[r_ft[b % 2]], writes=[r_f1b[b]])
        for b in range(NB):
            for j in range(8):
                s, oc = divmod(j, 2)
                yp = j % 2

                def b2_mm(q, b=b, s=s, oc=oc, yp=yp):
                    return q.matmul(py[:, yp, :], MT[0:14, (4 * b + s) * 128:(4 * b + s + 1) * 128],
                                    b2s[:, oc * 512:(oc + 1) * 512],
                                    start=True, stop=True)
                S("te", b2_mm, reads=[r_MT, r_cst], writes=[r_py[yp]])
                S("ve", lambda q, b=b, s=s, oc=oc, yp=yp: q.tensor_copy(
                    ft[:, b % 2, s, oc * 512:(oc + 1) * 512], py[:, yp, :]),
                  reads=[r_py[yp]], writes=[r_ft[b % 2]])
            dst = o_f1.rearrange("(bb q) c -> q bb c", q=128)
            S("gpd", lambda q, b=b, dst=dst: q.dma_start(
                out=dst[:, 4 * b:4 * b + 4, :], in_=ft[:, b % 2, :, :],
                accum_op=ALU.add),
              reads=[r_ft[b % 2]], writes=[r_f1b[b]])

        with nc.Block() as block:
            sch.emit(nc, block)
    return nc



def build_s7a(nc, io):
    o_qfT, o_kvz = io["qfT_d"], io["kvz_d"]
    with ExitStack() as _es:
        ec = _es.enter_context
        sb = lambda *a: ec(nc.sbuf_tensor(*a))
        cc = sb("s7a_cc", [128, 8], F32)
        idb = sb("s7a_idb", [128, 128], BF16)
        ones = sb("s7a_ones", [128, 128], BF16)
        qw = sb("s7a_qw", [128, 8, 1024], BF16)
        kvw = sb("s7a_kvw", [128, 8, 2048], BF16)
        qbc = sb("s7a_qbc", [128, 8], F32)
        kvb = sb("s7a_kvb", [128, 2048], F32)
        g1 = sb("s7a_g1", [128, 1024], F32)
        be1 = sb("s7a_be1", [128, 1024], F32)
        xt = sb("s7a_x", [128, 2, 1024], F32)
        ncol = sb("s7a_ncol", [128, 2, 8], F32)
        sq = sb("s7a_sq", [128, 1024], F32)
        ln1 = sb("s7a_ln1", [128, 1024], F32)
        ln1b = sb("s7a_ln1b", [128, 1024], BF16)
        lnT = sb("s7a_lnT", [128, 2, 8, 128], BF16)
        qcs = sb("s7a_qcs", [128, 2, 512], F32)
        qfT = sb("s7a_qfT", [128, 8, 128], BF16)
        kvcs = sb("s7a_kvcs", [128, 2048], F32)
        kfb = sb("s7a_kfb", [128, 1024], BF16)
        vfb = sb("s7a_vfb", [128, 1024], BF16)
        kvzs = sb("s7a_kvz", [128, 8, 66], BF16)
        zrow = sb("s7a_zrow", [1, 1024], F32)
        zrowb = sb("s7a_zrowb", [1, 1024], BF16)
        zcs = sb("s7a_zc", [128, 8], F32)
        ptr = ec(nc.psum_tensor("s7a_ptr", [128, 2, 1024], BF16))
        pq = ec(nc.psum_tensor("s7a_pq", [128, 2, 512], F32))
        pkv = ec(nc.psum_tensor("s7a_pkv", [128, 2, 512], F32))
        pacc = ec(nc.psum_tensor("s7a_pacc", [128, 512], F32))
        pz = ec(nc.psum_tensor("s7a_pz", [128, 512], F32))

        sch = Sched()
        S = mk_S(sch)
        r_cst = Res()
        r_x = [Res(), Res()]
        r_nc = [Res(), Res()]
        r_sq, r_ln1, r_ln1b = Res(), Res(), Res()
        r_lnT = [Res(), Res()]
        r_ptr = [Res(), Res()]
        r_pq = [Res(), Res()]
        r_pkv = [Res(), Res()]
        r_qcs = [Res(), Res()]
        r_qfT, r_kvcs, r_kfb, r_vfb = Res(), Res(), Res(), Res()
        r_pacc, r_pz, r_kvz, r_zrow, r_zc = Res(), Res(), Res(), Res(), Res()

        def ld(q, dst, src):
            return q.dma_start(out=dst, in_=src)

        S("gpd", lambda q: ld(q, qw[:], io["s7_qw"].rearrange("(fc f) o -> f fc o", f=128)),
          writes=[r_cst])
        S("gpd", lambda q: ld(q, kvw[:], io["s7_kvw"].rearrange("(fc f) o -> f fc o", f=128)),
          writes=[r_cst])
        S("gpd", lambda q: ld(q, idb[:], io["c_ident_bf"][:, :]), writes=[r_cst])
        S("gpd", lambda q: ld(q, ones[:], io["c_ones_bf"][:, :]), writes=[r_cst])
        for nm, dst in [("s7_qbc", qbc), ("s7_kvb_t", kvb), ("s7_g1_t", g1),
                        ("s7_be1_t", be1), ("c_consts", cc)]:
            S("sy", lambda q, nm=nm, dst=dst: ld(q, dst[:], io[nm][:, :]), writes=[r_cst])

        for i in range(NT):
            p = i % 2
            rows = slice(i * 128, (i + 1) * 128)
            S("sy", lambda q, rows=rows, p=p: ld(q, xt[:, p, :], io["xn1_d"][rows, :]),
              reads=[r_cst], writes=[r_x[p]])
            S("ve", lambda q, p=p: q.reduce_sum(ncol[:, p, 0:1], xt[:, p, :], AX.X),
              reads=[r_x[p]], writes=[r_nc[p]])
            S("sc", lambda q, p=p: q.activation(sq[:], xt[:, p, :], AF.Square,
                                                accum_out=ncol[:, p, 1:2]),
              reads=[r_x[p]], writes=[r_sq, r_nc[p]])
            S("ve", lambda q, p=p: q.tensor_scalar(ncol[:, p, 2:3], ncol[:, p, 0:1],
                                                   1.0 / C, None, ALU.mult),
              reads=[r_nc[p]], writes=[r_nc[p]])
            S("ve", lambda q, p=p: q.tensor_tensor(ncol[:, p, 3:4], ncol[:, p, 0:1],
                                                   ncol[:, p, 2:3], ALU.mult),
              reads=[r_nc[p]], writes=[r_nc[p]])
            S("ve", lambda q, p=p: q.tensor_tensor(ncol[:, p, 4:5], ncol[:, p, 1:2],
                                                   ncol[:, p, 3:4], ALU.subtract),
              reads=[r_nc[p]], writes=[r_nc[p]])
            S("ve", lambda q, p=p: q.tensor_scalar(ncol[:, p, 5:6], ncol[:, p, 4:5],
                                                   1.0 / C, None, ALU.mult),
              reads=[r_nc[p]], writes=[r_nc[p]])
            S("sc", lambda q, p=p: q.activation(ncol[:, p, 6:7], ncol[:, p, 5:6],
                                                AF.Ln, bias=cc[:, 0:1]),
              reads=[r_nc[p], r_cst], writes=[r_nc[p]])
            S("sc", lambda q, p=p: q.activation(ncol[:, p, 7:8], ncol[:, p, 6:7],
                                                AF.Exp, scale=-0.5),
              reads=[r_nc[p]], writes=[r_nc[p]])
            S("ve", lambda q, p=p: q.tensor_scalar(ln1[:], xt[:, p, :],
                                                   ncol[:, p, 2:3], ncol[:, p, 7:8],
                                                   ALU.subtract, ALU.mult),
              reads=[r_x[p], r_nc[p]], writes=[r_ln1])
            S("ve", lambda q: q.tensor_tensor(ln1[:], ln1[:], g1[:], ALU.mult),
              reads=[r_ln1, r_cst], writes=[r_ln1])
            S("ve", lambda q: q.tensor_tensor(ln1b[:], ln1[:], be1[:], ALU.add),
              reads=[r_ln1], writes=[r_ln1b])

            def tr_ln(q, p=p):
                ii = None
                for j in range(8):
                    ii = q.transpose(ptr[:, p, j * 128:(j + 1) * 128],
                                     ln1b[:, j * 128:(j + 1) * 128], idb[:])
                return ii
            S("te", tr_ln, reads=[r_ln1b, r_cst], writes=[r_ptr[p]], no_drain=True)
            S("ve", lambda q, p=p: q.tensor_copy(
                lnT[:, p, :, :].rearrange("a b c -> a (b c)"), ptr[:, p, :]),
              reads=[r_ptr[p]], writes=[r_lnT[p]])

            for hb in range(2):
                def q_mm(q, p=p, hb=hb):
                    ii = None
                    for oj in range(4):
                        oc = hb * 4 + oj
                        for fc in range(8):
                            ii = q.matmul(pq[:, hb, oj * 128:(oj + 1) * 128],
                                          qw[:, fc, oc * 128:(oc + 1) * 128],
                                          lnT[:, p, fc, :],
                                          start=(fc == 0), stop=(fc == 7))
                    return ii
                S("te", q_mm, reads=[r_lnT[p], r_cst], writes=[r_pq[hb]])

                def q_elu(q, hb=hb):
                    for oj in range(4):
                        oc = hb * 4 + oj
                        cl = slice(oj * 128, (oj + 1) * 128)
                        q.tensor_scalar(qcs[:, hb, cl], pq[:, hb, cl],
                                        qbc[:, oc:oc + 1], None, ALU.add)
                    return q.tensor_scalar(qcs[:, hb, 0:1], qcs[:, hb, 0:1], 0.0, None,
                                           ALU.add)
                S("ve", q_elu, reads=[r_pq[hb], r_cst], writes=[r_qcs[hb]])
                S("ve", lambda q, hb=hb: q.tensor_scalar(
                    pq[:, hb, :], qcs[:, hb, :], 0.0, None, ALU.min),
                  reads=[r_qcs[hb]], writes=[r_pq[hb]])
                S("sc", lambda q, hb=hb: q.activation(sq[:, hb * 512:(hb + 1) * 512],
                                                      pq[:, hb, :], AF.Exp),
                  reads=[r_pq[hb]], writes=[r_sq])
                S("ve", lambda q, hb=hb: q.tensor_relu(qcs[:, hb, :], qcs[:, hb, :]),
                  reads=[r_qcs[hb]], writes=[r_qcs[hb]])
                S("ve", lambda q, hb=hb: q.tensor_tensor(
                    qfT[:, :, :].rearrange("a b c -> a (b c)")[:, hb * 512:(hb + 1) * 512],
                    qcs[:, hb, :], sq[:, hb * 512:(hb + 1) * 512], ALU.add),
                  reads=[r_qcs[hb], r_sq], writes=[r_qfT])
            S("gpd", lambda q, i=i: ld(
                q, o_qfT.rearrange("(fc f) t -> f fc t", f=128)[:, :, i * 128:(i + 1) * 128],
                qfT[:, :, :]), reads=[r_qfT])

            for ob in range(4):
                def kv_mm(q, p=p, ob=ob):
                    ii = None
                    for fc in range(8):
                        ii = q.matmul(pkv[:, ob % 2, :], lnT[:, p, fc, :],
                                      kvw[:, fc, ob * 512:(ob + 1) * 512],
                                      start=(fc == 0), stop=(fc == 7))
                    return ii
                S("te", kv_mm, reads=[r_lnT[p], r_cst], writes=[r_pkv[ob % 2]])
                S("ve", lambda q, ob=ob: q.tensor_tensor(
                    kvcs[:, ob * 512:(ob + 1) * 512], pkv[:, ob % 2, :],
                    kvb[:, ob * 512:(ob + 1) * 512], ALU.add),
                  reads=[r_pkv[ob % 2], r_cst], writes=[r_kvcs])
            S("ve", lambda q: q.tensor_scalar(sq[:], kvcs[:, 0:1024], 0.0, None, ALU.min),
              reads=[r_kvcs], writes=[r_sq])
            S("sc", lambda q: q.activation(sq[:], sq[:], AF.Exp),
              reads=[r_sq], writes=[r_sq])
            S("ve", lambda q: q.tensor_relu(kvcs[:, 0:1024], kvcs[:, 0:1024]),
              reads=[r_kvcs], writes=[r_kvcs])
            S("ve", lambda q: q.tensor_tensor(kfb[:], kvcs[:, 0:1024], sq[:], ALU.add),
              reads=[r_kvcs, r_sq], writes=[r_kfb])
            S("ve", lambda q: q.tensor_copy(vfb[:], kvcs[:, 1024:2048]),
              reads=[r_kvcs], writes=[r_vfb])

            def kvacc(q, i=i):
                ii = None
                for h in range(16):
                    rr = slice((h % 2) * 64, (h % 2) * 64 + 64)
                    ii = q.matmul(pacc[rr, (h // 2) * 64:(h // 2) * 64 + 64],
                                  kfb[:, h * 64:(h + 1) * 64],
                                  vfb[:, h * 64:(h + 1) * 64],
                                  start=(i == 0), stop=(i == NT - 1))
                return ii
            S("te", kvacc, reads=[r_kfb, r_vfb], writes=[r_pacc], no_drain=True)

            def zacc(q, i=i):
                q.matmul(pz[0:1, 0:512], ones[:, 0:1], kfb[:, 0:512],
                         start=(i == 0), stop=(i == NT - 1))
                return q.matmul(pz[32:33, 0:512], ones[:, 0:1], kfb[:, 512:1024],
                                start=(i == 0), stop=(i == NT - 1))
            S("te", zacc, reads=[r_kfb, r_cst], writes=[r_pz], no_drain=True)

        def kv_ev(q):
            ii = None
            for h2 in range(8):
                ii = q.tensor_copy(kvzs[:, h2, 0:64], pacc[:, h2 * 64:(h2 + 1) * 64])
            return ii
        S("ve", kv_ev, reads=[r_pacc], writes=[r_kvz])
        S("ve", lambda q: q.tensor_copy(zrow[0:1, 0:512], pz[0:1, 0:512]),
          reads=[r_pz], writes=[r_zrow])
        S("ve", lambda q: q.tensor_copy(zrow[0:1, 512:1024], pz[32:33, 0:512]),
          reads=[r_pz], writes=[r_zrow])

        S("ve", lambda q: q.tensor_copy(zrowb[:], zrow[:]), reads=[r_zrow],
          writes=[r_zrow])

        def z_tr(q):
            ii = None
            for j in range(8):
                ii = q.transpose(ptr[:, 0, j * 128:(j + 1) * 128][:, 0:1],
                                 zrowb[0:1, j * 128:(j + 1) * 128], idb[0:1, 0:1])
            return ii
        S("te", z_tr, reads=[r_zrow, r_cst], writes=[r_ptr[0]], no_drain=True)

        def z_ev(q):
            ii = None
            for j in range(8):
                ii = q.tensor_copy(zcs[:, j:j + 1], ptr[:, 0, j * 128:j * 128 + 1])
            return ii
        S("ve", z_ev, reads=[r_ptr[0]], writes=[r_zc])

        def z_fill(q):
            ii = None
            for h2 in range(8):
                ii = q.tensor_copy(kvzs[:, h2, 64:65], zcs[:, h2:h2 + 1])
            return ii
        S("ve", z_fill, reads=[r_zc], writes=[r_kvz])
        S("ve", lambda q: q.memset(kvzs[:, :, 65:66], 0.0), writes=[r_kvz])
        S("gpd", lambda q: ld(q, o_kvz[:, :],
                              kvzs[:, :, :].rearrange("a b c -> a (b c)")),
          reads=[r_kvz])

        with nc.Block() as block:
            sch.emit(nc, block)
    return nc



def build_s7b(nc, io):
    o_out = io["out_d"]
    with ExitStack() as _es:
        ec = _es.enter_context
        sb = lambda *a: ec(nc.sbuf_tensor(*a))
        cc = sb("s7b_cc", [128, 8], F32)
        idb = sb("s7b_idb", [128, 128], BF16)
        ones = sb("s7b_ones", [128, 128], BF16)
        ow = sb("s7b_ow", [128, 8, 1024], BF16)
        obt = sb("s7b_obt", [128, 1024], BF16)
        w1 = sb("s7b_w1", [128, 8, 4096], BF16)
        b1c = sb("s7b_b1c", [128, 32], F32)
        w2 = sb("s7b_w2", [128, 32, 1024], BF16)
        b2t = sb("s7b_b2t", [128, 1024], BF16)
        g2 = sb("s7b_g2", [128, 1024], BF16)
        be2 = sb("s7b_be2", [128, 1024], BF16)
        kvzs = sb("s7b_kvz", [128, 8, 66], BF16)
        qfb = sb("s7b_qfb", [128, 8, 512], BF16)
        den = sb("s7b_den", [1, 1024], F32)
        bcs = sb("s7b_bc", [64, 512], BF16)
        ysb = sb("s7b_y", [128, 8, 512], BF16)
        xn1t = sb("s7b_xn1", [128, 1024], F32)
        x1t = sb("s7b_x1", [128, 1024], F32)
        xxs = sb("s7b_xx", [128, 1, 1024], F32)
        ncol = sb("s7b_ncol", [128, 2, 8], F32)
        sq = sb("s7b_sq", [128, 1024], F32)
        ln2b = sb("s7b_ln2b", [128, 1024], BF16)
        lnT = sb("s7b_lnT", [128, 8, 128], BF16)
        hms = sb("s7b_hm", [128, 32, 128], BF16)
        outs = sb("s7b_out", [128, 1024], F32)
        pA = ec(nc.psum_tensor("s7b_pA", [128, 2, 512], F32))
        pB = ec(nc.psum_tensor("s7b_pB", [128, 2, 512], F32))
        pC = ec(nc.psum_tensor("s7b_pC", [128, 2, 512], F32))
        ptr = ec(nc.psum_tensor("s7b_ptr", [128, 1024], BF16))
        pD = ec(nc.psum_tensor("s7b_pD", [128, 512], F32))

        sch = Sched()
        S = mk_S(sch)
        r_cst = Res()
        r_qfb, r_kvz = Res(), Res()
        r_pA = [Res(), Res()]
        r_pB = [Res(), Res()]
        r_pC = [Res(), Res()]
        r_pD, r_ptr = Res(), Res()
        r_den = [Res(), Res()]
        r_rdb = [Res(), Res()]
        r_bc = [Res(), Res()]
        r_y, r_xn1, r_x1 = Res(), Res(), Res()
        r_xx = [Res(), Res()]
        r_nc = [Res(), Res()]
        r_sq, r_ln2b, r_lnT, r_hm, r_out = Res(), Res(), Res(), Res(), Res()

        def ld(q, dst, src):
            return q.dma_start(out=dst, in_=src)

        S("gpd", lambda q: ld(q, ow[:], io["s7_ow"].rearrange("(fc f) o -> f fc o", f=128)),
          writes=[r_cst])
        S("gpd", lambda q: ld(q, w1[:], io["s7_w1"].rearrange("(fc f) o -> f fc o", f=128)),
          writes=[r_cst])
        S("gpd", lambda q: ld(q, w2[:], io["s7_w2"].rearrange("(ic i) o -> i ic o", i=128)),
          writes=[r_cst])
        S("gpd", lambda q: ld(q, idb[:], io["c_ident_bf"][:, :]), writes=[r_cst])
        S("gpd", lambda q: ld(q, ones[:], io["c_ones_bf"][:, :]), writes=[r_cst])
        S("gpd", lambda q: ld(q, kvzs[:].rearrange("a b c -> a (b c)"), io["kvz_d"][:, :]),
          writes=[r_kvz])
        for nm, dst in [("s7_ob_t", obt), ("s7_b2_t", b2t), ("s7_g2_t", g2),
                        ("s7_be2_t", be2)]:
            S("gpd", lambda q, nm=nm, dst=dst: ld(q, dst[:], io[nm][:, :]), writes=[r_cst])
        for nm, dst in [("s7_b1c", b1c), ("c_consts", cc)]:
            S("sy", lambda q, nm=nm, dst=dst: ld(q, dst[:], io[nm][:, :]), writes=[r_cst])

        qfr = io["qfT_d"].rearrange("(fc f) t -> f fc t", f=128)

        for bb in range(16):
            tb = bb * 512
            S("sy", lambda q, tb=tb: ld(q, qfb[:, :, :], qfr[:, :, tb:tb + 512]),
              reads=[r_cst], writes=[r_qfb])
            for h in range(16):
                rr = slice((h % 2) * 64, (h % 2) * 64 + 64)
                hp = h % 2

                def aug_mm(q, h=h, rr=rr, hp=hp):
                    return q.matmul(pA[0:65, hp, :], kvzs[rr, h // 2, 0:65],
                                    qfb[rr, h // 2, :], start=True, stop=True)
                S("te", aug_mm, reads=[r_kvz, r_qfb], writes=[r_pA[hp]])
                S("ve", lambda q, hp=hp: q.tensor_scalar(
                    den[0:1, 0:512], pA[64:65, hp, :], 1e-6, None, ALU.max),
                  reads=[r_pA[hp]], writes=[r_den[0]])
                S("ve", lambda q: q.reciprocal(den[0:1, 512:1024], den[0:1, 0:512]),
                  reads=[r_den[0]], writes=[r_den[0]])
                S("ve", lambda q: q.tensor_copy(bcs[0:1, :], den[0:1, 512:1024]),
                  reads=[r_den[0]], writes=[r_rdb[0]])

                def bc_mm(q, hp=hp):
                    return q.matmul(pB[0:64, hp, :], ones[0:1, 0:64], bcs[0:1, :],
                                    start=True, stop=True)
                S("te", bc_mm, reads=[r_rdb[0], r_cst], writes=[r_pB[hp]])
                S("ve", lambda q, hp=hp: q.tensor_copy(bcs[:, :], pB[0:64, hp, :]),
                  reads=[r_pB[hp]], writes=[r_bc[0]])
                S("ve", lambda q, h=h, rr=rr, hp=hp: q.tensor_tensor(
                    ysb[rr, h // 2, :], pA[0:64, hp, :], bcs[:, :], ALU.mult),
                  reads=[r_pA[hp], r_bc[0]], writes=[r_y])

            for s in range(4):
                sp = s % 2
                rows = slice(tb + s * 128, tb + (s + 1) * 128)

                def proj_mm(q, s=s):
                    ii = None
                    for oc in range(2):
                        for fc in range(8):
                            ii = q.matmul(pC[:, oc, :], ysb[:, fc, s * 128:(s + 1) * 128],
                                          ow[:, fc, oc * 512:(oc + 1) * 512],
                                          start=(fc == 0), stop=(fc == 7))
                    return ii
                S("te", proj_mm, reads=[r_y, r_cst], writes=[r_pC[0], r_pC[1]])
                S("sy", lambda q, rows=rows: ld(q, xn1t[:], io["xn1_d"][rows, :]),
                  writes=[r_xn1])

                def xx_c(q, sp=sp):
                    for oc in range(2):
                        cl = slice(oc * 512, (oc + 1) * 512)
                        q.tensor_tensor(xxs[:, 0, cl], pC[:, oc, :], obt[:, cl], ALU.add)
                    ii = None
                    for oc in range(2):
                        cl = slice(oc * 512, (oc + 1) * 512)
                        ii = q.tensor_tensor(xxs[:, 0, cl], xxs[:, 0, cl],
                                             xn1t[:, cl], ALU.add)
                    return ii
                S("ve", xx_c, reads=[r_pC[0], r_pC[1], r_xn1, r_cst], writes=[r_xx[sp]])
                S("ve", lambda q, sp=sp: q.reduce_sum(ncol[:, sp, 0:1], xxs[:, 0, :], AX.X),
                  reads=[r_xx[sp]], writes=[r_nc[sp]])
                S("sc", lambda q, sp=sp: q.activation(sq[:], xxs[:, 0, :], AF.Square,
                                                      accum_out=ncol[:, sp, 1:2]),
                  reads=[r_xx[sp]], writes=[r_sq, r_nc[sp]])
                S("ve", lambda q, sp=sp: q.tensor_scalar(ncol[:, sp, 2:3], ncol[:, sp, 0:1],
                                                         1.0 / C, None, ALU.mult),
                  reads=[r_nc[sp]], writes=[r_nc[sp]])
                S("ve", lambda q, sp=sp: q.tensor_tensor(ncol[:, sp, 3:4], ncol[:, sp, 0:1],
                                                         ncol[:, sp, 2:3], ALU.mult),
                  reads=[r_nc[sp]], writes=[r_nc[sp]])
                S("ve", lambda q, sp=sp: q.tensor_tensor(ncol[:, sp, 4:5], ncol[:, sp, 1:2],
                                                         ncol[:, sp, 3:4], ALU.subtract),
                  reads=[r_nc[sp]], writes=[r_nc[sp]])
                S("ve", lambda q, sp=sp: q.tensor_scalar(ncol[:, sp, 5:6], ncol[:, sp, 4:5],
                                                         1.0 / C, None, ALU.mult),
                  reads=[r_nc[sp]], writes=[r_nc[sp]])
                S("sc", lambda q, sp=sp: q.activation(ncol[:, sp, 6:7], ncol[:, sp, 5:6],
                                                      AF.Ln, bias=cc[:, 0:1]),
                  reads=[r_nc[sp], r_cst], writes=[r_nc[sp]])
                S("sc", lambda q, sp=sp: q.activation(ncol[:, sp, 7:8], ncol[:, sp, 6:7],
                                                      AF.Exp, scale=-0.5),
                  reads=[r_nc[sp]], writes=[r_nc[sp]])
                S("ve", lambda q, sp=sp: q.tensor_scalar(sq[:], xxs[:, 0, :],
                                                         ncol[:, sp, 2:3], ncol[:, sp, 7:8],
                                                         ALU.subtract, ALU.mult),
                  reads=[r_xx[sp], r_nc[sp]], writes=[r_sq])
                S("ve", lambda q: q.tensor_tensor(ln2b[:], sq[:], g2[:], ALU.mult),
                  reads=[r_sq, r_cst], writes=[r_ln2b])
                S("ve", lambda q: q.scalar_tensor_tensor(ln2b[:], ln2b[:], 1.0, be2[:],
                                                         ALU.mult, ALU.add),
                  reads=[r_ln2b, r_cst], writes=[r_ln2b])

                def tr_ln2(q):
                    ii = None
                    for j in range(8):
                        ii = q.transpose(ptr[:, j * 128:(j + 1) * 128],
                                         ln2b[:, j * 128:(j + 1) * 128], idb[:])
                    return ii
                S("te", tr_ln2, reads=[r_ln2b, r_cst], writes=[r_ptr], no_drain=True)
                S("ve", lambda q: q.tensor_copy(
                    lnT[:, :, :].rearrange("a b c -> a (b c)"), ptr[:, :]),
                  reads=[r_ptr], writes=[r_lnT])

                for hb in range(8):
                    def h_mm(q, hb=hb):
                        ii = None
                        for oj in range(4):
                            oc = hb * 4 + oj
                            for fc in range(8):
                                ii = q.matmul(pC[:, hb % 2, oj * 128:(oj + 1) * 128],
                                              w1[:, fc, oc * 128:(oc + 1) * 128],
                                              lnT[:, fc, :],
                                              start=(fc == 0), stop=(fc == 7))
                        return ii
                    S("te", h_mm, reads=[r_lnT, r_cst], writes=[r_pC[hb % 2]])

                    def h_act(q, hb=hb):
                        ii = None
                        for oj in range(4):
                            oc = hb * 4 + oj
                            ii = q.activation(hms[:, oc, :],
                                              pC[:, hb % 2, oj * 128:(oj + 1) * 128],
                                              AF.Gelu, bias=b1c[:, oc:oc + 1])
                        return ii
                    S("sc", h_act, reads=[r_pC[hb % 2], r_cst], writes=[r_hm])

                S("sy", lambda q, rows=rows: ld(q, x1t[:], io["x1_d"][rows, :]),
                  writes=[r_x1])
                S("gpd", lambda q, rows=rows: q.dma_start(
                    out=x1t[:], in_=io["f1_d"][rows, :], accum_op=ALU.add),
                  reads=[r_x1], writes=[r_x1])
                for oc in range(2):
                    def o_mm(q, oc=oc):
                        ii = None
                        for ic in range(32):
                            ii = q.matmul(pD[:, :], hms[:, ic, :],
                                          w2[:, ic, oc * 512:(oc + 1) * 512],
                                          start=(ic == 0), stop=(ic == 31))
                        return ii
                    S("te", o_mm, reads=[r_hm, r_cst], writes=[r_pD])

                    def fin(q, sp=sp, oc=oc):
                        cl = slice(oc * 512, (oc + 1) * 512)
                        q.tensor_tensor(outs[:, cl], pD[:, :], b2t[:, cl], ALU.add)
                        q.drain()
                        q.tensor_tensor(outs[:, cl], outs[:, cl], xxs[:, 0, cl], ALU.add)
                        q.drain()
                        return q.tensor_tensor(outs[:, cl], outs[:, cl], x1t[:, cl], ALU.add)
                    S("ve", fin, reads=[r_pD, r_xx[sp], r_x1, r_cst], writes=[r_out])
                S("scd", lambda q, rows=rows: ld(q, o_out[rows, :], outs[:]),
                  reads=[r_out])

        with nc.Block() as block:
            sch.emit(nc, block)
    return nc



S4_TILES = NT
S4_PHASE = 5
S4_APPS = 6
S4_CPRE = 3


def build_s4(nc, io):
    o_y = io["y_d"]
    f_int = nc.dram_tensor("s4_fint", [128, 1024], BF16, kind="Internal").ap()
    with ExitStack() as _es:
        ec = _es.enter_context
        sb = lambda *a: ec(nc.sbuf_tensor(*a))
        dmM = sb("s4_dmM", [128, 128], F32)
        dmP = sb("s4_dmP", [128, 128], F32)
        msk = sb("s4_msk", [64, 320], F32)
        idb = sb("s4_idb", [128, 128], BF16)
        F_sb = sb("s4_F", [128, 8, 128], BF16)
        Ff = sb("s4_Ff", [128, 8, 128], F32)
        fsum = sb("s4_fsum", [128, 1024], F32)
        fb = sb("s4_fb", [128, 1024], BF16)
        wt = sb("s4_w", [128, 2, 1024], F32)
        rt = sb("s4_r", [128, 2, 1024], BF16)
        kt = sb("s4_k", [128, 2, 1024], BF16)
        kkt = sb("s4_kk", [128, 2, 1024], BF16)
        bt = sb("s4_b", [128, 2, 1024], BF16)
        vt = sb("s4_v", [64, 2, 2, 1024], BF16)
        vt2 = sb("s4_v2", [128, 2, 2, 1024], BF16)
        ers = sb("s4_er", [128, 1024], F32)
        eas = sb("s4_ea", [128, 1024], F32)
        ebs = sb("s4_eb", [128, 1024], F32)
        rts = sb("s4_rt", [64, 2, 1024], BF16)
        ats = sb("s4_at", [64, 2, 1024], BF16)
        bhs = sb("s4_bh", [64, 2, 1024], BF16)
        khs = sb("s4_kh", [64, 2, 1024], BF16)
        chT = sb("s4_chT", [128, 2, 4, 8, 64], BF16)
        TmA = sb("s4_TmA", [128, 3, 128], BF16)
        TmB = sb("s4_TmB", [128, 2, 128], BF16)
        mskq = sb("s4_mskq", [128, 640], F32)
        Qp = sb("s4_Qp", [128, 5, 128], BF16)
        Qu = sb("s4_Qu", [128, 2, 128], BF16)
        Xs = sb("s4_X", [128, 2, 128], BF16)
        Cs = sb("s4_C", [128, 2, 128], BF16)
        Gs = sb("s4_G", [128, 8, 64], BF16)
        yts = sb("s4_y", [128, 2, 1024], F32)
        pcs = ec(nc.psum_tensor("s4_pcs", [128, 2, 512], F32))
        pcp = ec(nc.psum_tensor("s4_pcp", [128, 2, 512], F32))
        ptr = ec(nc.psum_tensor("s4_ptr", [128, 2, 1024], BF16))
        pT = ec(nc.psum_tensor("s4_pT", [128, 2, 512], F32))

        sch = Sched()
        S = mk_S(sch)
        r_cst, r_fsum, r_fb, r_fint, r_F = Res(), Res(), Res(), Res(), Res()
        r_w = [Res(), Res()]
        r_r = [Res(), Res()]
        r_k = [Res(), Res()]
        r_kk = [Res(), Res()]
        r_b = [Res(), Res()]
        r_v = [Res(), Res()]
        r_pcs = [Res(), Res()]
        r_pcp = [Res(), Res()]
        r_ptr = [Res(), Res()]
        r_pT = [Res(), Res()]
        r_er, r_ea, r_eb = Res(), Res(), Res()
        r_rts, r_ats, r_bhs, r_khs = Res(), Res(), Res(), Res()
        r_chT = [Res(), Res()]
        r_Tm, r_Qp, r_Cs = Res(), Res(), Res()
        r_Qu = [Res(), Res()]
        r_X = [Res(), Res()]
        r_G = [Res() for _ in range(8)]
        r_y = [Res(), Res()]

        def ld(q, dst, src):
            return q.dma_start(out=dst, in_=src)

        for nm, dst in [("s4_dmatM", dmM), ("s4_dmatP", dmP), ("s4_masks", msk)]:
            S("sy", lambda q, nm=nm, dst=dst: ld(q, dst[:], io[nm][:, :]), writes=[r_cst])
        S("sy", lambda q: ld(q, mskq[:, :], io["s4_maskq"][:, :]), writes=[r_cst])
        S("gpd", lambda q: ld(q, idb[:], io["c_ident_bf"][:, :]), writes=[r_cst])
        gz = S("ve", lambda q: q.memset(Gs[:].rearrange("a b c -> a (b c)"), 0.0))
        for h2 in range(8):
            r_G[h2].w = gz

        S("sy", lambda q: ld(q, fsum[:], io["wlm_d"][:, :]), writes=[r_fsum])
        S("gpd", lambda q: q.dma_start(out=fsum[:], in_=io["mcs_d"][:, :],
                                       accum_op=ALU.add),
          reads=[r_fsum], writes=[r_fsum])
        S("sc", lambda q: q.activation(fb[:], fsum[:], AF.Exp),
          reads=[r_fsum], writes=[r_fb])
        S("gpd", lambda q: ld(q, f_int[:, :], fb[:]), reads=[r_fb], writes=[r_fint])

        for fc in range(8):
            S("sy", lambda q, fc=fc: q.dma_start_transpose(
                F_sb[:, fc, :], f_int[:, fc * 128:(fc + 1) * 128]),
              reads=[r_fint], writes=[r_F])
        S("ve", lambda q: q.tensor_copy(Ff[:].rearrange("a b c -> a (b c)"),
                                        F_sb[:].rearrange("a b c -> a (b c)")),
          reads=[r_F], writes=[r_F])

        for i in range(S4_TILES):
            p = i % 2
            rows = slice(i * 128, (i + 1) * 128)
            S("sy", lambda q, rows=rows, p=p: ld(q, wt[:, p, :], io["w_d"][rows, :]),
              reads=[r_cst], writes=[r_w[p]])
            S("sy", lambda q, rows=rows, p=p: ld(q, rt[:, p, :], io["r_d"][rows, :]),
              writes=[r_r[p]])
            S("sy", lambda q, rows=rows, p=p: ld(q, kt[:, p, :], io["k_d"][rows, :]),
              writes=[r_k[p]])
            S("scd", lambda q, rows=rows, p=p: ld(q, kkt[:, p, :], io["kk_d"][rows, :]),
              writes=[r_kk[p]])
            S("scd", lambda q, rows=rows, p=p: ld(q, bt[:, p, :], io["b_d"][rows, :]),
              writes=[r_b[p]])
            S("scd", lambda q, rows=rows, p=p: ld(
                q, vt[:, p, :, :],
                io["v_d"][rows, :].rearrange("(cj t) c -> t cj c", cj=2)),
              writes=[r_v[p]])
            S("scd", lambda q, rows=rows, p=p: ld(
                q, vt2[0:64, p, :, :],
                io["v_d"][rows, :].rearrange("(cj t) c -> t cj c", cj=2)),
              writes=[r_v[p]])
            S("scd", lambda q, rows=rows, p=p: ld(
                q, vt2[64:128, p, :, :],
                io["v_d"][rows, :].rearrange("(cj t) c -> t cj c", cj=2)),
              writes=[r_v[p]])

            def cs_mm(q, p=p):
                q.matmul(pcs[:, 0, :], dmM[:], wt[:, p, 0:512], start=True, stop=True)
                q.matmul(pcs[:, 1, :], dmM[:], wt[:, p, 512:1024], start=True, stop=True)
                q.matmul(pcp[:, 0, :], dmP[:], wt[:, p, 0:512], start=True, stop=True)
                return q.matmul(pcp[:, 1, :], dmP[:], wt[:, p, 512:1024],
                                start=True, stop=True)
            S("te", cs_mm, reads=[r_w[p], r_cst],
              writes=[r_pcs[0], r_pcs[1], r_pcp[0], r_pcp[1]])
            def exps(q):
                q.activation(ers[:, 0:512], pcs[:, 0, :], AF.Exp)
                q.drain()
                q.activation(ers[:, 512:1024], pcs[:, 1, :], AF.Exp)
                q.drain()
                q.activation(ebs[:, 0:512], pcs[:, 0, :], AF.Exp, scale=-1.0)
                q.drain()
                q.activation(ebs[:, 512:1024], pcs[:, 1, :], AF.Exp, scale=-1.0)
                q.drain()
                q.activation(eas[:, 0:512], pcp[:, 0, :], AF.Exp)
                q.drain()
                return q.activation(eas[:, 512:1024], pcp[:, 1, :], AF.Exp)
            S("sc", exps, reads=[r_pcs[0], r_pcs[1], r_pcp[0], r_pcp[1]],
              writes=[r_er, r_eb, r_ea])

            def scl(q, p=p):
                ii = None
                for cj in range(2):
                    rsl = slice(cj * 64, (cj + 1) * 64)
                    q.tensor_tensor(rts[:, cj, :], rt[rsl, p, :], ers[rsl, :], ALU.mult)
                    q.scalar_tensor_tensor(ats[:, cj, :], kt[rsl, p, :], -1.0,
                                           eas[rsl, :], ALU.mult, ALU.mult)
                    q.tensor_tensor(bhs[:, cj, :], bt[rsl, p, :], ebs[rsl, :], ALU.mult)
                    ii = q.tensor_tensor(khs[:, cj, :], kkt[rsl, p, :], ebs[rsl, :],
                                         ALU.mult)
                return ii
            S("ve", scl, reads=[r_r[p], r_k[p], r_b[p], r_kk[p], r_er, r_ea, r_eb],
              writes=[r_rts, r_ats, r_bhs, r_khs])

            for cj in range(2):
                def tr_w1(q, cj=cj):
                    ii = None
                    for ai, arr in enumerate((ats, bhs)):
                        for fc in range(8):
                            ii = q.transpose(
                                ptr[:, cj, (ai * 8 + fc) * 64:(ai * 8 + fc + 1) * 64],
                                arr[:, cj, fc * 128:(fc + 1) * 128], idb[0:64, 0:64])
                    return ii
                S("te", tr_w1, reads=[r_ats, r_bhs, r_cst], writes=[r_ptr[cj]],
                  no_drain=True)
                S("ve", lambda q, cj=cj: q.tensor_copy(
                    chT[:, cj, 0:2, :, :].rearrange("a b c d -> a (b c d)"),
                    ptr[:, cj, :]),
                  reads=[r_ptr[cj]], writes=[r_chT[cj]])

                def tr_w2(q, cj=cj):
                    ii = None
                    for ai, arr in enumerate((khs, rts)):
                        for fc in range(8):
                            ii = q.transpose(
                                ptr[:, cj, (ai * 8 + fc) * 64:(ai * 8 + fc + 1) * 64],
                                arr[:, cj, fc * 128:(fc + 1) * 128], idb[0:64, 0:64])
                    return ii
                S("te", tr_w2, reads=[r_khs, r_rts, r_cst], writes=[r_ptr[cj]],
                  no_drain=True)
                S("ve", lambda q, cj=cj: q.tensor_copy(
                    chT[:, cj, 2:4, :, :].rearrange("a b c d -> a (b c d)"),
                    ptr[:, cj, :]),
                  reads=[r_ptr[cj]], writes=[r_chT[cj]])

            for cj in range(2):
                for qd in range(4):
                    def tf_mm(q, cj=cj, qd=qd):
                        ii = None
                        for hf in range(2):
                            rr = slice(hf * 64, hf * 64 + 64)
                            for qh in range(2):
                                h2 = 2 * qd + qh
                                co = 64 * qh
                                a_ = chT[rr, cj, 0, h2, :]
                                b_ = chT[rr, cj, 1, h2, :]
                                k_ = chT[rr, cj, 2, h2, :]
                                r_ = chT[rr, cj, 3, h2, :]
                                q.matmul(pT[rr, 0, co:co + 64], a_, b_, start=True, stop=True)
                                q.matmul(pT[rr, 1, co:co + 64], b_, a_, start=True, stop=True)
                                q.matmul(pT[rr, 0, 128 + co:128 + co + 64], k_, a_,
                                         start=True, stop=True)
                                q.matmul(pT[rr, 1, 128 + co:128 + co + 64], b_, r_,
                                         start=True, stop=True)
                                ii = q.matmul(pT[rr, 0, 256 + co:256 + co + 64], k_, r_,
                                              start=True, stop=True)
                        return ii
                    S("te", tf_mm, reads=[r_chT[cj], r_cst], writes=[r_pT[0], r_pT[1]])

                    def tf_msk(q):
                        q.tensor_tensor(TmA[:, :, :].rearrange("a b c -> a (b c)"),
                                        pT[:, 0, 0:384], mskq[:, 0:384], ALU.mult)
                        return q.tensor_tensor(TmB[:, :, :].rearrange("a b c -> a (b c)"),
                                               pT[:, 1, 0:256], mskq[:, 384:640],
                                               ALU.mult)
                    S("ve", tf_msk, reads=[r_pT[0], r_pT[1], r_cst], writes=[r_Tm])

                    for li in range(5):
                        def sq_mm(q, li=li):
                            ii = None
                            for hf in range(2):
                                rr = slice(hf * 64, hf * 64 + 64)
                                for qh in range(2):
                                    co = 64 * qh
                                    if li == 0:
                                        qu_ = TmA[rr, 0, co:co + 64]
                                        qp_ = TmB[rr, 0, co:co + 64]
                                    else:
                                        qu_ = Qu[rr, (li - 1) % 2, co:co + 64]
                                        qp_ = Qp[rr, li - 1, co:co + 64]
                                    if li < 4:
                                        q.matmul(pT[rr, 0, 384 + co:384 + co + 64],
                                                 qp_, qu_, start=True, stop=True)
                                    ii = q.matmul(pT[rr, 1, 384 + co:384 + co + 64],
                                                  qu_, qp_, start=True, stop=True)
                            return ii
                        S("te", sq_mm,
                          reads=[r_Tm, r_Qu[(li - 1) % 2] if li > 0 else r_Tm, r_Qp],
                          writes=[r_pT[0], r_pT[1]])

                        def sq_ev(q, li=li):
                            if li < 4:
                                q.tensor_copy(Qu[:, li % 2, :], pT[:, 0, 384:512])
                                q.drain()
                            return q.tensor_copy(Qp[:, li, :], pT[:, 1, 384:512])
                        S("ve", sq_ev, reads=[r_pT[0], r_pT[1]],
                          writes=[r_Qu[li % 2], r_Qp])

                    def cpre_mm(q, cj=cj, qd=qd, p=p):
                        ii = None
                        for hf in range(2):
                            rr = slice(hf * 64, hf * 64 + 64)
                            for qh in range(2):
                                h2 = 2 * qd + qh
                                co = 64 * qh
                                hcf = slice((2 * h2 + hf) * 64, (2 * h2 + hf + 1) * 64)
                                q.matmul(pcs[rr, hf, co:co + 64], chT[rr, cj, 0, h2, :],
                                         Gs[rr, h2, :], start=True, stop=True)
                                ii = q.matmul(pcs[rr, 1 - hf, 128 + co:128 + co + 64],
                                              TmA[rr, 1, co:co + 64],
                                              vt2[rr, p, cj, hcf], start=True, stop=True)
                        return ii
                    S("te", cpre_mm, reads=[r_chT[cj], r_G[2 * qd], r_G[2 * qd + 1],
                                            r_Tm, r_v[p]],
                      writes=[r_pcs[0], r_pcs[1]])

                    def x0_ev(q):
                        q.tensor_copy(Xs[0:64, 1, :], pcs[0:64, 0, 0:128])
                        q.tensor_copy(Xs[64:128, 1, :], pcs[64:128, 1, 0:128])
                        q.drain()
                        q.tensor_tensor(Xs[0:64, 0, :], pcs[0:64, 1, 128:256],
                                        Xs[0:64, 1, :], ALU.add)
                        return q.tensor_tensor(Xs[64:128, 0, :], pcs[64:128, 0, 128:256],
                                               Xs[64:128, 1, :], ALU.add)
                    S("ve", x0_ev, reads=[r_pcs[0], r_pcs[1]], writes=[r_X[0], r_X[1]])

                    for ai2 in range(6):
                        xp = ai2 % 2

                        def app_mm(q, ai2=ai2, xp=xp):
                            ii = None
                            for hf in range(2):
                                rr = slice(hf * 64, hf * 64 + 64)
                                for qh in range(2):
                                    co = 64 * qh
                                    qq = (TmB[rr, 0, co:co + 64] if ai2 == 5
                                          else Qp[rr, 4 - ai2, co:co + 64])
                                    ii = q.matmul(pcs[rr, hf, 256 + co:256 + co + 64],
                                                  qq, Xs[rr, xp, co:co + 64],
                                                  start=True, stop=True)
                            return ii
                        S("te", app_mm, reads=[r_Qp, r_Tm, r_X[xp]],
                          writes=[r_pcs[0], r_pcs[1]])
                        if ai2 < 5:
                            def x_ev(q, xp=xp):
                                q.tensor_tensor(Xs[0:64, 1 - xp, :],
                                                pcs[0:64, 0, 256:384],
                                                Xs[0:64, xp, :], ALU.add)
                                return q.tensor_tensor(Xs[64:128, 1 - xp, :],
                                                       pcs[64:128, 1, 256:384],
                                                       Xs[64:128, xp, :], ALU.add)
                            S("ve", x_ev, reads=[r_pcs[0], r_pcs[1], r_X[xp]],
                              writes=[r_X[1 - xp]])
                        else:
                            def c_ev(q, xp=xp):
                                q.tensor_tensor(Cs[0:64, 0, :], pcs[0:64, 0, 256:384],
                                                Xs[0:64, xp, :], ALU.add)
                                q.tensor_tensor(Cs[64:128, 0, :],
                                                pcs[64:128, 1, 256:384],
                                                Xs[64:128, xp, :], ALU.add)
                                q.drain()
                                return q.tensor_copy(Cs[0:64, 1, :], Cs[64:128, 0, :])
                            S("ve", c_ev, reads=[r_pcs[0], r_pcs[1], r_X[xp]],
                              writes=[r_Cs])

                    def y_mm(q, cj=cj, qd=qd, p=p):
                        ii = None
                        for hf in range(2):
                            rr = slice(hf * 64, hf * 64 + 64)
                            for qh in range(2):
                                h2 = 2 * qd + qh
                                co = 64 * qh
                                hcf = slice((2 * h2 + hf) * 64, (2 * h2 + hf + 1) * 64)
                                q.matmul(pT[rr, hf, co:co + 64], chT[rr, cj, 3, h2, :],
                                         Gs[rr, h2, :], start=True, stop=True)
                                q.matmul(pT[rr, 1 - hf, 128 + co:128 + co + 64],
                                         TmB[rr, 1, co:co + 64], Cs[rr, 0, co:co + 64],
                                         start=True, stop=False)
                                ii = q.matmul(pT[rr, 1 - hf, 128 + co:128 + co + 64],
                                              TmA[rr, 2, co:co + 64],
                                              vt2[rr, p, cj, hcf], start=False, stop=True)
                        return ii
                    S("te", y_mm, reads=[r_chT[cj], r_G[2 * qd], r_G[2 * qd + 1],
                                         r_Tm, r_Cs, r_v[p]],
                      writes=[r_pT[0], r_pT[1]])

                    def y_ev(q, cj=cj, p=p, qd=qd):
                        rows = slice(cj * 64, (cj + 1) * 64)
                        for hf in range(2):
                            for qh in range(2):
                                h = 4 * qd + 2 * qh + hf
                                hc = slice(h * 64, (h + 1) * 64)
                                co = 64 * qh
                                q.tensor_copy(yts[rows, p, hc],
                                              pT[hf * 64:hf * 64 + 64, hf, co:co + 64])
                        q.drain()
                        ii = None
                        for hf in range(2):
                            for qh in range(2):
                                h = 4 * qd + 2 * qh + hf
                                hc = slice(h * 64, (h + 1) * 64)
                                co = 64 * qh
                                ii = q.tensor_tensor(
                                    yts[rows, p, hc],
                                    pT[hf * 64:hf * 64 + 64, 1 - hf,
                                       128 + co:128 + co + 64],
                                    yts[rows, p, hc], ALU.add)
                        return ii
                    S("ve", y_ev, reads=[r_pT[0], r_pT[1], r_y[p]], writes=[r_y[p]])

                    def st_mm(q, cj=cj, p=p, qd=qd):
                        ii = None
                        for hf in range(2):
                            rr = slice(hf * 64, hf * 64 + 64)
                            for qh in range(2):
                                h2 = 2 * qd + qh
                                co = 64 * qh
                                hcf = slice((2 * h2 + hf) * 64, (2 * h2 + hf + 1) * 64)
                                cs_src = Cs[0:64, hf, co:co + 64]
                                q.matmul(pcp[rr, hf, co:co + 64], bhs[:, cj, hcf],
                                         cs_src, start=True, stop=False)
                                ii = q.matmul(pcp[rr, hf, co:co + 64], khs[:, cj, hcf],
                                              vt[:, p, cj, hcf], start=False, stop=True)
                        return ii
                    S("te", st_mm, reads=[r_bhs, r_khs, r_Cs, r_v[p]],
                      writes=[r_pcp[0], r_pcp[1]])

                    def st_up(q, cj=cj, qd=qd, i=i):
                        ci = 2 * i + cj
                        gfl = Gs[:, 2 * qd:2 * qd + 2, :].rearrange("a b c -> a (b c)")
                        q.tensor_tensor(Xs[0:64, 0, :], pcp[0:64, 0, 0:128],
                                        gfl[0:64, :], ALU.add)
                        q.tensor_tensor(Xs[64:128, 0, :], pcp[64:128, 1, 0:128],
                                        gfl[64:128, :], ALU.add)
                        q.drain()
                        q.tensor_scalar(Gs[:, 2 * qd, :], Xs[:, 0, 0:64],
                                        Ff[:, 2 * qd, ci:ci + 1], None, ALU.mult)
                        return q.tensor_scalar(Gs[:, 2 * qd + 1, :], Xs[:, 0, 64:128],
                                               Ff[:, 2 * qd + 1, ci:ci + 1], None,
                                               ALU.mult)
                    S("ve", st_up, reads=[r_pcp[0], r_pcp[1], r_G[2 * qd],
                                          r_G[2 * qd + 1], r_F, r_X[0]],
                      writes=[r_G[2 * qd], r_G[2 * qd + 1], r_X[0]])

            S("scd", lambda q, rows=rows, p=p: ld(q, o_y[rows, :], yts[:, p, :]),
              reads=[r_y[p]])

        with nc.Block() as block:
            sch.emit(nc, block)
    return nc


BUILDERS = {"s1": build_s1, "s2": build_s2, "s3": build_s3, "s5": build_s5,
            "s6": build_s6, "s7a": build_s7a, "s7b": build_s7b, "s4": build_s4}


def build_program(stages):
    nc = bass.Bass("TRN2", target_bir_lowering=False)
    needed_in, needed_out = [], []
    for s in stages:
        for n in STAGE_IN[s]:
            if n not in needed_in and not any(n in STAGE_OUT[t] for t in stages[:stages.index(s)]):
                needed_in.append(n)
        for n in STAGE_OUT[s]:
            needed_out.append(n)
    io = {}
    for n in needed_in:
        shape, dt, _ = IO_SPECS[n]
        io[n] = nc.dram_tensor(n, shape, dt, kind="ExternalInput").ap()
    for n in needed_out:
        shape, dt, _ = IO_SPECS[n]
        io[n] = nc.dram_tensor(n, shape, dt, kind="ExternalOutput").ap()
    for s in stages:
        BUILDERS[s](nc, io)
    return nc, needed_in, needed_out


_NP_DT = {F32: np.float32, BF16: BF}


def run_stages(stage_groups, host_shared, core_vals):
    for stages in stage_groups:
        nc, ins, outs = build_program(stages)
        in_maps = []
        for cv in core_vals:
            m = {}
            for n in ins:
                m[n] = cv[n] if IO_SPECS[n][2] else host_shared[n]
            in_maps.append(m)
        res = run_bass_kernel_spmd(nc, in_maps, core_ids=list(range(8)))
        for c, r in enumerate(res.results):
            for n in outs:
                core_vals[c][n] = r[n]
    return core_vals


ALL_STAGES = ["s1", "s2", "s3", "s4", "s5", "s6", "s7a", "s7b"]


def build_fused(stages=None):
    stages = stages or ALL_STAGES
    nc = bass.Bass("TRN2", target_bir_lowering=False)
    needed = []
    for s in stages:
        for n in STAGE_IN[s] + STAGE_OUT[s]:
            if n not in needed:
                needed.append(n)
    produced = {n for s in stages for n in STAGE_OUT[s]}
    io, ext_in = {}, []
    for n in needed:
        shape, dt, per_core = IO_SPECS[n]
        if n == "out_d":
            kind = "ExternalOutput"
        elif n in produced:
            kind = "Internal"
        else:
            kind = "ExternalInput"
            ext_in.append(n)
        io[n] = nc.dram_tensor(n, shape, dt, kind=kind).ap()
    bar = nc.alloc_semaphore(name="g_bar")
    for k, s in enumerate(stages):
        BUILDERS[s](nc, io)
        with nc.Block() as bb:
            for eng in ["sync", "vector", "scalar", "tensor", "gpsimd"]:
                def mk(eng=eng, k=k):
                    def f(q):
                        q.drain().then_inc(bar, 1)
                        q.wait_ge(bar, 5 * (k + 1))
                    return f
                getattr(bb, eng)(mk())
    return nc, ext_in


def kernel(**inputs):
    host_shared = _prep_host(inputs)
    core_vals = [_prep_core(inputs, b) for b in range(8)]
    nc, ext_in = build_fused()
    in_maps = []
    for cv in core_vals:
        m = {}
        for n in ext_in:
            m[n] = cv[n] if IO_SPECS[n][2] else host_shared[n]
        in_maps.append(m)
    res = run_bass_kernel_spmd(nc, in_maps, core_ids=list(range(8)))
    return np.stack([res.results[b]["out_d"] for b in range(8)], axis=0)
